# version 08
# baseline (speedup 1.0000x reference)
import numpy as np
from contextlib import ExitStack

import concourse.bass as bass
import concourse.mybir as mybir
from concourse.bass_utils import run_bass_kernel_spmd

F32 = mybir.dt.float32
BF16 = mybir.dt.bfloat16
AF = mybir.ActivationFunctionType
ALU = mybir.AluOpType

P = 128
D = 1024
DI = 2048
HD = 1024
NST = 16
NE = 8
HF = 512
TS = 512
PAIRS = [[0, 1], [2, 3], [4, 5], [6, 7]]


class Sems:
    def __init__(self, es, nc, names):
        self.h = {n: es.enter_context(nc.semaphore(n)) for n in names}
        self.c = {n: 0 for n in names}

    def inc(self, inst, name, k=1):
        inst.then_inc(self.h[name], k)
        self.c[name] += k
        return self.c[name]


def build_program(L, stop_after=None):
    nCH = L // 512
    nSU = L // TS
    TH = L // 2
    nTB = TH // 512

    nc = bass.Bass("TRN2", target_bir_lowering=False, num_devices=8,
                   detect_race_conditions=False)

    dxin = nc.dram_tensor("x", [L, D], F32, kind="ExternalInput").ap()
    dwin = nc.dram_tensor("w_in", [D, DI], F32, kind="ExternalInput").ap()
    dcw = nc.dram_tensor("conv_w", [HD, 4], F32, kind="ExternalInput").ap()
    dcb = nc.dram_tensor("conv_b", [HD, 1], F32, kind="ExternalInput").ap()
    dwx2 = nc.dram_tensor("w_x2", [DI, NST], F32, kind="ExternalInput").ap()
    dwdt = nc.dram_tensor("w_dt", [DI, HD], F32, kind="ExternalInput").ap()
    dbdt = nc.dram_tensor("b_dt", [HD, 1], F32, kind="ExternalInput").ap()
    dalog = nc.dram_tensor("a_log", [HD, NST], F32, kind="ExternalInput").ap()
    ddp = nc.dram_tensor("d_p", [HD, 1], F32, kind="ExternalInput").ap()
    dwout = nc.dram_tensor("w_out", [HD, D], F32, kind="ExternalInput").ap()
    drw = nc.dram_tensor("r_w", [D, NE], F32, kind="ExternalInput").ap()
    deb = nc.dram_tensor("e_b", [NE, 1], F32, kind="ExternalInput").ap()
    dw1 = nc.dram_tensor("w1", [NE + 1, D, HF], F32, kind="ExternalInput").ap()
    db1 = nc.dram_tensor("b1", [NE + 1, HF], F32, kind="ExternalInput").ap()
    dw2 = nc.dram_tensor("w2", [NE + 1, HF, D], F32, kind="ExternalInput").ap()
    db2 = nc.dram_tensor("b2", [NE + 1, D], F32, kind="ExternalInput").ap()
    didf = nc.dram_tensor("identf", [P, P], F32, kind="ExternalInput").ap()
    didb = nc.dram_tensor("identb", [P, P], BF16, kind="ExternalInput").ap()
    de16 = nc.dram_tensor("e16", [NST, NST * P], BF16, kind="ExternalInput").ap()
    de9 = nc.dram_tensor("e9", [NE + 1, (NE + 1) * P], BF16,
                         kind="ExternalInput").ap()
    dout = nc.dram_tensor("out", [TH, D], F32, kind="ExternalOutput").ap()

    shxc = nc.dram_tensor("sh_xc", [16, P, L], BF16, kind="Internal",
                          addr_space="Shared").ap()
    shpo = nc.dram_tensor("sh_po", [2, 8, P, L], BF16, kind="Internal",
                          addr_space="Shared").ap()
    pz = nc.dram_tensor("p_z", [8, P, L], BF16, kind="Internal").ap()
    pxnt = nc.dram_tensor("p_xnt", [8, P, L], BF16, kind="Internal").ap()
    bi1 = nc.dram_tensor("bar_i1", [1, 16], F32, kind="Internal").ap()
    bo1 = nc.dram_tensor("bar_o1", [1, 16], F32, kind="Internal").ap()
    bi2 = nc.dram_tensor("bar_i2", [1, 16], F32, kind="Internal").ap()
    bo2 = nc.dram_tensor("bar_o2", [1, 16], F32, kind="Internal").ap()

    es = ExitStack()

    def sb(name, shape, dtype):
        return es.enter_context(nc.sbuf_tensor(name, shape, dtype))

    S = Sems(es, nc, ["qs", "qg", "qo", "sa", "sv", "sp", "sg", "cc"])
    block = es.enter_context(nc.Block())
    hreg = {}

    def get_h(eng, ename):
        if ename not in hreg:
            hreg[ename] = eng.to_reg(eng.partition_id() % 2)
        return hreg[ename]

    wx2 = sb("wx2_s", [P, 16 * NST], BF16)
    cwv = sb("cwv", [P, 8 * 4], F32)
    cbv = sb("cbv", [P, 8], F32)
    bdtv = sb("bdtv", [P, 8], F32)
    dpv = sb("dpv", [P, 8], F32)
    alog = sb("alog", [P, 8 * NST], F32)
    aneg = sb("aneg", [P, 8 * NST], F32)
    idf = sb("idf", [P, P], F32)
    idb = sb("idb", [P, P], BF16)
    e16 = sb("e16s", [NST, NST * P], BF16)
    e9 = sb("e9s", [NE + 1, (NE + 1) * P], BF16)
    cst = sb("cst", [P, 8], F32)
    zb = sb("zb", [1, 16], F32)
    C_EPS, C_MH, C_ONE, C_MONE, C_HALF, C_TINY = range(6)
    carry = sb("carry", [P, NST * 8], F32)
    ebv = sb("ebv", [NE, 1], F32)
    b1v = sb("b1v", [P, (NE + 1) * 4], F32)

    def cc_(i):
        return cst[:, i:i + 1]

    es_mw = ExitStack()
    es_wi = ExitStack()
    wdt = es_mw.enter_context(nc.sbuf_tensor("wdt_s", [P, 16 * HD], BF16))
    wout = es_mw.enter_context(nc.sbuf_tensor("wout_s", [P, 8 * D], BF16))
    win = es_wi.enter_context(nc.sbuf_tensor("win_s", [P, 8 * DI], BF16))

    def win_t(kt, mt):
        return win[:, kt * DI + mt * P:kt * DI + (mt + 1) * P]

    def wdt_t(kt, mt):
        return wdt[:, kt * HD + mt * P:kt * HD + (mt + 1) * P]

    def wout_t(kt, mt):
        return wout[:, kt * D + mt * P:kt * D + (mt + 1) * P]

    @block.gpsimd
    def _(g):
        S.inc(g.dma_start(out=win[:].rearrange("p (k c) -> p k c", k=8),
                          in_=dwin.rearrange("(k p) c -> p k c", p=P)), "qg", 16)
        S.inc(g.dma_start(out=wdt[:].rearrange("p (k c) -> p k c", k=16),
                          in_=dwdt.rearrange("(k p) c -> p k c", p=P)), "qg", 16)
        S.inc(g.dma_start(out=wout[:].rearrange("p (k c) -> p k c", k=8),
                          in_=dwout.rearrange("(k p) c -> p k c", p=P)), "qg", 16)
        S.inc(g.dma_start(out=wx2[:].rearrange("p (k n) -> p k n", k=16),
                          in_=dwx2.rearrange("(k p) n -> p k n", p=P)), "qg", 16)
        S.inc(g.dma_start(out=idb[:], in_=didb), "qg", 16)
        S.inc(g.dma_start(out=e16[:], in_=de16), "qg", 16)
        S.inc(g.dma_start(out=e9[:], in_=de9), "qg", 16)
        g.memset(zb[:], 0.0)
        g.memset(cst[:, C_EPS:C_EPS + 1], 1e-5)
        g.memset(cst[:, C_MH:C_MH + 1], -0.5)
        g.memset(cst[:, C_ONE:C_ONE + 1], 1.0)
        g.memset(cst[:, C_MONE:C_MONE + 1], -1.0)
        g.memset(cst[:, C_HALF:C_HALF + 1], 0.5)
        S.inc(g.memset(cst[:, C_TINY:C_TINY + 1], 1e-30), "sg", 1)
        g.wait_ge(S.h["sg"], 1)
        S.inc(g.dma_start(out=bi1, in_=zb[0:1, :]), "qg", 16)
        S.inc(g.dma_start(out=bi2, in_=zb[0:1, :]), "qg", 16)
    QG_W = S.c["qg"]

    @block.sync
    def _(sy):
        S.inc(sy.dma_start(out=cwv[:].rearrange("p (d f) -> p d f", d=8),
                           in_=dcw.rearrange("(d p) f -> p d f", p=P)), "qs", 16)
        S.inc(sy.dma_start(out=cbv[:],
                           in_=dcb.rearrange("(d p) o -> p (d o)", p=P), allow_slow_non_contiguous=True), "qs", 16)
        S.inc(sy.dma_start(out=bdtv[:],
                           in_=dbdt.rearrange("(d p) o -> p (d o)", p=P), allow_slow_non_contiguous=True), "qs", 16)
        S.inc(sy.dma_start(out=dpv[:],
                           in_=ddp.rearrange("(d p) o -> p (d o)", p=P), allow_slow_non_contiguous=True), "qs", 16)
        S.inc(sy.dma_start(out=alog[:].rearrange("p (d n) -> p d n", d=8),
                           in_=dalog.rearrange("(d p) n -> p d n", p=P)), "qs", 16)
        S.inc(sy.dma_start(out=idf[:], in_=didf), "qs", 16)
        S.inc(sy.dma_start(out=ebv[:], in_=deb), "qs", 16)
        S.inc(sy.dma_start(out=b1v[:].rearrange("p (e m) -> p e m", e=NE + 1),
                           in_=db1.rearrange("e (m p) -> p e m", p=P), allow_slow_non_contiguous=True), "qs", 16)
    QS_SETUP = S.c["qs"]

    @block.scalar
    def _(s):
        s.wait_ge(S.h["qs"], QS_SETUP)
        s.wait_ge(S.h["sg"], 1)
        s.activation(out=cst[:, 6:7], in_=cst[:, C_ONE:C_ONE + 1], func=AF.Ln)
        s.activation(out=cst[:, 6:7], in_=cst[:, C_ONE:C_ONE + 1], func=AF.Silu)
        S.inc(s.activation(out=aneg[:], in_=alog[:], func=AF.Exp), "sa")
    t_aexp = S.c["sa"]

    @block.vector
    def _(v):
        v.wait_ge(S.h["sa"], t_aexp)
        S.inc(v.tensor_scalar_mul(aneg[:], aneg[:], -1.0), "sv")
    T_ANEG = S.c["sv"]

    if stop_after == "S":
        qs_f, qg_f = S.c["qs"], S.c["qg"]

        @block.gpsimd
        def _(g):
            g.wait_ge(S.h["qs"], qs_f)
            g.wait_ge(S.h["qg"], qg_f)
            g.wait_ge(S.h["sv"], T_ANEG)
        es_mw.close()
        es.close()
        return nc

    es_a = ExitStack()

    def sba(name, shape, dtype):
        return es_a.enter_context(nc.sbuf_tensor(name, shape, dtype))

    xa = sba("xa", [P, 2 * 4 * D], BF16)
    stats = sba("stats", [P, 12], F32)
    mvb = sba("mvb", [P, 2], F32)
    istd = sba("istd", [P, 2 * 4], F32)
    negm = sba("negm", [P, 2 * 4], F32)
    xn = sba("xn", [P, 2 * 4 * D], BF16)
    xnt = sba("xnt", [P, 2 * 8 * 512], BF16)
    xcp = sba("xcp", [P, 2 * 8 * 515], BF16)
    cvt = sba("cvt", [P, 8 * 512], BF16)
    cvtm = sba("cvtm", [P, 512], BF16)
    xco = sba("xco", [P, 2 * 8 * 512], BF16)
    zso = sba("zso", [P, 2 * 8 * 512], BF16)
    ps_tr = [es_a.enter_context(nc.psum_tensor(f"ps_tr{i}", [P, 512], BF16))
             for i in range(4)]
    ps_xz = [es_a.enter_context(nc.psum_tensor(f"ps_xz{i}", [P, 512], F32))
             for i in range(4)]

    t_ln = [0] * nCH
    t_tr = [0] * nCH
    t_trev = [0] * nCH
    t_mm = [0] * nCH
    t_evz = [0] * nCH
    t_cv = [0] * nCH
    t_sil = [0] * nCH
    t_silq = [0] * nCH
    t_trg, t_trgev, t_mmg, t_mmgev = [], [], [], []

    for k in range(nCH):
        pb = k % 2
        xa_v = xa[:, pb * 4 * D:(pb + 1) * 4 * D]
        xn_v = xn[:, pb * 4 * D:(pb + 1) * 4 * D]
        xnt_v = xnt[:, pb * 8 * 512:(pb + 1) * 8 * 512]
        xcp_v = xcp[:, pb * 8 * 515:(pb + 1) * 8 * 515]
        xcp_p = xcp[:, (1 - pb) * 8 * 515:(2 - pb) * 8 * 515]
        cvt_v = cvt[:, 0:8 * 512]
        xco_v = xco[:, pb * 8 * 512:(pb + 1) * 8 * 512]
        zso_v = zso[:, pb * 8 * 512:(pb + 1) * 8 * 512]
        istd_v = istd[:, pb * 4:(pb + 1) * 4]
        negm_v = negm[:, pb * 4:(pb + 1) * 4]

        @block.gpsimd
        def _(g, k=k, xa_v=xa_v):
            if k >= 2:
                g.wait_ge(S.h["sv"], t_ln[k - 2])
            for i in range(4):
                t0 = k * 512 + i * 128
                S.inc(g.dma_start(out=xa_v[:, i * D:(i + 1) * D],
                                  in_=dxin[t0:t0 + 128, :]), "qg", 16)
        t_ldx = S.c["qg"]

        @block.vector
        def _(v, t_ldx=t_ldx, xa_v=xa_v, pb=pb):
            v.wait_ge(S.h["qg"], t_ldx)
            for i in range(4):
                v.bn_stats(stats[:, 0:6], xa_v[:, i * D:i * D + 512])
                _li = v.bn_stats(stats[:, 6:12], xa_v[:, i * D + 512:(i + 1) * D])
                S.inc(_li, "sv")
                v.wait_ge(S.h["sv"], S.c["sv"])
                _li = v.bn_aggr(mvb[:], stats[:])
                S.inc(_li, "sv")
                v.wait_ge(S.h["sv"], S.c["sv"])
                v.tensor_copy(istd[:, pb * 4 + i:pb * 4 + i + 1], mvb[:, 1:2])
                _li = v.tensor_copy(negm[:, pb * 4 + i:pb * 4 + i + 1], mvb[:, 0:1])
            S.inc(_li, "sv")
        t_bn = S.c["sv"]

        @block.scalar
        def _(s, t_bn=t_bn, istd_v=istd_v):
            s.wait_ge(S.h["sv"], t_bn)
            for i in range(4):
                s.activation(out=istd_v[:, i:i + 1], in_=istd_v[:, i:i + 1],
                             func=AF.Ln, bias=cc_(C_EPS))
                _li = s.activation(out=istd_v[:, i:i + 1], in_=istd_v[:, i:i + 1],
                             func=AF.Exp, scale=cc_(C_MH))
            S.inc(_li, "sa")
        t_istd = S.c["sa"]

        @block.vector
        def _(v, k=k, t_istd=t_istd, xa_v=xa_v, xn_v=xn_v,
              istd_v=istd_v, negm_v=negm_v):
            v.wait_ge(S.h["sa"], t_istd)
            if k >= 2:
                v.wait_ge(S.h["sp"], t_tr[k - 2])
            for i in range(4):
                _li = v.tensor_scalar(out=negm_v[:, i:i + 1],
                                      in0=negm_v[:, i:i + 1],
                                      scalar1=istd_v[:, i:i + 1],
                                      scalar2=-1.0,
                                      op0=ALU.mult, op1=ALU.mult)
            S.inc(_li, "sv")
            v.wait_ge(S.h["sv"], S.c["sv"])
            for i in range(4):
                _li = v.tensor_scalar(out=xn_v[:, i * D:(i + 1) * D],
                                in0=xa_v[:, i * D:(i + 1) * D],
                                scalar1=negm_v[:, i:i + 1],
                                scalar2=istd_v[:, i:i + 1],
                                op0=ALU.add, op1=ALU.mult)
            S.inc(_li, "sv")
        t_ln[k] = S.c["sv"]

        for g_ in range(4):
            gidx = len(t_trg)

            @block.tensor
            def _(t, k=k, g_=g_, gidx=gidx, xn_v=xn_v):
                if g_ == 0:
                    t.wait_ge(S.h["sv"], t_ln[k])
                    if k == 0:
                        t.wait_ge(S.h["qg"], QG_W)
                if gidx >= 2:
                    t.wait_ge(S.h["sv"], t_trgev[gidx - 2])
                for j in range(2):
                    dt = g_ * 2 + j
                    pt = ps_tr[(g_ % 2) * 2 + j]
                    for i in range(4):
                        _li = t.transpose(pt[:, i * P:(i + 1) * P],
                                    xn_v[:, i * D + dt * P:i * D + (dt + 1) * P],
                                    idb[:])
                S.inc(_li, "sp")
            t_trg.append(S.c["sp"])

            @block.vector
            def _(v, k=k, g_=g_, gidx=gidx, xnt_v=xnt_v):
                v.wait_ge(S.h["sp"], t_trg[gidx])
                if k >= 2 and g_ == 0:
                    v.wait_ge(S.h["qo"], t_silq[k - 2])
                    v.wait_ge(S.h["sp"], t_mm[k - 2])
                for j in range(2):
                    dt = g_ * 2 + j
                    _li = v.tensor_copy(xnt_v[:, dt * 512:(dt + 1) * 512],
                                        ps_tr[(g_ % 2) * 2 + j][:])
                S.inc(_li, "sv")
            t_trgev.append(S.c["sv"])
        t_tr[k] = S.c["sp"]
        t_trev[k] = S.c["sv"]

        @block.scalar
        def _(s, k=k, xnt_v=xnt_v):
            s.wait_ge(S.h["sv"], t_trev[k])
            S.inc(s.dma_start(
                out=pxnt.rearrange("d p l -> p d l")[:, :, k * 512:(k + 1) * 512],
                in_=xnt_v[:].rearrange("p (d t) -> p d t", d=8)), "qo", 16)

        for g_ in range(8):
            gidx = len(t_mmg)

            @block.tensor
            def _(t, k=k, g_=g_, gidx=gidx, xnt_v=xnt_v):
                if g_ == 0:
                    t.wait_ge(S.h["sv"], t_trev[k])
                    if k == 0:
                        t.wait_ge(S.h["sv"], T_ANEG)
                    if k >= 2:
                        t.wait_ge(S.h["sa"], t_sil[k - 2])
                    if k >= 1:
                        t.wait_ge(S.h["sv"], t_cv[k - 1])
                if gidx >= 2:
                    sem_, val_ = t_mmgev[gidx - 2]
                    t.wait_ge(S.h[sem_], val_)
                for j in range(2):
                    mt = g_ * 2 + j
                    pj = ps_xz[(g_ % 2) * 2 + j]
                    for kt in range(8):
                        _li = t.matmul(pj[:], win_t(kt, mt),
                                 xnt_v[:, kt * 512:(kt + 1) * 512],
                                 start=(kt == 0), stop=(kt == 7))
                S.inc(_li, "sp")
            t_mmg.append(S.c["sp"])

            if g_ < 4:
                @block.vector
                def _(v, g_=g_, gidx=gidx, xcp_v=xcp_v):
                    v.wait_ge(S.h["sp"], t_mmg[gidx])
                    for j in range(2):
                        mt = g_ * 2 + j
                        pj = ps_xz[(g_ % 2) * 2 + j]
                        _li = v.tensor_copy(
                            xcp_v[:, mt * 515 + 3:mt * 515 + 515], pj[:])
                    S.inc(_li, "sv")
                t_mmgev.append(("sv", S.c["sv"]))
            else:
                @block.scalar
                def _(s, g_=g_, gidx=gidx, zso_v=zso_v):
                    s.wait_ge(S.h["sp"], t_mmg[gidx])
                    for j in range(2):
                        z = g_ * 2 + j - 8
                        pj = ps_xz[(g_ % 2) * 2 + j]
                        _li = s.activation(out=zso_v[:, z * 512:(z + 1) * 512],
                                     in_=pj[:], func=AF.Silu)
                    S.inc(_li, "sa")
                t_mmgev.append(("sa", S.c["sa"]))
        t_mm[k] = S.c["sp"]
        t_evz[k] = (S.c["sa"], S.c["sv"])

        @block.scalar
        def _(s, k=k, zso_v=zso_v):
            s.wait_ge(S.h["sa"], t_evz[k][0])
            S.inc(s.dma_start(
                out=pz.rearrange("d p l -> p d l")[:, :, k * 512:(k + 1) * 512],
                in_=zso_v[:].rearrange("p (d t) -> p d t", d=8)), "qo", 16)

        @block.vector
        def _(v, k=k, xcp_v=xcp_v, xcp_p=xcp_p, cvt_v=cvt_v):
            v.wait_ge(S.h["sa"], t_evz[k][0])
            v.wait_ge(S.h["sv"], t_evz[k][1])
            if k >= 1:
                v.wait_ge(S.h["sa"], t_sil[k - 1])
            for dt in range(8):
                if k == 0:
                    v.memset(xcp_v[:, dt * 515:dt * 515 + 3], 0.0)
                else:
                    v.tensor_copy(xcp_v[:, dt * 515:dt * 515 + 3],
                                  xcp_p[:, dt * 515 + 512:dt * 515 + 515])
                cv = cvt_v[:, dt * 512:(dt + 1) * 512]
                v.tensor_scalar_mul(cv, xcp_v[:, dt * 515:dt * 515 + 512],
                                    cwv[:, dt * 4:dt * 4 + 1])
                for j in range(1, 4):
                    v.tensor_scalar_mul(cvtm[:],
                                        xcp_v[:, dt * 515 + j:dt * 515 + j + 512],
                                        cwv[:, dt * 4 + j:dt * 4 + j + 1])
                    _li = v.tensor_tensor(cv, cv, cvtm[:], ALU.add)
            S.inc(_li, "sv")
        t_cv[k] = S.c["sv"]

        @block.scalar
        def _(s, k=k, cvt_v=cvt_v, xco_v=xco_v):
            s.wait_ge(S.h["sv"], t_cv[k])
            if k >= 2:
                s.wait_ge(S.h["qo"], t_silq[k - 2])
            for dt in range(8):
                _li = s.activation(out=xco_v[:, dt * 512:(dt + 1) * 512],
                                   in_=cvt_v[:, dt * 512:(dt + 1) * 512],
                                   func=AF.Silu, bias=cbv[:, dt:dt + 1])
            S.inc(_li, "sa")
            s.wait_ge(S.h["sa"], S.c["sa"])
            h = get_h(s, "scalar")
            src = xco_v[:].rearrange("p (d t) -> p d t", d=8)
            with s.If_eq(h, 0):
                s.dma_start(out=shxc.rearrange("d p l -> p d l")[
                    :, 0:8, k * 512:(k + 1) * 512],
                    in_=src).then_inc(S.h["qo"], 16)
            with s.Else():
                s.dma_start(out=shxc.rearrange("d p l -> p d l")[
                    :, 8:16, k * 512:(k + 1) * 512],
                    in_=src).then_inc(S.h["qo"], 16)
            S.c["qo"] += 16
        t_sil[k] = S.c["sa"]
        t_silq[k] = S.c["qo"]

    QO_A = S.c["qo"]
    QS_A = S.c["qs"]

    @block.gpsimd
    def _(g):
        g.wait_ge(S.h["qo"], QO_A)
        g.wait_ge(S.h["qs"], QS_A)
        g.wait_ge(S.h["sa"], t_sil[nCH - 1])
        g.wait_ge(S.h["qg"], QG_W)
        S.inc(g.collective_compute(
            "AllReduce", ALU.add, replica_groups=PAIRS,
            ins=[bi1], outs=[bo1]), "cc", 1)
    T_BAR1 = S.c["cc"]

    if stop_after == "A":
        dxc = nc.dram_tensor("dbg_xc", [16, P, L], F32, kind="ExternalOutput").ap()
        dzz = nc.dram_tensor("dbg_z", [8, P, L], F32, kind="ExternalOutput").ap()
        dxn = nc.dram_tensor("dbg_xnt", [8, P, L], F32, kind="ExternalOutput").ap()
        dst_ = nc.dram_tensor("dbg_stats", [P, 32], F32, kind="ExternalOutput").ap()
        dxa = nc.dram_tensor("dbg_xa", [P, 4 * D], F32, kind="ExternalOutput").ap()
        qo_f, qs_f, qg_f = S.c["qo"], S.c["qs"], S.c["qg"]

        @block.gpsimd
        def _(g):
            g.wait_ge(S.h["qo"], qo_f)
            g.wait_ge(S.h["qs"], qs_f)
            g.wait_ge(S.h["qg"], qg_f)
            g.wait_ge(S.h["cc"], T_BAR1)
            g.dma_start(out=dxc, in_=shxc).then_inc(S.h["qg"], 16)
            g.dma_start(out=dzz, in_=pz).then_inc(S.h["qg"], 16)
            g.dma_start(out=dxn, in_=pxnt).then_inc(S.h["qg"], 16)
            g.dma_start(out=dst_[:, 0:8], in_=istd[:]).then_inc(S.h["qg"], 16)
            g.dma_start(out=dst_[:, 8:16], in_=negm[:]).then_inc(S.h["qg"], 16)
            g.dma_start(out=dxa, in_=xa[:, 0:4 * D]).then_inc(S.h["qg"], 16)
            g.wait_ge(S.h["qg"], qg_f + 96)
        es_a.close()
        es_wi.close()
        es_mw.close()
        es.close()
        return nc

    es_a.close()
    es_wi.close()

    es_b = ExitStack()

    def sbb(name, shape, dtype):
        return es_b.enter_context(nc.sbuf_tensor(name, shape, dtype))

    xcs2 = sbb("xcs2", [P, 2 * 16 * TS], BF16)
    zss2 = sbb("zss2", [P, 8 * TS], BF16)
    dlt = sbb("dlt", [P, 8 * TS], BF16)
    dxb = sbb("dxb", [P, 8 * TS], BF16)
    dab = sbb("dab", [P, 2 * 8 * TS], BF16)
    ubf = sbb("ubf", [P, TS], BF16)
    sbf = sbb("sbf", [P, TS], BF16)
    pbf2 = sbb("pbf2", [P, 2 * 8 * TS], BF16)
    yac = sbb("yac", [P, 8 * TS], BF16)
    yfn2 = sbb("yfn2", [P, 2 * 8 * TS], BF16)
    bsb = sbb("bsb", [NST, TS], BF16)
    bbc = sbb("bbc", [P, NST * TS], BF16)
    pre = sbb("pre", [P, 2 * TS], F32)
    spa = sbb("spa", [P, TS], F32)
    spr = sbb("spr", [P, 2 * TS], BF16)
    post = sbb("post", [P, 8 * TS], BF16)
    ps_dt = [es_b.enter_context(nc.psum_tensor(f"ps_dt{i}", [P, TS], F32))
             for i in range(2)]
    ps_bt = es_b.enter_context(nc.psum_tensor("ps_bt", [NST, TS], F32))
    ps_bb = es_b.enter_context(nc.psum_tensor("ps_bb", [P, TS], F32))
    ps_wo = [es_b.enter_context(nc.psum_tensor(f"ps_wo{i}", [P, TS], F32))
             for i in range(2)]

    t_ldb = [0] * nSU
    t_ldz = [0] * nSU
    t_powq = [0] * nSU
    prev = None

    prev2 = None
    for s_ in range(nSU):
        t0 = s_ * TS
        par = s_ % 2
        xcs = xcs2[:, par * 16 * TS:(par + 1) * 16 * TS]
        zss = zss2[:, 0:8 * TS]
        yfn = yfn2[:, par * 8 * TS:(par + 1) * 8 * TS]
        yfn_p = yfn2[:, (1 - par) * 8 * TS:(2 - par) * 8 * TS]
        sp0, sa0, sv0 = S.c["sp"], S.c["sa"], S.c["sv"]

        has_wo = s_ >= 1
        SP_WX2 = sp0 + 1
        SP_WDT = [SP_WX2 + 1 + m for m in range(8)]
        SP_BB = [SP_WX2 + 9 + n for n in range(16)]
        SP_WO = [SP_WX2 + 25 + m for m in range(8)] if has_wo else []
        SA_BSB = sa0 + 1
        SA_SP = [SA_BSB + 1 + m for m in range(8)]
        SA_DA = [SA_BSB + 9 + n for n in range(16)]
        SA_WOEV = [SA_BSB + 25 + m for m in range(8)] if has_wo else []
        seq = []
        for m in range(8):
            if m >= 2:
                seq.append(("d", m - 2))
            seq.append(("p", m))
        seq += [("d", 6), ("d", 7)]
        SV_PRE, SV_DLT = {}, {}
        for i, (kind, m) in enumerate(seq):
            (SV_PRE if kind == "p" else SV_DLT)[m] = sv0 + i + 1
        SV_DX = sv0 + 17
        SV_BBEV = [SV_DX + 1 + n for n in range(16)]
        SV_SCAN = [SV_DX + 17 + n for n in range(16)]
        SV_YFN = SV_DX + 33

        @block.sync
        def _(sy, s_=s_, t0=t0, xcs=xcs, zss=zss):
            if s_ == 0:
                sy.wait_ge(S.h["qo"], QO_A)
            if s_ >= 2:
                sy.wait_ge(S.h["sv"], prev2["SV_YFN"])
                sy.wait_ge(S.h["sp"], prev2["SP_WDT7"])
            h = get_h(sy, "sync")
            xv = shxc.rearrange("d p l -> p d l")
            with sy.If_eq(h, 0):
                sy.dma_start(out=xcs[:, 0:8 * TS].rearrange(
                    "p (d t) -> p d t", d=8),
                    in_=xv[:, 0:8, t0:t0 + TS]).then_inc(S.h["qs"], 16)
            with sy.Else():
                sy.dma_start(out=xcs[:, 0:8 * TS].rearrange(
                    "p (d t) -> p d t", d=8),
                    in_=xv[:, 8:16, t0:t0 + TS]).then_inc(S.h["qs"], 16)
            S.c["qs"] += 16
            if s_ == 0:
                sy.wait_ge(S.h["cc"], T_BAR1)
            with sy.If_eq(h, 0):
                sy.dma_start(out=xcs[:, 8 * TS:16 * TS].rearrange(
                    "p (d t) -> p d t", d=8),
                    in_=xv[:, 8:16, t0:t0 + TS]).then_inc(S.h["qs"], 16)
            with sy.Else():
                sy.dma_start(out=xcs[:, 8 * TS:16 * TS].rearrange(
                    "p (d t) -> p d t", d=8),
                    in_=xv[:, 0:8, t0:t0 + TS]).then_inc(S.h["qs"], 16)
            S.c["qs"] += 16
        t_ldb[s_] = S.c["qs"]

        @block.sync
        def _(sy, s_=s_, t0=t0, zss=zss, prev=prev):
            if s_ >= 1:
                sy.wait_ge(S.h["sv"], prev["SV_YFN"])
            S.inc(sy.dma_start(out=zss[:].rearrange("p (d t) -> p d t", d=8),
                               in_=pz.rearrange("d p l -> p d l")[
                                   :, :, t0:t0 + TS]), "qs", 16)
        t_ldz[s_] = S.c["qs"]

        @block.tensor
        def _(t, s_=s_, SP_WDT=SP_WDT, SV_PRE=SV_PRE, SV_BBEV=SV_BBEV,
              SV_YFN=SV_YFN, SA_BSB=SA_BSB, SA_WOEV=SA_WOEV, SA_DA=SA_DA,
              prev=prev, xcs=xcs, yfn=yfn, yfn_p=yfn_p):
            t.wait_ge(S.h["qs"], t_ldb[s_])
            if s_ == 0:
                t.wait_ge(S.h["cc"], T_BAR1)
                t.wait_ge(S.h["qg"], QG_W)
            else:
                t.wait_ge(S.h["sa"], prev["SA_BSB"])
            for kt in range(16):
                _li = t.matmul(ps_bt[:], wx2[:, kt * NST:(kt + 1) * NST],
                               xcs[:, kt * TS:(kt + 1) * TS],
                               start=(kt == 0), stop=(kt == 15))
            S.inc(_li, "sp")
            for mt in range(8):
                if mt >= 2:
                    t.wait_ge(S.h["sv"], SV_PRE[mt - 2])
                if mt == 0 and s_ >= 1:
                    t.wait_ge(S.h["sa"], prev["SA_DA15"])
                for kt in range(16):
                    _li = t.matmul(ps_dt[mt % 2][:], wdt_t(kt, mt),
                                   xcs[:, kt * TS:(kt + 1) * TS],
                                   start=(kt == 0), stop=(kt == 15))
                S.inc(_li, "sp")
            for n in range(NST):
                if n == 0:
                    t.wait_ge(S.h["sa"], SA_BSB)
                else:
                    t.wait_ge(S.h["sv"], SV_BBEV[n - 1])
                _li = t.matmul(ps_bb[:], e16[:, n * P:(n + 1) * P], bsb[:],
                               start=True, stop=True)
                S.inc(_li, "sp")
            if s_ >= 1:
                for mt in range(8):
                    if mt == 0:
                        t.wait_ge(S.h["sv"], prev["SV_YFN"])
                        if s_ >= 2:
                            t.wait_ge(S.h["sa"], prev["SA_WOEV7"])
                    if mt >= 2:
                        t.wait_ge(S.h["sa"], SA_WOEV[mt - 2])
                    for kt in range(8):
                        _li = t.matmul(ps_wo[mt % 2][:], wout_t(kt, mt),
                                       yfn_p[:, kt * TS:(kt + 1) * TS],
                                       start=(kt == 0), stop=(kt == 7))
                    S.inc(_li, "sp")

        @block.scalar
        def _(s, s_=s_, t0=t0, SP_WX2=SP_WX2, SP_WDT=SP_WDT, SP_BB=SP_BB,
              SP_WO=SP_WO, SV_PRE=SV_PRE, SV_DLT=SV_DLT, SV_DX=SV_DX,
              SV_SCAN=SV_SCAN, SA_WOEV=SA_WOEV, prev=prev):
            s.wait_ge(S.h["sp"], SP_WX2)
            if s_ >= 1:
                s.wait_ge(S.h["sp"], prev["SP_BB15"])
            _li = s.activation(out=bsb[:], in_=ps_bt[:], func=AF.Copy)
            S.inc(_li, "sa")
            for mt in range(8):
                s.wait_ge(S.h["sv"], SV_PRE[mt])
                if mt >= 2:
                    s.wait_ge(S.h["sv"], SV_DLT[mt - 2])
                pv = pre[:, (mt % 2) * TS:(mt % 2 + 1) * TS]
                s.activation(out=spa[:], in_=pv, func=AF.Abs)
                s.activation(out=spa[:], in_=spa[:], func=AF.Exp,
                             scale=cc_(C_MONE))
                _li = s.activation(out=spr[:, (mt % 2) * TS:(mt % 2 + 1) * TS],
                                   in_=spa[:], func=AF.Ln, bias=cc_(C_ONE))
                S.inc(_li, "sa")
            for n in range(NST):
                if n == 0:
                    s.wait_ge(S.h["sv"], SV_DX)
                elif n >= 2:
                    s.wait_ge(S.h["sv"], SV_SCAN[n - 2])
                np_ = n % 2
                for dt in range(8):
                    _li = s.activation(
                        out=dab[:, (np_ * 8 + dt) * TS:(np_ * 8 + dt + 1) * TS],
                        in_=dlt[:, dt * TS:(dt + 1) * TS], func=AF.Exp,
                        scale=aneg[:, dt * NST + n:dt * NST + n + 1])
                S.inc(_li, "sa")
            if s_ >= 1:
                for mt in range(8):
                    s.wait_ge(S.h["sp"], SP_WO[mt])
                    if s_ >= 2 and mt == 0:
                        s.wait_ge(S.h["qo"], t_powq[s_ - 1])
                    _li = s.activation(out=post[:, mt * TS:(mt + 1) * TS],
                                       in_=ps_wo[mt % 2][:], func=AF.Copy)
                    S.inc(_li, "sa")
                s.wait_ge(S.h["sa"], S.c["sa"])
                h = get_h(s, "scalar")
                srcp = post[:].rearrange("p (d t) -> p d t", d=8)
                tp0 = (s_ - 1) * TS
                with s.If_eq(h, 0):
                    s.dma_start(out=shpo.rearrange("w d p l -> p w d l")[
                        :, 0, :, tp0:tp0 + TS], in_=srcp).then_inc(S.h["qo"], 16)
                with s.Else():
                    s.dma_start(out=shpo.rearrange("w d p l -> p w d l")[
                        :, 1, :, tp0:tp0 + TS], in_=srcp).then_inc(S.h["qo"], 16)
                S.c["qo"] += 16
        t_powq[s_] = S.c["qo"]

        sg_base = S.c["sg"]
        SG_SCAN = [sg_base + n for n in range(NST)]

        @block.vector
        def _(v, s_=s_, seq=seq, SP_WDT=SP_WDT, SP_BB=SP_BB, SA_SP=SA_SP,
              SA_DA=SA_DA, prev=prev, xcs=xcs, zss=zss, yfn=yfn,
              SG_SCAN=SG_SCAN):
            for kind, m in seq:
                if kind == "p":
                    v.wait_ge(S.h["sp"], SP_WDT[m])
                    if m >= 2:
                        v.wait_ge(S.h["sa"], SA_SP[m - 2])
                    _li = v.tensor_scalar(
                        out=pre[:, (m % 2) * TS:(m % 2 + 1) * TS],
                        in0=ps_dt[m % 2][:], scalar1=bdtv[:, m:m + 1],
                        scalar2=None, op0=ALU.add)
                    S.inc(_li, "sv")
                else:
                    v.wait_ge(S.h["sa"], SA_SP[m])
                    pv = pre[:, (m % 2) * TS:(m % 2 + 1) * TS]
                    dv = dlt[:, m * TS:(m + 1) * TS]
                    v.tensor_relu(dv, pv)
                    _li = v.tensor_tensor(
                        dv, dv, spr[:, (m % 2) * TS:(m % 2 + 1) * TS], ALU.add)
                    S.inc(_li, "sv")
            for dt in range(8):
                _li = v.tensor_tensor(dxb[:, dt * TS:(dt + 1) * TS],
                                      dlt[:, dt * TS:(dt + 1) * TS],
                                      xcs[:, dt * TS:(dt + 1) * TS], ALU.mult)
            S.inc(_li, "sv")
            for n in range(NST):
                v.wait_ge(S.h["sp"], SP_BB[n])
                _li = v.tensor_copy(bbc[:, n * TS:(n + 1) * TS], ps_bb[:])
                S.inc(_li, "sv")
            for n in range(NST):
                v.wait_ge(S.h["sa"], SA_DA[n])
                if n >= 3:
                    v.wait_ge(S.h["sg"], SG_SCAN[n - 2])
                np_ = n % 2
                bb_v = bbc[:, n * TS:(n + 1) * TS]
                for dt in range(8):
                    da_v = dab[:, (np_ * 8 + dt) * TS:(np_ * 8 + dt + 1) * TS]
                    v.tensor_tensor(ubf[:], dxb[:, dt * TS:(dt + 1) * TS],
                                    bb_v, ALU.mult)
                    ini = 0.0 if s_ == 0 else carry[:, n * 8 + dt:n * 8 + dt + 1]
                    v.tensor_tensor_scan(sbf[:], da_v, ubf[:], ini,
                                         ALU.mult, ALU.add)
                    if n == 0:
                        v.tensor_tensor(yac[:, dt * TS:(dt + 1) * TS],
                                        sbf[:], bb_v, ALU.mult)
                    else:
                        v.tensor_tensor(
                            pbf2[:, (np_ * 8 + dt) * TS:(np_ * 8 + dt + 1) * TS],
                            sbf[:], bb_v, ALU.mult)
                    _li = v.tensor_copy(carry[:, n * 8 + dt:n * 8 + dt + 1],
                                        sbf[:, TS - 1:TS])
                S.inc(_li, "sv")
            if s_ >= 2:
                v.wait_ge(S.h["sp"], prev["SP_WO7"])
            v.wait_ge(S.h["sg"], SG_SCAN[15])
            v.wait_ge(S.h["qs"], t_ldz[s_])
            for dt in range(8):
                v.scalar_tensor_tensor(
                    out=yfn[:, dt * TS:(dt + 1) * TS],
                    in0=xcs[:, dt * TS:(dt + 1) * TS],
                    scalar=dpv[:, dt:dt + 1],
                    in1=yac[:, dt * TS:(dt + 1) * TS],
                    op0=ALU.mult, op1=ALU.add)
                _li = v.tensor_tensor(yfn[:, dt * TS:(dt + 1) * TS],
                                      yfn[:, dt * TS:(dt + 1) * TS],
                                      zss[:, dt * TS:(dt + 1) * TS], ALU.mult)
            S.inc(_li, "sv")

        @block.gpsimd
        def _(g, SV_SCAN=SV_SCAN):
            for n in range(1, NST):
                g.wait_ge(S.h["sv"], SV_SCAN[n])
                np_ = n % 2
                _li = g.tensor_tensor(
                    yac[:], yac[:],
                    pbf2[:, np_ * 8 * TS:(np_ + 1) * 8 * TS], ALU.add)
                S.inc(_li, "sg")
        assert S.c["sg"] == sg_base + 15, (S.c["sg"], sg_base)

        assert S.c["sp"] == (SP_WO[7] if has_wo else SP_BB[15]), S.c["sp"]
        assert S.c["sa"] == (SA_WOEV[7] if has_wo else SA_DA[15]), S.c["sa"]
        assert S.c["sv"] == SV_YFN, (S.c["sv"], SV_YFN)
        prev2 = prev
        prev = {"SV_YFN": SV_YFN, "SP_WDT7": SP_WDT[7], "SA_BSB": SA_BSB,
                "SA_DA15": SA_DA[15],
                "SA_WOEV7": SA_WOEV[7] if has_wo else
                (prev["SA_WOEV7"] if prev else 0),
                "SP_BB15": SP_BB[15],
                "SP_WO7": SP_WO[7] if has_wo else
                (prev["SP_WO7"] if prev else 0)}


    par_f = (nSU - 1) % 2
    yfn_f = yfn2[:, par_f * 8 * TS:(par_f + 1) * 8 * TS]
    SP_WOF = [S.c["sp"] + 1 + m for m in range(8)]
    SA_WOFE = [S.c["sa"] + 1 + m for m in range(8)]

    @block.tensor
    def _(t):
        for mt in range(8):
            if mt == 0:
                t.wait_ge(S.h["sv"], prev["SV_YFN"])
                t.wait_ge(S.h["sa"], prev["SA_WOEV7"])
            if mt >= 2:
                t.wait_ge(S.h["sa"], SA_WOFE[mt - 2])
            for kt in range(8):
                _li = t.matmul(ps_wo[mt % 2][:], wout_t(kt, mt),
                               yfn_f[:, kt * TS:(kt + 1) * TS],
                               start=(kt == 0), stop=(kt == 7))
            S.inc(_li, "sp")

    @block.scalar
    def _(s):
        for mt in range(8):
            s.wait_ge(S.h["sp"], SP_WOF[mt])
            if mt == 0:
                s.wait_ge(S.h["qo"], t_powq[nSU - 1])
            _li = s.activation(out=post[:, mt * TS:(mt + 1) * TS],
                               in_=ps_wo[mt % 2][:], func=AF.Copy)
            S.inc(_li, "sa")
        s.wait_ge(S.h["sa"], S.c["sa"])
        h = get_h(s, "scalar")
        srcp = post[:].rearrange("p (d t) -> p d t", d=8)
        tf0 = (nSU - 1) * TS
        with s.If_eq(h, 0):
            s.dma_start(out=shpo.rearrange("w d p l -> p w d l")[
                :, 0, :, tf0:tf0 + TS], in_=srcp).then_inc(S.h["qo"], 16)
        with s.Else():
            s.dma_start(out=shpo.rearrange("w d p l -> p w d l")[
                :, 1, :, tf0:tf0 + TS], in_=srcp).then_inc(S.h["qo"], 16)
        S.c["qo"] += 16
    assert S.c["sp"] == SP_WOF[7]
    assert S.c["sa"] == SA_WOFE[7]

    QO_B = S.c["qo"]

    @block.gpsimd
    def _(g):
        g.wait_ge(S.h["qo"], QO_B)
        S.inc(g.collective_compute(
            "AllReduce", ALU.add, replica_groups=PAIRS,
            ins=[bi2], outs=[bo2]), "cc", 1)
    T_BAR2 = S.c["cc"]

    if stop_after == "B":
        dpo = nc.dram_tensor("dbg_po", [2, 8, P, L], F32,
                             kind="ExternalOutput").ap()
        ddl = nc.dram_tensor("dbg_dlt", [P, 8 * TS], F32,
                             kind="ExternalOutput").ap()
        dbs = nc.dram_tensor("dbg_bsb", [NST, TS], F32,
                             kind="ExternalOutput").ap()
        dbb = nc.dram_tensor("dbg_bbc", [P, NST * TS], F32,
                             kind="ExternalOutput").ap()
        dya = nc.dram_tensor("dbg_yac", [P, 8 * TS], F32,
                             kind="ExternalOutput").ap()
        dyf = nc.dram_tensor("dbg_yfn", [P, 8 * TS], F32,
                             kind="ExternalOutput").ap()
        qo_f, qs_f, qg_f = S.c["qo"], S.c["qs"], S.c["qg"]

        @block.gpsimd
        def _(g):
            g.wait_ge(S.h["qo"], qo_f)
            g.wait_ge(S.h["qs"], qs_f)
            g.wait_ge(S.h["qg"], qg_f)
            g.wait_ge(S.h["cc"], T_BAR2)
            g.dma_start(out=dpo, in_=shpo).then_inc(S.h["qg"], 16)
            g.dma_start(out=ddl, in_=dlt[:]).then_inc(S.h["qg"], 16)
            g.dma_start(out=dbs, in_=bsb[:]).then_inc(S.h["qg"], 16)
            g.dma_start(out=dbb, in_=bbc[:]).then_inc(S.h["qg"], 16)
            g.dma_start(out=dya, in_=yac[:]).then_inc(S.h["qg"], 16)
            g.dma_start(out=dyf, in_=yfn[:]).then_inc(S.h["qg"], 16)
            g.wait_ge(S.h["qg"], qg_f + 96)
        es_b.close()
        es_mw.close()
        es.close()
        return nc

    es_b.close()
    es_mw.close()

    if False:
        pass

    es_c = ExitStack()

    def sbc(name, shape, dtype):
        return es_c.enter_context(nc.sbuf_tensor(name, shape, dtype))

    xnh = sbc("xnh", [P, 8 * TH], BF16)
    racc = sbc("racc", [P, 8 * TH], F32)
    cft = sbc("cft", [NE + 1, TH], BF16)
    lgb = sbc("lgb", [NE, 512], F32)
    spb = sbc("spb", [NE, 512], F32)
    acb = sbc("acb", [NE, 512], F32)
    sptok = sbc("sptok", [P, 4 * NE], F32)
    actok = sbc("actok", [P, 4 * NE], F32)
    cfe = sbc("cfe", [P, 4 * (NE + 1)], F32)
    m1b = sbc("m1b", [P, 3], F32)
    rwT = sbc("rwT", [P, 8 * NE], BF16)
    rb2x = sbc("rb2x", [NE + 1, D], BF16)
    w1e = sbc("w1e", [P, 2 * 8 * HF], BF16)
    w2e = sbc("w2e", [P, 2 * 4 * D], BF16)
    hbuf = sbc("hbuf", [P, 4 * 512], BF16)
    cfbb = sbc("cfbb", [P, 512], BF16)
    rtmp = sbc("rtmp", [P, 2 * 512], BF16)
    pobuf = sbc("pobuf", [P, 2 * 8 * 512], BF16)
    xrl = sbc("xrl", [P, 4 * D], F32)
    otok = sbc("otok", [P, 2 * D], F32)

    @block.gpsimd
    def _(g):
        g.wait_ge(S.h["cc"], T_BAR2)
        S.inc(g.dma_start(out=rwT[:].rearrange("p (k e) -> p k e", k=8),
                          in_=drw.rearrange("(k p) e -> p k e", p=P)), "qg", 16)
        S.inc(g.dma_start(out=rb2x[:], in_=db2), "qg", 16)
    QG_C0 = S.c["qg"]

    @block.sync
    def _(sy):
        sy.wait_ge(S.h["cc"], T_BAR2)
        h = get_h(sy, "sync")
        with sy.If_eq(h, 0):
            sy.dma_start(out=xnh[:].rearrange("p (d t) -> p d t", d=8),
                         in_=pxnt.rearrange("d p l -> p d l")[
                             :, :, 0:TH]).then_inc(S.h["qs"], 16)
        with sy.Else():
            sy.dma_start(out=xnh[:].rearrange("p (d t) -> p d t", d=8),
                         in_=pxnt.rearrange("d p l -> p d l")[
                             :, :, TH:L]).then_inc(S.h["qs"], 16)
        S.c["qs"] += 16
    T_XNH = S.c["qs"]

    es_c1 = ExitStack()
    ps_lg = es_c1.enter_context(nc.psum_tensor("ps_lg", [NE, 512], F32))
    ps_tp = es_c1.enter_context(nc.psum_tensor("ps_tp", [P, 64], F32))
    ps_cf = es_c1.enter_context(nc.psum_tensor("ps_cf", [NE + 1, 512], F32))

    t_rt_done = 0
    t_top2 = [0] * nTB
    for tb in range(nTB):
        @block.tensor
        def _(t, tb=tb):
            t.wait_ge(S.h["qs"], T_XNH)
            if tb == 0:
                t.wait_ge(S.h["qg"], QG_C0)
            else:
                t.wait_ge(S.h["sv"], t_top2[tb - 1])
            for kt in range(8):
                _li = t.matmul(ps_lg[:], rwT[:, kt * NE:(kt + 1) * NE],
                         xnh[:, kt * TH + tb * 512:kt * TH + (tb + 1) * 512],
                         start=(kt == 0), stop=(kt == 7))
            S.inc(_li, "sp")
        t_lgm = S.c["sp"]

        @block.vector
        def _(v, t_lgm=t_lgm):
            v.wait_ge(S.h["sp"], t_lgm)
            _li = v.tensor_scalar(out=lgb[:], in0=ps_lg[:], scalar1=ebv[:, 0:1],
                            scalar2=None, op0=ALU.add)
            S.inc(_li, "sv")
        t_lg = S.c["sv"]

        @block.scalar
        def _(s, t_lg=t_lg):
            s.wait_ge(S.h["sv"], t_lg)
            s.activation(out=spb[:], in_=lgb[:], func=AF.Abs)
            s.activation(out=spb[:], in_=spb[:], func=AF.Exp,
                         scale=cst[0:NE, C_MONE:C_MONE + 1])
            _li = s.activation(out=spb[:], in_=spb[:], func=AF.Ln,
                         bias=cst[0:NE, C_ONE:C_ONE + 1])
            S.inc(_li, "sa")
        t_sp1 = S.c["sa"]

        @block.vector
        def _(v, t_sp1=t_sp1):
            v.wait_ge(S.h["sa"], t_sp1)
            v.tensor_relu(acb[:], lgb[:])
            v.tensor_tensor(spb[:], spb[:], acb[:], ALU.add)
            _li = v.tensor_scalar_max(spb[:], spb[:], cst[0:NE, C_TINY:C_TINY + 1])
            S.inc(_li, "sv")
        t_sp2 = S.c["sv"]

        @block.scalar
        def _(s, t_sp2=t_sp2):
            s.wait_ge(S.h["sv"], t_sp2)
            s.activation(out=acb[:], in_=spb[:], func=AF.Ln)
            _li = s.activation(out=acb[:], in_=acb[:], func=AF.Exp,
                         scale=cst[0:NE, C_HALF:C_HALF + 1])
            S.inc(_li, "sa")
        t_act = S.c["sa"]

        @block.tensor
        def _(t, t_act=t_act, tb=tb):
            t.wait_ge(S.h["sa"], t_act)
            if tb > 0:
                t.wait_ge(S.h["sa"], S.c["sa"])
            for tt in range(4):
                t.transpose(ps_tp[:, tt * 8:(tt + 1) * 8],
                            spb[:, tt * P:(tt + 1) * P], idf[0:NE, 0:NE])
                _li = t.transpose(ps_tp[:, 32 + tt * 8:32 + (tt + 1) * 8],
                            acb[:, tt * P:(tt + 1) * P], idf[0:NE, 0:NE])
            S.inc(_li, "sp")
        t_tpm = S.c["sp"]

        @block.scalar
        def _(s, t_tpm=t_tpm):
            s.wait_ge(S.h["sp"], t_tpm)
            s.activation(out=sptok[:], in_=ps_tp[:, 0:32], func=AF.Copy)
            _li = s.activation(out=actok[:], in_=ps_tp[:, 32:64], func=AF.Copy)
            S.inc(_li, "sa")
        t_tpe = S.c["sa"]

        @block.vector
        def _(v, t_tpe=t_tpe):
            v.wait_ge(S.h["sa"], t_tpe)
            def rtw(inst):
                S.inc(inst, "sv")
                v.wait_ge(S.h["sv"], S.c["sv"])
            for tt in range(4):
                sp_t = sptok[:, tt * NE:(tt + 1) * NE]
                ac_t = actok[:, tt * NE:(tt + 1) * NE]
                cf_t = cfe[:, tt * (NE + 1):(tt + 1) * (NE + 1)]
                rtw(v.reduce_max(m1b[:, 0:1], sp_t, axis=mybir.AxisListType.X))
                rtw(v.tensor_scalar(out=cf_t[:, 0:NE], in0=sp_t,
                                    scalar1=m1b[:, 0:1], scalar2=None,
                                    op0=ALU.is_equal))
                rtw(v.tensor_scalar_mul(cf_t[:, 0:NE], cf_t[:, 0:NE], 1e30))
                rtw(v.tensor_tensor(cf_t[:, 0:NE], sp_t, cf_t[:, 0:NE],
                                    ALU.subtract))
                rtw(v.reduce_max(m1b[:, 1:2], cf_t[:, 0:NE],
                                 axis=mybir.AxisListType.X))
                rtw(v.tensor_scalar(out=cf_t[:, 0:NE], in0=sp_t,
                                    scalar1=m1b[:, 1:2], scalar2=None,
                                    op0=ALU.is_ge))
                v.tensor_tensor(cf_t[:, 0:NE], cf_t[:, 0:NE], ac_t, ALU.mult)
                _li = v.memset(cf_t[:, NE:NE + 1], 1.0)
            S.inc(_li, "sv")
        t_top2[tb] = S.c["sv"]

        @block.tensor
        def _(t, tb=tb):
            t.wait_ge(S.h["sv"], t_top2[tb])
            if tb > 0:
                t.wait_ge(S.h["sa"], S.c["sa"])
            for tt in range(4):
                _li = t.transpose(ps_cf[:, tt * P:(tt + 1) * P],
                            cfe[:, tt * (NE + 1):(tt + 1) * (NE + 1)], idf[:])
            S.inc(_li, "sp")
        t_cfm = S.c["sp"]

        @block.scalar
        def _(s, t_cfm=t_cfm, tb=tb):
            s.wait_ge(S.h["sp"], t_cfm)
            _li = s.activation(out=cft[:, tb * 512:(tb + 1) * 512], in_=ps_cf[:],
                         func=AF.Copy)
            S.inc(_li, "sa")
        t_rt_done = S.c["sa"]

    es_c1.close()

    es_c2 = ExitStack()
    ps_h = [es_c2.enter_context(nc.psum_tensor(f"ps_h{i}", [P, 512], F32))
            for i in range(2)]
    ps_rt = [es_c2.enter_context(nc.psum_tensor(f"ps_rt{i}", [P, 512], F32))
             for i in range(2)]
    ps_cb = es_c2.enter_context(nc.psum_tensor("ps_cb", [P, 512], F32))

    e_order = [8] + list(range(8))
    sp0e, sa0e, sv0e = S.c["sp"], S.c["sa"], S.c["sv"]
    SP_MM1 = {}
    SP_CFB = {}
    SP_MM2 = {}
    SA_M1E = {}
    SV_CFB = {}
    SV_RT = {}
    spv, sav, svv = sp0e, sa0e, sv0e
    for ei, e in enumerate(e_order):
        for tb in range(nTB):
            for mt1 in range(4):
                spv += 1; SP_MM1[(ei, tb, mt1)] = spv
                sav += 1; SA_M1E[(ei, tb, mt1)] = sav
            if e != 8:
                spv += 1; SP_CFB[(ei, tb)] = spv
                svv += 1; SV_CFB[(ei, tb)] = svv
            for mt in range(8):
                spv += 1; SP_MM2[(ei, tb, mt)] = spv
                svv += 1; SV_RT[(ei, tb, mt)] = svv
    SP_E_END, SA_E_END, SV_E_END = spv, sav, svv

    def prev_mm2_last(ei, tb):
        if tb > 0:
            return SP_MM2[(ei, tb - 1, 7)]
        if ei > 0:
            return SP_MM2[(ei - 1, nTB - 1, 7)]
        return None

    t_wld = [0] * 9

    @block.gpsimd
    def _(g):
        for ei, e in enumerate(e_order):
            ep = ei % 2
            if ei >= 2:
                g.wait_ge(S.h["sp"], SP_MM2[(ei - 2, nTB - 1, 7)])
            S.inc(g.dma_start(
                out=w1e[:, ep * 8 * HF:(ep + 1) * 8 * HF].rearrange(
                    "p (k c) -> p k c", k=8),
                in_=dw1[e].rearrange("(k p) c -> p k c", p=P)), "qg", 16)
            S.inc(g.dma_start(
                out=w2e[:, ep * 4 * D:(ep + 1) * 4 * D].rearrange(
                    "p (k c) -> p k c", k=4),
                in_=dw2[e].rearrange("(k p) c -> p k c", p=P)), "qg", 16)
            t_wld[ei] = S.c["qg"]

    @block.tensor
    def _(t):
        t.wait_ge(S.h["sa"], t_rt_done)
        for ei, e in enumerate(e_order):
            ep = ei % 2
            for tb in range(nTB):
                if tb == 0:
                    t.wait_ge(S.h["qg"], t_wld[ei])
                for mt1 in range(4):
                    g_prev = (ei * nTB + tb) * 4 + mt1 - 2
                    if g_prev >= 0:
                        pei, r = divmod(g_prev, nTB * 4)
                        ptb, pmt1 = divmod(r, 4)
                        t.wait_ge(S.h["sa"], SA_M1E[(pei, ptb, pmt1)])
                    for kt in range(8):
                        _li = t.matmul(
                            ps_h[mt1 % 2][:],
                            w1e[:, ep * 8 * HF + kt * HF + mt1 * P:
                                ep * 8 * HF + kt * HF + (mt1 + 1) * P],
                            xnh[:, kt * TH + tb * 512:kt * TH + (tb + 1) * 512],
                            start=(kt == 0), stop=(kt == 7))
                    S.inc(_li, "sp")
                if e != 8:
                    keys = sorted(SV_CFB)
                    my_i = keys.index((ei, tb))
                    if my_i > 0:
                        t.wait_ge(S.h["sv"], SV_CFB[keys[my_i - 1]])
                    _li = t.matmul(ps_cb[:], e9[:, e * P:(e + 1) * P],
                                   cft[:, tb * 512:(tb + 1) * 512],
                                   start=True, stop=True)
                    S.inc(_li, "sp")
                for mt in range(8):
                    if mt == 0:
                        if e != 8:
                            t.wait_ge(S.h["sv"], SV_CFB[(ei, tb)])
                        else:
                            t.wait_ge(S.h["sa"], SA_M1E[(ei, tb, 3)])
                    g2 = (ei * nTB + tb) * 8 + mt - 2
                    if g2 >= 0:
                        pei, r = divmod(g2, nTB * 8)
                        ptb, pmt = divmod(r, 8)
                        t.wait_ge(S.h["sv"], SV_RT[(pei, ptb, pmt)])
                    for kt in range(4):
                        _li = t.matmul(
                            ps_rt[mt % 2][:],
                            w2e[:, ep * 4 * D + kt * D + mt * P:
                                ep * 4 * D + kt * D + (mt + 1) * P],
                            hbuf[:, kt * 512:(kt + 1) * 512],
                            start=(kt == 0), stop=(kt == 3 and e != 8))
                    if e == 8:
                        _li = t.matmul(ps_rt[mt % 2][:],
                                       rb2x[:, mt * P:(mt + 1) * P],
                                       cft[:, tb * 512:(tb + 1) * 512],
                                       start=False, stop=True)
                    S.inc(_li, "sp")
    assert S.c["sp"] == SP_E_END, (S.c["sp"], SP_E_END)

    @block.scalar
    def _(s):
        for ei, e in enumerate(e_order):
            for tb in range(nTB):
                pm = prev_mm2_last(ei, tb)
                for mt1 in range(4):
                    s.wait_ge(S.h["sp"], SP_MM1[(ei, tb, mt1)])
                    if pm is not None and mt1 == 0:
                        s.wait_ge(S.h["sp"], pm)
                    _li = s.activation(out=hbuf[:, mt1 * 512:(mt1 + 1) * 512],
                                       in_=ps_h[mt1 % 2][:], func=AF.Silu,
                                       bias=b1v[:, e * 4 + mt1:e * 4 + mt1 + 1])
                    S.inc(_li, "sa")
    assert S.c["sa"] == SA_E_END, (S.c["sa"], SA_E_END)

    @block.vector
    def _(v):
        for ei, e in enumerate(e_order):
            for tb in range(nTB):
                if e != 8:
                    v.wait_ge(S.h["sp"], SP_CFB[(ei, tb)])
                    v.wait_ge(S.h["sa"], SA_M1E[(ei, tb, 3)])
                    v.tensor_copy(cfbb[:], ps_cb[:])
                    for mt1 in range(4):
                        _li = v.tensor_tensor(
                            hbuf[:, mt1 * 512:(mt1 + 1) * 512],
                            hbuf[:, mt1 * 512:(mt1 + 1) * 512],
                            cfbb[:], ALU.mult)
                    S.inc(_li, "sv")
                for mt in range(8):
                    v.wait_ge(S.h["sp"], SP_MM2[(ei, tb, mt)])
                    dst = racc[:, mt * TH + tb * 512:mt * TH + (tb + 1) * 512]
                    if e == 8:
                        _li = v.tensor_copy(dst, ps_rt[mt % 2][:])
                    else:
                        _li = v.tensor_tensor(dst, dst, ps_rt[mt % 2][:],
                                              ALU.add)
                    S.inc(_li, "sv")
    assert S.c["sv"] == SV_E_END, (S.c["sv"], SV_E_END)

    T_EXP_DVE = SV_E_END

    es_c2.close()

    es_c3 = ExitStack()
    ps_ot = [es_c3.enter_context(nc.psum_tensor(f"ps_ot{i}", [P, D], F32))
             for i in range(2)]

    t_po_qs = [0] * nTB
    t_po_qg = [0] * nTB
    t_xr_qs = [0] * nTB
    t_xr_qo = [0] * nTB
    t_accs = [0] * nTB
    t_fin, t_finev, t_oadd, t_odma = [], [], [], []

    def _po_dma(eng, qn, w, tb, h):
        for hh in range(2):
            ctx = eng.If_eq(h, 0) if hh == 0 else eng.Else()
            with ctx:
                eng.dma_start(
                    out=pobuf[:, w * 8 * 512:(w + 1) * 8 * 512].rearrange(
                        "p (d t) -> p d t", d=8),
                    in_=shpo.rearrange("w d p l -> p w d l")[
                        :, w, :, hh * TH + tb * 512:hh * TH + (tb + 1) * 512]
                ).then_inc(S.h[qn], 16)
        S.c[qn] += 16

    def _xr_dma(eng, qn, i, tb, h):
        for hh in range(2):
            ctx = eng.If_eq(h, 0) if hh == 0 else eng.Else()
            with ctx:
                eng.dma_start(out=xrl[:, i * D:(i + 1) * D],
                              in_=dxin[hh * TH + tb * 512 + i * 128:
                                       hh * TH + tb * 512 + (i + 1) * 128, :]
                              ).then_inc(S.h[qn], 16)
        S.c[qn] += 16

    def emit_loads(tb):
        if tb == 0:
            @block.sync
            def _(sy):
                h = get_h(sy, "sync")
                for w in range(2):
                    _po_dma(sy, "qs", w, 0, h)
                for i in range(4):
                    _xr_dma(sy, "qs", i, 0, h)
            t_po_qg[0] = S.c["qg"]
            t_xr_qo[0] = S.c["qo"]
        else:
            @block.sync
            def _(sy, tb=tb):
                sy.wait_ge(S.h["sv"], t_accs[tb - 1])
                sy.wait_ge(S.h["sg"], t_oadd[(tb - 1) * 4 + 3])
                h = get_h(sy, "sync")
                _po_dma(sy, "qs", 0, tb, h)
                for i in range(2):
                    _xr_dma(sy, "qs", i, tb, h)

            @block.gpsimd
            def _(g, tb=tb):
                g.wait_ge(S.h["sv"], t_accs[tb - 1])
                h = get_h(g, "gpsimd")
                _po_dma(g, "qg", 1, tb, h)
            t_po_qg[tb] = S.c["qg"]

            @block.scalar
            def _(s, tb=tb):
                s.wait_ge(S.h["sg"], t_oadd[(tb - 1) * 4 + 3])
                h = get_h(s, "scalar")
                for i in range(2, 4):
                    _xr_dma(s, "qo", i, tb, h)
            t_xr_qo[tb] = S.c["qo"]
        t_po_qs[tb] = S.c["qs"]
        t_xr_qs[tb] = S.c["qs"]

    emit_loads(0)
    for tb in range(nTB):
        @block.vector
        def _(v, tb=tb):
            v.wait_ge(S.h["qs"], t_po_qs[tb])
            if tb >= 1:
                v.wait_ge(S.h["qg"], t_po_qg[tb])
            for mt in range(8):
                dst = racc[:, mt * TH + tb * 512:mt * TH + (tb + 1) * 512]
                for w in range(2):
                    _li = v.tensor_tensor(
                        dst, dst,
                        pobuf[:, (w * 8 + mt) * 512:(w * 8 + mt + 1) * 512],
                        ALU.add)
            S.inc(_li, "sv")
        t_accs[tb] = S.c["sv"]

        for tt in range(4):
            fi = len(t_fin)

            @block.tensor
            def _(t, tb=tb, tt=tt, fi=fi):
                t.wait_ge(S.h["sv"], t_accs[tb])
                if fi >= 2:
                    t.wait_ge(S.h["sa"], t_finev[fi - 2])
                for mt in range(8):
                    _li = t.transpose(
                        ps_ot[fi % 2][:, mt * P:(mt + 1) * P],
                        racc[:, mt * TH + tb * 512 + tt * P:
                             mt * TH + tb * 512 + (tt + 1) * P], idf[:])
                S.inc(_li, "sp")
            t_fin.append(S.c["sp"])

            @block.scalar
            def _(s, fi=fi):
                s.wait_ge(S.h["sp"], t_fin[fi])
                if fi >= 2:
                    s.wait_ge(S.h["sg"], t_oadd[fi - 2])
                    s.wait_ge(S.h["qo"], t_odma[fi - 2])
                _li = s.activation(out=otok[:, (fi % 2) * D:((fi % 2) + 1) * D],
                             in_=ps_ot[fi % 2][:], func=AF.Copy)
                S.inc(_li, "sa")
            t_finev.append(S.c["sa"])

            @block.gpsimd
            def _(g, tb=tb, tt=tt, fi=fi):
                g.wait_ge(S.h["sa"], t_finev[fi])
                g.wait_ge(S.h["qs"], t_xr_qs[tb])
                if tt >= 2:
                    g.wait_ge(S.h["qo"], t_xr_qo[tb])
                _li = g.tensor_tensor(otok[:, (fi % 2) * D:((fi % 2) + 1) * D],
                                otok[:, (fi % 2) * D:((fi % 2) + 1) * D],
                                xrl[:, tt * D:(tt + 1) * D], ALU.add)
                S.inc(_li, "sg")
            t_oadd.append(S.c["sg"])

            @block.scalar
            def _(s, tb=tb, tt=tt, fi=fi):
                s.wait_ge(S.h["sg"], t_oadd[fi])
                S.inc(s.dma_start(
                    out=dout[tb * 512 + tt * P:tb * 512 + (tt + 1) * P, :],
                    in_=otok[:, (fi % 2) * D:((fi % 2) + 1) * D]), "qo", 16)
            t_odma.append(S.c["qo"])
        if tb + 1 < nTB:
            emit_loads(tb + 1)

    QO_END = S.c["qo"]
    QS_END = S.c["qs"]
    QG_END = S.c["qg"]

    @block.gpsimd
    def _(g):
        g.wait_ge(S.h["qo"], QO_END)
        g.wait_ge(S.h["qs"], QS_END)
        g.wait_ge(S.h["qg"], QG_END)

    es_c3.close()
    es_c.close()
    es.close()
    return nc


_NC_CACHE = {}


def _get_nc(L):
    if L not in _NC_CACHE:
        _NC_CACHE[L] = build_program(L)
    return _NC_CACHE[L]


def make_in_maps(inputs, L):
    import ml_dtypes
    x = np.asarray(inputs["L0_attn_x"], np.float32)
    w_in = np.asarray(inputs["L0_attn_W_in"], np.float32)
    conv_w = np.asarray(inputs["L0_attn_conv_w"], np.float32)
    conv_b = np.asarray(inputs["L0_attn_conv_b"], np.float32)
    w_xp = np.asarray(inputs["L0_attn_W_xproj"], np.float32)
    w_dt = np.asarray(inputs["L0_attn_W_dt"], np.float32)
    b_dt = np.asarray(inputs["L0_attn_b_dt"], np.float32)
    a_log = np.asarray(inputs["L0_attn_A_log"], np.float32)
    d_p = np.asarray(inputs["L0_attn_Dp"], np.float32)
    w_out = np.asarray(inputs["L0_attn_W_out"], np.float32)
    r_w = np.asarray(inputs["L0_ffn_routing_W"], np.float32)
    e_b = np.asarray(inputs["L0_ffn_expert_bias"], np.float32)
    sW1 = np.asarray(inputs["L0_ffn_sW1"], np.float32)
    sb1 = np.asarray(inputs["L0_ffn_sb1"], np.float32)
    sW2 = np.asarray(inputs["L0_ffn_sW2"], np.float32)
    sb2 = np.asarray(inputs["L0_ffn_sb2"], np.float32)
    rW1 = np.asarray(inputs["L0_ffn_rW1"], np.float32)
    rb1 = np.asarray(inputs["L0_ffn_rb1"], np.float32)
    rW2 = np.asarray(inputs["L0_ffn_rW2"], np.float32)
    rb2 = np.asarray(inputs["L0_ffn_rb2"], np.float32)

    w1 = np.ascontiguousarray(np.concatenate([rW1, sW1[None]], 0))
    b1 = np.ascontiguousarray(np.concatenate([rb1, sb1[None]], 0))
    w2 = np.ascontiguousarray(np.concatenate([rW2, sW2[None]], 0))
    b2 = np.ascontiguousarray(np.concatenate([rb2, sb2[None]], 0))
    identf = np.eye(P, dtype=np.float32)
    identb = np.eye(P, dtype=np.float32).astype(ml_dtypes.bfloat16)
    e16m = np.zeros((NST, NST * P), np.float32)
    for n in range(NST):
        e16m[n, n * P:(n + 1) * P] = 1.0
    e16m = e16m.astype(ml_dtypes.bfloat16)
    e9m = np.zeros((NE + 1, (NE + 1) * P), np.float32)
    for n in range(NE + 1):
        e9m[n, n * P:(n + 1) * P] = 1.0
    e9m = e9m.astype(ml_dtypes.bfloat16)

    c = np.ascontiguousarray
    in_maps = []
    for core in range(8):
        b, h = core // 2, core % 2
        own = slice(h * HD, (h + 1) * HD)
        pair = slice((1 - h) * HD, (2 - h) * HD)
        in_maps.append({
            "x": c(x[b, :L]),
            "w_in": c(np.concatenate(
                [w_in[:, own], w_in[:, DI + h * HD:DI + (h + 1) * HD]], 1)),
            "conv_w": c(conv_w[own, 0, :]),
            "conv_b": c(conv_b[own].reshape(HD, 1)),
            "w_x2": c(np.concatenate(
                [w_xp[own, NST:2 * NST], w_xp[pair, NST:2 * NST]], 0)),
            "w_dt": c(np.concatenate([w_dt[own, own], w_dt[pair, own]], 0)),
            "b_dt": c(b_dt[own].reshape(HD, 1)),
            "a_log": c(a_log[own]),
            "d_p": c(d_p[own].reshape(HD, 1)),
            "w_out": c(w_out[own, :]),
            "r_w": c(r_w.T),
            "e_b": c(e_b.reshape(NE, 1)),
            "w1": w1, "b1": b1, "w2": w2, "b2": b2,
            "identf": identf, "identb": identb, "e16": e16m, "e9": e9m,
        })
    return in_maps


def kernel(**inputs):
    L = 4096
    in_maps = make_in_maps(inputs, L)
    nc = _get_nc(L)
    res = run_bass_kernel_spmd(nc, in_maps, core_ids=list(range(8)))
    out = np.empty((4, L, D), np.float32)
    for core in range(8):
        b, h = core // 2, core % 2
        out[b, h * (L // 2):(h + 1) * (L // 2), :] = res.results[core]["out"]
    return out


# version 10
# speedup vs baseline: 1.1525x; 1.1525x over previous
import numpy as np
from contextlib import ExitStack

import concourse.bass as bass
import concourse.mybir as mybir
from concourse.bass_utils import run_bass_kernel_spmd

F32 = mybir.dt.float32
BF16 = mybir.dt.bfloat16
AF = mybir.ActivationFunctionType
ALU = mybir.AluOpType

P = 128
D = 1024
DI = 2048
HD = 1024
NST = 16
NE = 8
HF = 512
TS = 512
PAIRS = [[0, 1], [2, 3], [4, 5], [6, 7]]


class Sems:
    def __init__(self, es, nc, names):
        self.h = {n: es.enter_context(nc.semaphore(n)) for n in names}
        self.c = {n: 0 for n in names}

    def inc(self, inst, name, k=1):
        inst.then_inc(self.h[name], k)
        self.c[name] += k
        return self.c[name]


def build_program(L, stop_after=None):
    nCH = L // 512
    nSU = L // TS
    TH = L // 2
    nTB = TH // 512

    nc = bass.Bass("TRN2", target_bir_lowering=False, num_devices=8,
                   detect_race_conditions=False)

    dxin = nc.dram_tensor("x", [L, D], F32, kind="ExternalInput").ap()
    dwin = nc.dram_tensor("w_in", [D, DI], F32, kind="ExternalInput").ap()
    dcw = nc.dram_tensor("conv_w", [HD, 4], F32, kind="ExternalInput").ap()
    dcb = nc.dram_tensor("conv_b", [HD, 1], F32, kind="ExternalInput").ap()
    dwx2 = nc.dram_tensor("w_x2", [DI, NST], F32, kind="ExternalInput").ap()
    dwdt = nc.dram_tensor("w_dt", [DI, HD], F32, kind="ExternalInput").ap()
    dbdt = nc.dram_tensor("b_dt", [HD, 1], F32, kind="ExternalInput").ap()
    dalog = nc.dram_tensor("a_log", [HD, NST], F32, kind="ExternalInput").ap()
    ddp = nc.dram_tensor("d_p", [HD, 1], F32, kind="ExternalInput").ap()
    dwout = nc.dram_tensor("w_out", [HD, D], F32, kind="ExternalInput").ap()
    drw = nc.dram_tensor("r_w", [D, NE], F32, kind="ExternalInput").ap()
    deb = nc.dram_tensor("e_b", [NE, 1], F32, kind="ExternalInput").ap()
    dw1 = nc.dram_tensor("w1", [NE + 1, D, HF], F32, kind="ExternalInput").ap()
    db1 = nc.dram_tensor("b1", [NE + 1, HF], F32, kind="ExternalInput").ap()
    dw2 = nc.dram_tensor("w2", [NE + 1, HF, D], F32, kind="ExternalInput").ap()
    db2 = nc.dram_tensor("b2", [NE + 1, D], F32, kind="ExternalInput").ap()
    didf = nc.dram_tensor("identf", [P, P], F32, kind="ExternalInput").ap()
    didb = nc.dram_tensor("identb", [P, P], BF16, kind="ExternalInput").ap()
    de16 = nc.dram_tensor("e16", [NST, NST * P], BF16, kind="ExternalInput").ap()
    de9 = nc.dram_tensor("e9", [NE + 1, (NE + 1) * P], BF16,
                         kind="ExternalInput").ap()
    dout = nc.dram_tensor("out", [TH, D], F32, kind="ExternalOutput").ap()

    shxc = nc.dram_tensor("sh_xc", [16, P, L], BF16, kind="Internal",
                          addr_space="Shared").ap()
    shpo = nc.dram_tensor("sh_po", [2, 8, P, L], BF16, kind="Internal",
                          addr_space="Shared").ap()
    pz = nc.dram_tensor("p_z", [8, P, L], BF16, kind="Internal").ap()
    pxnt = nc.dram_tensor("p_xnt", [8, P, L], BF16, kind="Internal").ap()
    bi1 = nc.dram_tensor("bar_i1", [1, 16], F32, kind="Internal").ap()
    bo1 = nc.dram_tensor("bar_o1", [1, 16], F32, kind="Internal").ap()
    bi2 = nc.dram_tensor("bar_i2", [1, 16], F32, kind="Internal").ap()
    bo2 = nc.dram_tensor("bar_o2", [1, 16], F32, kind="Internal").ap()

    es = ExitStack()

    def sb(name, shape, dtype):
        return es.enter_context(nc.sbuf_tensor(name, shape, dtype))

    S = Sems(es, nc, ["qs", "qg", "qo", "sa", "sv", "sp", "sg", "cc"])
    block = es.enter_context(nc.Block())
    hreg = {}

    def get_h(eng, ename):
        if ename not in hreg:
            hreg[ename] = eng.to_reg(eng.partition_id() % 2)
        return hreg[ename]

    wx2 = sb("wx2_s", [P, 16 * NST], BF16)
    cwv = sb("cwv", [P, 8 * 4], F32)
    cbv = sb("cbv", [P, 8], F32)
    bdtv = sb("bdtv", [P, 8], F32)
    dpv = sb("dpv", [P, 8], F32)
    alog = sb("alog", [P, 8 * NST], F32)
    aneg = sb("aneg", [P, 8 * NST], F32)
    idf = sb("idf", [P, P], F32)
    idb = sb("idb", [P, P], BF16)
    e16 = sb("e16s", [NST, NST * P], BF16)
    e9 = sb("e9s", [NE + 1, (NE + 1) * P], BF16)
    cst = sb("cst", [P, 8], F32)
    zb = sb("zb", [1, 16], F32)
    C_EPS, C_MH, C_ONE, C_MONE, C_HALF, C_TINY = range(6)
    carry = sb("carry", [P, NST * 8], F32)
    ebv = sb("ebv", [NE, 1], F32)
    b1v = sb("b1v", [P, (NE + 1) * 4], F32)

    def cc_(i):
        return cst[:, i:i + 1]

    es_mw = ExitStack()
    es_wi = ExitStack()
    wdt = es_mw.enter_context(nc.sbuf_tensor("wdt_s", [P, 16 * HD], BF16))
    wout = es_mw.enter_context(nc.sbuf_tensor("wout_s", [P, 8 * D], BF16))
    win = es_wi.enter_context(nc.sbuf_tensor("win_s", [P, 8 * DI], BF16))

    def win_t(kt, mt):
        return win[:, kt * DI + mt * P:kt * DI + (mt + 1) * P]

    def wdt_t(kt, mt):
        return wdt[:, kt * HD + mt * P:kt * HD + (mt + 1) * P]

    def wout_t(kt, mt):
        return wout[:, kt * D + mt * P:kt * D + (mt + 1) * P]

    @block.gpsimd
    def _(g):
        S.inc(g.dma_start(out=win[:].rearrange("p (k c) -> p k c", k=8),
                          in_=dwin.rearrange("(k p) c -> p k c", p=P)), "qg", 16)
        S.inc(g.dma_start(out=wdt[:].rearrange("p (k c) -> p k c", k=16),
                          in_=dwdt.rearrange("(k p) c -> p k c", p=P)), "qg", 16)
        S.inc(g.dma_start(out=wout[:].rearrange("p (k c) -> p k c", k=8),
                          in_=dwout.rearrange("(k p) c -> p k c", p=P)), "qg", 16)
        S.inc(g.dma_start(out=wx2[:].rearrange("p (k n) -> p k n", k=16),
                          in_=dwx2.rearrange("(k p) n -> p k n", p=P)), "qg", 16)
        S.inc(g.dma_start(out=idb[:], in_=didb), "qg", 16)
        S.inc(g.dma_start(out=e16[:], in_=de16), "qg", 16)
        S.inc(g.dma_start(out=e9[:], in_=de9), "qg", 16)
        g.memset(zb[:], 0.0)
        g.memset(cst[:, C_EPS:C_EPS + 1], 1e-5)
        g.memset(cst[:, C_MH:C_MH + 1], -0.5)
        g.memset(cst[:, C_ONE:C_ONE + 1], 1.0)
        g.memset(cst[:, C_MONE:C_MONE + 1], -1.0)
        g.memset(cst[:, C_HALF:C_HALF + 1], 0.5)
        S.inc(g.memset(cst[:, C_TINY:C_TINY + 1], 1e-30), "sg", 1)
        g.wait_ge(S.h["sg"], 1)
        S.inc(g.dma_start(out=bi1, in_=zb[0:1, :]), "qg", 16)
        S.inc(g.dma_start(out=bi2, in_=zb[0:1, :]), "qg", 16)
    QG_W = S.c["qg"]

    @block.sync
    def _(sy):
        S.inc(sy.dma_start(out=cwv[:].rearrange("p (d f) -> p d f", d=8),
                           in_=dcw.rearrange("(d p) f -> p d f", p=P)), "qs", 16)
        S.inc(sy.dma_start(out=cbv[:],
                           in_=dcb.rearrange("(d p) o -> p (d o)", p=P), allow_slow_non_contiguous=True), "qs", 16)
        S.inc(sy.dma_start(out=bdtv[:],
                           in_=dbdt.rearrange("(d p) o -> p (d o)", p=P), allow_slow_non_contiguous=True), "qs", 16)
        S.inc(sy.dma_start(out=dpv[:],
                           in_=ddp.rearrange("(d p) o -> p (d o)", p=P), allow_slow_non_contiguous=True), "qs", 16)
        S.inc(sy.dma_start(out=alog[:].rearrange("p (d n) -> p d n", d=8),
                           in_=dalog.rearrange("(d p) n -> p d n", p=P)), "qs", 16)
        S.inc(sy.dma_start(out=idf[:], in_=didf), "qs", 16)
        S.inc(sy.dma_start(out=ebv[:], in_=deb), "qs", 16)
        S.inc(sy.dma_start(out=b1v[:].rearrange("p (e m) -> p e m", e=NE + 1),
                           in_=db1.rearrange("e (m p) -> p e m", p=P), allow_slow_non_contiguous=True), "qs", 16)
    QS_SETUP = S.c["qs"]

    @block.scalar
    def _(s):
        s.wait_ge(S.h["qs"], QS_SETUP)
        s.wait_ge(S.h["sg"], 1)
        s.activation(out=cst[:, 6:7], in_=cst[:, C_ONE:C_ONE + 1], func=AF.Ln)
        s.activation(out=cst[:, 6:7], in_=cst[:, C_ONE:C_ONE + 1], func=AF.Silu)
        S.inc(s.activation(out=aneg[:], in_=alog[:], func=AF.Exp), "sa")
    t_aexp = S.c["sa"]

    @block.vector
    def _(v):
        v.wait_ge(S.h["sa"], t_aexp)
        S.inc(v.tensor_scalar_mul(aneg[:], aneg[:], -1.0), "sv")
    T_ANEG = S.c["sv"]

    if stop_after == "S":
        qs_f, qg_f = S.c["qs"], S.c["qg"]

        @block.gpsimd
        def _(g):
            g.wait_ge(S.h["qs"], qs_f)
            g.wait_ge(S.h["qg"], qg_f)
            g.wait_ge(S.h["sv"], T_ANEG)
        es_mw.close()
        es.close()
        return nc

    es_a = ExitStack()

    def sba(name, shape, dtype):
        return es_a.enter_context(nc.sbuf_tensor(name, shape, dtype))

    xa = sba("xa", [P, 2 * 4 * D], BF16)
    stats = sba("stats", [P, 48], F32)
    mvb = sba("mvb", [P, 8], F32)
    istd = sba("istd", [P, 2 * 4], F32)
    negm = sba("negm", [P, 2 * 4], F32)
    xn = sba("xn", [P, 2 * 4 * D], BF16)
    xnt = sba("xnt", [P, 2 * 8 * 512], BF16)
    xcp = sba("xcp", [P, 2 * 8 * 515], BF16)
    cvt = sba("cvt", [P, 8 * 512], BF16)
    cvtm = sba("cvtm", [P, 512], BF16)
    xco = sba("xco", [P, 2 * 8 * 512], BF16)
    zso = sba("zso", [P, 2 * 8 * 512], BF16)
    ps_tr = [es_a.enter_context(nc.psum_tensor(f"ps_tr{i}", [P, 512], BF16))
             for i in range(4)]
    ps_xz = [es_a.enter_context(nc.psum_tensor(f"ps_xz{i}", [P, 512], F32))
             for i in range(4)]

    t_ln = [0] * nCH
    t_tr = [0] * nCH
    t_mm = [0] * nCH
    t_cv = [0] * nCH
    t_sil = [0] * nCH
    t_silq = [0] * nCH
    t_trg, t_trgev, t_mmg = [], [], []
    ec_tok = {}


    def a_views(k):
        pb = k % 2
        return dict(
            pb=pb,
            xa_v=xa[:, pb * 4 * D:(pb + 1) * 4 * D],
            xn_v=xn[:, pb * 4 * D:(pb + 1) * 4 * D],
            xnt_v=xnt[:, pb * 8 * 512:(pb + 1) * 8 * 512],
            xcp_v=xcp[:, pb * 8 * 515:(pb + 1) * 8 * 515],
            xcp_p=xcp[:, (1 - pb) * 8 * 515:(2 - pb) * 8 * 515],
            cvt_v=cvt[:, 0:8 * 512],
            xco_v=xco[:, pb * 8 * 512:(pb + 1) * 8 * 512],
            zso_v=zso[:, pb * 8 * 512:(pb + 1) * 8 * 512],
            istd_v=istd[:, pb * 4:(pb + 1) * 4],
            negm_v=negm[:, pb * 4:(pb + 1) * 4])

    def emit_load(k):
        vw = a_views(k)

        @block.gpsimd
        def _(g, k=k, xa_v=vw["xa_v"]):
            if k >= 2:
                g.wait_ge(S.h["sv"], t_ln[k - 2])
            for i in range(4):
                t0 = k * 512 + i * 128
                S.inc(g.dma_start(out=xa_v[:, i * D:(i + 1) * D],
                                  in_=dxin[t0:t0 + 128, :]), "qg", 16)
        return S.c["qg"]

    def emit_stats(k, t_ldx):
        vw = a_views(k)

        @block.vector
        def _(v, t_ldx=t_ldx, xa_v=vw["xa_v"], pb=vw["pb"]):
            v.wait_ge(S.h["qg"], t_ldx)
            for i in range(4):
                v.bn_stats(stats[:, i * 12:i * 12 + 6],
                           xa_v[:, i * D:i * D + 512])
                _li = v.bn_stats(stats[:, i * 12 + 6:i * 12 + 12],
                                 xa_v[:, i * D + 512:(i + 1) * D])
            S.inc(_li, "sv")
            v.wait_ge(S.h["sv"], S.c["sv"])
            for i in range(4):
                _li = v.bn_aggr(mvb[:, 2 * i:2 * i + 2],
                                stats[:, i * 12:(i + 1) * 12])
            S.inc(_li, "sv")
            v.wait_ge(S.h["sv"], S.c["sv"])
            for i in range(4):
                v.tensor_copy(istd[:, pb * 4 + i:pb * 4 + i + 1],
                              mvb[:, 2 * i + 1:2 * i + 2])
                _li = v.tensor_copy(negm[:, pb * 4 + i:pb * 4 + i + 1],
                                    mvb[:, 2 * i:2 * i + 1])
            S.inc(_li, "sv")
        return S.c["sv"]

    def emit_istd(k, t_bn):
        vw = a_views(k)

        @block.scalar
        def _(s, t_bn=t_bn, istd_v=vw["istd_v"]):
            s.wait_ge(S.h["sv"], t_bn)
            for i in range(4):
                s.activation(out=istd_v[:, i:i + 1], in_=istd_v[:, i:i + 1],
                             func=AF.Ln, bias=cc_(C_EPS))
                _li = s.activation(out=istd_v[:, i:i + 1],
                                   in_=istd_v[:, i:i + 1],
                                   func=AF.Exp, scale=cc_(C_MH))
            S.inc(_li, "sa")
        return S.c["sa"]

    def emit_norm(k, t_istd):
        vw = a_views(k)

        @block.vector
        def _(v, k=k, t_istd=t_istd, xa_v=vw["xa_v"], xn_v=vw["xn_v"],
              istd_v=vw["istd_v"], negm_v=vw["negm_v"]):
            v.wait_ge(S.h["sa"], t_istd)
            if k >= 2:
                v.wait_ge(S.h["sp"], t_tr[k - 2])
            for i in range(4):
                _li = v.tensor_scalar(out=negm_v[:, i:i + 1],
                                      in0=negm_v[:, i:i + 1],
                                      scalar1=istd_v[:, i:i + 1],
                                      scalar2=-1.0,
                                      op0=ALU.mult, op1=ALU.mult)
            S.inc(_li, "sv")
            v.wait_ge(S.h["sv"], S.c["sv"])
            for i in range(4):
                _li = v.tensor_scalar(out=xn_v[:, i * D:(i + 1) * D],
                                in0=xa_v[:, i * D:(i + 1) * D],
                                scalar1=negm_v[:, i:i + 1],
                                scalar2=istd_v[:, i:i + 1],
                                op0=ALU.add, op1=ALU.mult)
            S.inc(_li, "sv")
        t_ln[k] = S.c["sv"]

    def emit_pe(k):
        vw = a_views(k)
        ec_base = S.c["sv"] + (3 if k < nCH - 1 else 0)
        sa_off = 1 if k < nCH - 1 else 0
        tok = dict(trev=[ec_base + 1 + g for g in range(4)],
                   xzev=[ec_base + 5 + g for g in range(4)],
                   cv=ec_base + 9,
                   zsil=[S.c["sa"] + sa_off + 1 + g for g in range(4)],
                   silcv=S.c["sa"] + sa_off + 5)
        ec_tok[k] = tok

        for g_ in range(4):
            @block.tensor
            def _(t, k=k, g_=g_, xn_v=vw["xn_v"]):
                if g_ == 0:
                    t.wait_ge(S.h["sv"], t_ln[k])
                    if k == 0:
                        t.wait_ge(S.h["qg"], QG_W)
                if g_ >= 2:
                    t.wait_ge(S.h["sv"], tok["trev"][g_ - 2])
                elif k >= 1:
                    t.wait_ge(S.h["sv"], ec_tok[k - 1]["trev"][g_ + 2])
                for j in range(2):
                    dt = g_ * 2 + j
                    pt = ps_tr[(g_ % 2) * 2 + j]
                    for i in range(4):
                        _li = t.transpose(pt[:, i * P:(i + 1) * P],
                                    xn_v[:, i * D + dt * P:i * D + (dt + 1) * P],
                                    idb[:])
                S.inc(_li, "sp")
            t_trg.append(S.c["sp"])
        t_tr[k] = S.c["sp"]

        for g_ in range(8):
            @block.tensor
            def _(t, k=k, g_=g_, xnt_v=vw["xnt_v"]):
                if g_ == 0:
                    t.wait_ge(S.h["sv"], tok["trev"][3])
                    if k == 0:
                        t.wait_ge(S.h["sv"], T_ANEG)
                    if k >= 2:
                        t.wait_ge(S.h["sa"], t_sil[k - 2])
                    if k >= 1:
                        t.wait_ge(S.h["sv"], ec_tok[k - 1]["cv"])
                if g_ in (2, 3, 4, 5):
                    t.wait_ge(S.h["sv"], tok["xzev"][g_ - 2])
                elif g_ in (6, 7):
                    t.wait_ge(S.h["sa"], tok["zsil"][g_ - 6])
                elif k >= 1:
                    t.wait_ge(S.h["sa"], ec_tok[k - 1]["zsil"][g_ + 2])
                for j in range(2):
                    mt = g_ * 2 + j
                    pj = ps_xz[(g_ % 2) * 2 + j]
                    for kt in range(8):
                        _li = t.matmul(pj[:], win_t(kt, mt),
                                 xnt_v[:, kt * 512:(kt + 1) * 512],
                                 start=(kt == 0), stop=(kt == 7))
                S.inc(_li, "sp")
            t_mmg.append(S.c["sp"])
        t_mm[k] = S.c["sp"]

    def emit_ec(k):
        vw = a_views(k)
        for g_ in range(4):
            @block.vector
            def _(v, k=k, g_=g_, xnt_v=vw["xnt_v"]):
                v.wait_ge(S.h["sp"], t_trg[k * 4 + g_])
                if k >= 2 and g_ == 0:
                    v.wait_ge(S.h["qo"], t_silq[k - 2])
                    v.wait_ge(S.h["sp"], t_mm[k - 2])
                for j in range(2):
                    dt = g_ * 2 + j
                    _li = v.tensor_copy(xnt_v[:, dt * 512:(dt + 1) * 512],
                                        ps_tr[(g_ % 2) * 2 + j][:])
                S.inc(_li, "sv")
            t_trgev.append(S.c["sv"])
        for g_ in range(4):
            @block.vector
            def _(v, k=k, g_=g_, xcp_v=vw["xcp_v"]):
                v.wait_ge(S.h["sp"], t_mmg[k * 8 + g_])
                for j in range(2):
                    mt = g_ * 2 + j
                    pj = ps_xz[(g_ % 2) * 2 + j]
                    _li = v.tensor_copy(
                        xcp_v[:, mt * 515 + 3:mt * 515 + 515], pj[:])
                S.inc(_li, "sv")

        @block.vector
        def _(v, k=k, xcp_v=vw["xcp_v"], xcp_p=vw["xcp_p"],
              cvt_v=vw["cvt_v"]):
            v.wait_ge(S.h["sa"], ec_tok[k]["zsil"][3])
            if k >= 1:
                v.wait_ge(S.h["sa"], t_sil[k - 1])
            for dt in range(8):
                if k == 0:
                    v.memset(xcp_v[:, dt * 515:dt * 515 + 3], 0.0)
                else:
                    v.tensor_copy(xcp_v[:, dt * 515:dt * 515 + 3],
                                  xcp_p[:, dt * 515 + 512:dt * 515 + 515])
                cv = cvt_v[:, dt * 512:(dt + 1) * 512]
                v.tensor_scalar_mul(cv, xcp_v[:, dt * 515:dt * 515 + 512],
                                    cwv[:, dt * 4:dt * 4 + 1])
                for j in range(1, 4):
                    v.tensor_scalar_mul(cvtm[:],
                                        xcp_v[:, dt * 515 + j:dt * 515 + j + 512],
                                        cwv[:, dt * 4 + j:dt * 4 + j + 1])
                    _li = v.tensor_tensor(cv, cv, cvtm[:], ALU.add)
            S.inc(_li, "sv")
        t_cv[k] = S.c["sv"]
        assert t_cv[k] == ec_tok[k]["cv"], (k, t_cv[k], ec_tok[k]["cv"])
        assert t_trgev[k * 4:k * 4 + 4] == ec_tok[k]["trev"], k

    def emit_sil(k):
        vw = a_views(k)

        @block.scalar
        def _(s, k=k, xnt_v=vw["xnt_v"]):
            s.wait_ge(S.h["sv"], t_trgev[k * 4 + 3])
            S.inc(s.dma_start(
                out=pxnt.rearrange("d p l -> p d l")[:, :, k * 512:(k + 1) * 512],
                in_=xnt_v[:].rearrange("p (d t) -> p d t", d=8)), "qo", 16)

        for g_ in range(4, 8):
            @block.scalar
            def _(s, k=k, g_=g_, zso_v=vw["zso_v"]):
                s.wait_ge(S.h["sp"], t_mmg[k * 8 + g_])
                for j in range(2):
                    z = g_ * 2 + j - 8
                    pj = ps_xz[(g_ % 2) * 2 + j]
                    _li = s.activation(out=zso_v[:, z * 512:(z + 1) * 512],
                                 in_=pj[:], func=AF.Silu)
                S.inc(_li, "sa")
        assert S.c["sa"] == ec_tok[k]["zsil"][3], k

        @block.scalar
        def _(s, k=k, zso_v=vw["zso_v"]):
            s.wait_ge(S.h["sa"], S.c["sa"])
            S.inc(s.dma_start(
                out=pz.rearrange("d p l -> p d l")[:, :, k * 512:(k + 1) * 512],
                in_=zso_v[:].rearrange("p (d t) -> p d t", d=8)), "qo", 16)

        @block.scalar
        def _(s, k=k, cvt_v=vw["cvt_v"], xco_v=vw["xco_v"]):
            s.wait_ge(S.h["sv"], t_cv[k])
            if k >= 2:
                s.wait_ge(S.h["qo"], t_silq[k - 2])
            for dt in range(8):
                _li = s.activation(out=xco_v[:, dt * 512:(dt + 1) * 512],
                                   in_=cvt_v[:, dt * 512:(dt + 1) * 512],
                                   func=AF.Silu, bias=cbv[:, dt:dt + 1])
            S.inc(_li, "sa")
            s.wait_ge(S.h["sa"], S.c["sa"])
            h = get_h(s, "scalar")
            src = xco_v[:].rearrange("p (d t) -> p d t", d=8)
            with s.If_eq(h, 0):
                s.dma_start(out=shxc.rearrange("d p l -> p d l")[
                    :, 0:8, k * 512:(k + 1) * 512],
                    in_=src).then_inc(S.h["qo"], 16)
            with s.Else():
                s.dma_start(out=shxc.rearrange("d p l -> p d l")[
                    :, 8:16, k * 512:(k + 1) * 512],
                    in_=src).then_inc(S.h["qo"], 16)
            S.c["qo"] += 16
        t_sil[k] = S.c["sa"]
        t_silq[k] = S.c["qo"]
        assert t_sil[k] == ec_tok[k]["silcv"], k

    for k in range(nCH):
        t_ldx = emit_load(k)
        t_bn = emit_stats(k, t_ldx)
        if k >= 1:
            emit_ec(k - 1)
        t_istd = emit_istd(k, t_bn)
        if k >= 1:
            emit_sil(k - 1)
        emit_norm(k, t_istd)
        emit_pe(k)
    emit_ec(nCH - 1)
    emit_sil(nCH - 1)

    QO_A = S.c["qo"]
    QS_A = S.c["qs"]

    @block.gpsimd
    def _(g):
        g.wait_ge(S.h["qo"], QO_A)
        g.wait_ge(S.h["qs"], QS_A)
        g.wait_ge(S.h["sa"], t_sil[nCH - 1])
        g.wait_ge(S.h["qg"], QG_W)
        S.inc(g.collective_compute(
            "AllReduce", ALU.add, replica_groups=PAIRS,
            ins=[bi1], outs=[bo1]), "cc", 1)
    T_BAR1 = S.c["cc"]

    if stop_after == "A":
        dxc = nc.dram_tensor("dbg_xc", [16, P, L], F32, kind="ExternalOutput").ap()
        dzz = nc.dram_tensor("dbg_z", [8, P, L], F32, kind="ExternalOutput").ap()
        dxn = nc.dram_tensor("dbg_xnt", [8, P, L], F32, kind="ExternalOutput").ap()
        dst_ = nc.dram_tensor("dbg_stats", [P, 32], F32, kind="ExternalOutput").ap()
        dxa = nc.dram_tensor("dbg_xa", [P, 4 * D], F32, kind="ExternalOutput").ap()
        qo_f, qs_f, qg_f = S.c["qo"], S.c["qs"], S.c["qg"]

        @block.gpsimd
        def _(g):
            g.wait_ge(S.h["qo"], qo_f)
            g.wait_ge(S.h["qs"], qs_f)
            g.wait_ge(S.h["qg"], qg_f)
            g.wait_ge(S.h["cc"], T_BAR1)
            g.dma_start(out=dxc, in_=shxc).then_inc(S.h["qg"], 16)
            g.dma_start(out=dzz, in_=pz).then_inc(S.h["qg"], 16)
            g.dma_start(out=dxn, in_=pxnt).then_inc(S.h["qg"], 16)
            g.dma_start(out=dst_[:, 0:8], in_=istd[:]).then_inc(S.h["qg"], 16)
            g.dma_start(out=dst_[:, 8:16], in_=negm[:]).then_inc(S.h["qg"], 16)
            g.dma_start(out=dxa, in_=xa[:, 0:4 * D]).then_inc(S.h["qg"], 16)
            g.wait_ge(S.h["qg"], qg_f + 96)
        es_a.close()
        es_wi.close()
        es_mw.close()
        es.close()
        return nc

    es_a.close()
    es_wi.close()

    es_b = ExitStack()

    def sbb(name, shape, dtype):
        return es_b.enter_context(nc.sbuf_tensor(name, shape, dtype))

    xcs2 = sbb("xcs2", [P, 2 * 16 * TS], BF16)
    zss2 = sbb("zss2", [P, 8 * TS], BF16)
    dlt = sbb("dlt", [P, 8 * TS], BF16)
    dxb = sbb("dxb", [P, 8 * TS], BF16)
    dab = sbb("dab", [P, 2 * 8 * TS], BF16)
    ubf = sbb("ubf", [P, TS], BF16)
    sbf = sbb("sbf", [P, TS], BF16)
    pbf2 = sbb("pbf2", [P, TS], BF16)
    yac = sbb("yac", [P, 8 * TS], BF16)
    yfn2 = sbb("yfn2", [P, 2 * 8 * TS], BF16)
    bsb = sbb("bsb", [NST, TS], BF16)
    bbc = sbb("bbc", [P, NST * TS], BF16)
    pre = sbb("pre", [P, 2 * TS], F32)
    spa = sbb("spa", [P, TS], F32)
    spr = sbb("spr", [P, 2 * TS], BF16)
    post = sbb("post", [P, 8 * TS], BF16)
    ps_dt = [es_b.enter_context(nc.psum_tensor(f"ps_dt{i}", [P, TS], F32))
             for i in range(2)]
    ps_bt = es_b.enter_context(nc.psum_tensor("ps_bt", [NST, TS], F32))
    ps_bb = es_b.enter_context(nc.psum_tensor("ps_bb", [P, TS], F32))
    ps_wo = [es_b.enter_context(nc.psum_tensor(f"ps_wo{i}", [P, TS], F32))
             for i in range(2)]

    t_ldb = [0] * nSU
    t_ldz = [0] * nSU
    t_powq = [0] * nSU
    prev = None

    prev2 = None
    for s_ in range(nSU):
        t0 = s_ * TS
        par = s_ % 2
        xcs = xcs2[:, par * 16 * TS:(par + 1) * 16 * TS]
        zss = zss2[:, 0:8 * TS]
        yfn = yfn2[:, par * 8 * TS:(par + 1) * 8 * TS]
        yfn_p = yfn2[:, (1 - par) * 8 * TS:(2 - par) * 8 * TS]
        sp0, sa0, sv0 = S.c["sp"], S.c["sa"], S.c["sv"]

        has_wo = s_ >= 1
        SP_WX2 = sp0 + 1
        SP_WDT = [SP_WX2 + 1 + m for m in range(8)]
        SP_BB = [SP_WX2 + 9 + n for n in range(16)]
        SP_WO = [SP_WX2 + 25 + m for m in range(8)] if has_wo else []
        SA_BSB = sa0 + 1
        SA_SP = [SA_BSB + 1 + m for m in range(8)]
        SA_DA = [SA_BSB + 9 + n for n in range(16)]
        SA_WOEV = [SA_BSB + 25 + m for m in range(8)] if has_wo else []
        seq = []
        for m in range(8):
            if m >= 2:
                seq.append(("d", m - 2))
            seq.append(("p", m))
        seq += [("d", 6), ("d", 7)]
        SV_PRE, SV_DLT = {}, {}
        for i, (kind, m) in enumerate(seq):
            (SV_PRE if kind == "p" else SV_DLT)[m] = sv0 + i + 1
        SV_DX = sv0 + 17
        SV_BBEV = [SV_DX + 1 + n for n in range(16)]
        SV_SCAN = [SV_DX + 17 + n for n in range(16)]
        SV_YFN = SV_DX + 33

        @block.sync
        def _(sy, s_=s_, t0=t0, xcs=xcs, zss=zss):
            if s_ == 0:
                sy.wait_ge(S.h["qo"], QO_A)
            if s_ >= 2:
                sy.wait_ge(S.h["sv"], prev2["SV_YFN"])
                sy.wait_ge(S.h["sp"], prev2["SP_WDT7"])
            h = get_h(sy, "sync")
            xv = shxc.rearrange("d p l -> p d l")
            with sy.If_eq(h, 0):
                sy.dma_start(out=xcs[:, 0:8 * TS].rearrange(
                    "p (d t) -> p d t", d=8),
                    in_=xv[:, 0:8, t0:t0 + TS]).then_inc(S.h["qs"], 16)
            with sy.Else():
                sy.dma_start(out=xcs[:, 0:8 * TS].rearrange(
                    "p (d t) -> p d t", d=8),
                    in_=xv[:, 8:16, t0:t0 + TS]).then_inc(S.h["qs"], 16)
            S.c["qs"] += 16
            if s_ == 0:
                sy.wait_ge(S.h["cc"], T_BAR1)
            with sy.If_eq(h, 0):
                sy.dma_start(out=xcs[:, 8 * TS:16 * TS].rearrange(
                    "p (d t) -> p d t", d=8),
                    in_=xv[:, 8:16, t0:t0 + TS]).then_inc(S.h["qs"], 16)
            with sy.Else():
                sy.dma_start(out=xcs[:, 8 * TS:16 * TS].rearrange(
                    "p (d t) -> p d t", d=8),
                    in_=xv[:, 0:8, t0:t0 + TS]).then_inc(S.h["qs"], 16)
            S.c["qs"] += 16
        t_ldb[s_] = S.c["qs"]

        @block.sync
        def _(sy, s_=s_, t0=t0, zss=zss, prev=prev):
            if s_ >= 1:
                sy.wait_ge(S.h["sv"], prev["SV_YFN"])
            S.inc(sy.dma_start(out=zss[:].rearrange("p (d t) -> p d t", d=8),
                               in_=pz.rearrange("d p l -> p d l")[
                                   :, :, t0:t0 + TS]), "qs", 16)
        t_ldz[s_] = S.c["qs"]

        @block.tensor
        def _(t, s_=s_, SP_WDT=SP_WDT, SV_PRE=SV_PRE, SV_BBEV=SV_BBEV,
              SV_YFN=SV_YFN, SA_BSB=SA_BSB, SA_WOEV=SA_WOEV, SA_DA=SA_DA,
              prev=prev, xcs=xcs, yfn=yfn, yfn_p=yfn_p):
            t.wait_ge(S.h["qs"], t_ldb[s_])
            if s_ == 0:
                t.wait_ge(S.h["cc"], T_BAR1)
                t.wait_ge(S.h["qg"], QG_W)
            else:
                t.wait_ge(S.h["sa"], prev["SA_BSB"])
            for kt in range(16):
                _li = t.matmul(ps_bt[:], wx2[:, kt * NST:(kt + 1) * NST],
                               xcs[:, kt * TS:(kt + 1) * TS],
                               start=(kt == 0), stop=(kt == 15))
            S.inc(_li, "sp")
            for mt in range(8):
                if mt >= 2:
                    t.wait_ge(S.h["sv"], SV_PRE[mt - 2])
                if mt == 0 and s_ >= 1:
                    t.wait_ge(S.h["sa"], prev["SA_DA15"])
                for kt in range(16):
                    _li = t.matmul(ps_dt[mt % 2][:], wdt_t(kt, mt),
                                   xcs[:, kt * TS:(kt + 1) * TS],
                                   start=(kt == 0), stop=(kt == 15))
                S.inc(_li, "sp")
            for n in range(NST):
                if n == 0:
                    t.wait_ge(S.h["sa"], SA_BSB)
                else:
                    t.wait_ge(S.h["sv"], SV_BBEV[n - 1])
                _li = t.matmul(ps_bb[:], e16[:, n * P:(n + 1) * P], bsb[:],
                               start=True, stop=True)
                S.inc(_li, "sp")
            if s_ >= 1:
                for mt in range(8):
                    if mt == 0:
                        t.wait_ge(S.h["sv"], prev["SV_YFN"])
                        if s_ >= 2:
                            t.wait_ge(S.h["sa"], prev["SA_WOEV7"])
                    if mt >= 2:
                        t.wait_ge(S.h["sa"], SA_WOEV[mt - 2])
                    for kt in range(8):
                        _li = t.matmul(ps_wo[mt % 2][:], wout_t(kt, mt),
                                       yfn_p[:, kt * TS:(kt + 1) * TS],
                                       start=(kt == 0), stop=(kt == 7))
                    S.inc(_li, "sp")

        @block.scalar
        def _(s, s_=s_, t0=t0, SP_WX2=SP_WX2, SP_WDT=SP_WDT, SP_BB=SP_BB,
              SP_WO=SP_WO, SV_PRE=SV_PRE, SV_DLT=SV_DLT, SV_DX=SV_DX,
              SV_SCAN=SV_SCAN, SA_WOEV=SA_WOEV, prev=prev):
            s.wait_ge(S.h["sp"], SP_WX2)
            if s_ >= 1:
                s.wait_ge(S.h["sp"], prev["SP_BB15"])
            _li = s.activation(out=bsb[:], in_=ps_bt[:], func=AF.Copy)
            S.inc(_li, "sa")
            for mt in range(8):
                s.wait_ge(S.h["sv"], SV_PRE[mt])
                if mt >= 2:
                    s.wait_ge(S.h["sv"], SV_DLT[mt - 2])
                pv = pre[:, (mt % 2) * TS:(mt % 2 + 1) * TS]
                s.activation(out=spa[:], in_=pv, func=AF.Abs)
                s.activation(out=spa[:], in_=spa[:], func=AF.Exp,
                             scale=cc_(C_MONE))
                _li = s.activation(out=spr[:, (mt % 2) * TS:(mt % 2 + 1) * TS],
                                   in_=spa[:], func=AF.Ln, bias=cc_(C_ONE))
                S.inc(_li, "sa")
            for n in range(NST):
                if n == 0:
                    s.wait_ge(S.h["sv"], SV_DX)
                elif n >= 2:
                    s.wait_ge(S.h["sv"], SV_SCAN[n - 2])
                np_ = n % 2
                for dt in range(8):
                    _li = s.activation(
                        out=dab[:, (np_ * 8 + dt) * TS:(np_ * 8 + dt + 1) * TS],
                        in_=dlt[:, dt * TS:(dt + 1) * TS], func=AF.Exp,
                        scale=aneg[:, dt * NST + n:dt * NST + n + 1])
                S.inc(_li, "sa")
            if s_ >= 1:
                for mt in range(8):
                    s.wait_ge(S.h["sp"], SP_WO[mt])
                    if s_ >= 2 and mt == 0:
                        s.wait_ge(S.h["qo"], t_powq[s_ - 1])
                    _li = s.activation(out=post[:, mt * TS:(mt + 1) * TS],
                                       in_=ps_wo[mt % 2][:], func=AF.Copy)
                    S.inc(_li, "sa")
                s.wait_ge(S.h["sa"], S.c["sa"])
                h = get_h(s, "scalar")
                srcp = post[:].rearrange("p (d t) -> p d t", d=8)
                tp0 = (s_ - 1) * TS
                with s.If_eq(h, 0):
                    s.dma_start(out=shpo.rearrange("w d p l -> p w d l")[
                        :, 0, :, tp0:tp0 + TS], in_=srcp).then_inc(S.h["qo"], 16)
                with s.Else():
                    s.dma_start(out=shpo.rearrange("w d p l -> p w d l")[
                        :, 1, :, tp0:tp0 + TS], in_=srcp).then_inc(S.h["qo"], 16)
                S.c["qo"] += 16
        t_powq[s_] = S.c["qo"]

        @block.vector
        def _(v, s_=s_, seq=seq, SP_WDT=SP_WDT, SP_BB=SP_BB, SA_SP=SA_SP,
              SA_DA=SA_DA, prev=prev, xcs=xcs, zss=zss, yfn=yfn):
            for kind, m in seq:
                if kind == "p":
                    v.wait_ge(S.h["sp"], SP_WDT[m])
                    if m >= 2:
                        v.wait_ge(S.h["sa"], SA_SP[m - 2])
                    _li = v.tensor_scalar(
                        out=pre[:, (m % 2) * TS:(m % 2 + 1) * TS],
                        in0=ps_dt[m % 2][:], scalar1=bdtv[:, m:m + 1],
                        scalar2=None, op0=ALU.add)
                    S.inc(_li, "sv")
                else:
                    v.wait_ge(S.h["sa"], SA_SP[m])
                    pv = pre[:, (m % 2) * TS:(m % 2 + 1) * TS]
                    dv = dlt[:, m * TS:(m + 1) * TS]
                    v.tensor_relu(dv, pv)
                    _li = v.tensor_tensor(
                        dv, dv, spr[:, (m % 2) * TS:(m % 2 + 1) * TS], ALU.add)
                    S.inc(_li, "sv")
            for dt in range(8):
                _li = v.tensor_tensor(dxb[:, dt * TS:(dt + 1) * TS],
                                      dlt[:, dt * TS:(dt + 1) * TS],
                                      xcs[:, dt * TS:(dt + 1) * TS], ALU.mult)
            S.inc(_li, "sv")
            for n in range(NST):
                v.wait_ge(S.h["sp"], SP_BB[n])
                _li = v.tensor_copy(bbc[:, n * TS:(n + 1) * TS], ps_bb[:])
                S.inc(_li, "sv")
            for n in range(NST):
                v.wait_ge(S.h["sa"], SA_DA[n])
                np_ = n % 2
                bb_v = bbc[:, n * TS:(n + 1) * TS]
                for dt in range(8):
                    da_v = dab[:, (np_ * 8 + dt) * TS:(np_ * 8 + dt + 1) * TS]
                    y_v = yac[:, dt * TS:(dt + 1) * TS]
                    v.tensor_tensor(ubf[:], dxb[:, dt * TS:(dt + 1) * TS],
                                    bb_v, ALU.mult)
                    ini = 0.0 if s_ == 0 else carry[:, n * 8 + dt:n * 8 + dt + 1]
                    v.tensor_tensor_scan(sbf[:], da_v, ubf[:], ini,
                                         ALU.mult, ALU.add)
                    if n == 0:
                        v.tensor_tensor(y_v, sbf[:], bb_v, ALU.mult)
                    else:
                        v.tensor_tensor(pbf2[:, 0:TS], sbf[:], bb_v, ALU.mult)
                        v.tensor_tensor(y_v, y_v, pbf2[:, 0:TS], ALU.add)
                    _li = v.tensor_copy(carry[:, n * 8 + dt:n * 8 + dt + 1],
                                        sbf[:, TS - 1:TS])
                S.inc(_li, "sv")
            if s_ >= 2:
                v.wait_ge(S.h["sp"], prev["SP_WO7"])
            v.wait_ge(S.h["qs"], t_ldz[s_])
            for dt in range(8):
                v.scalar_tensor_tensor(
                    out=yfn[:, dt * TS:(dt + 1) * TS],
                    in0=xcs[:, dt * TS:(dt + 1) * TS],
                    scalar=dpv[:, dt:dt + 1],
                    in1=yac[:, dt * TS:(dt + 1) * TS],
                    op0=ALU.mult, op1=ALU.add)
                _li = v.tensor_tensor(yfn[:, dt * TS:(dt + 1) * TS],
                                      yfn[:, dt * TS:(dt + 1) * TS],
                                      zss[:, dt * TS:(dt + 1) * TS], ALU.mult)
            S.inc(_li, "sv")

        assert S.c["sp"] == (SP_WO[7] if has_wo else SP_BB[15]), S.c["sp"]
        assert S.c["sa"] == (SA_WOEV[7] if has_wo else SA_DA[15]), S.c["sa"]
        assert S.c["sv"] == SV_YFN, (S.c["sv"], SV_YFN)
        prev2 = prev
        prev = {"SV_YFN": SV_YFN, "SP_WDT7": SP_WDT[7], "SA_BSB": SA_BSB,
                "SA_DA15": SA_DA[15],
                "SA_WOEV7": SA_WOEV[7] if has_wo else
                (prev["SA_WOEV7"] if prev else 0),
                "SP_BB15": SP_BB[15],
                "SP_WO7": SP_WO[7] if has_wo else
                (prev["SP_WO7"] if prev else 0)}


    par_f = (nSU - 1) % 2
    yfn_f = yfn2[:, par_f * 8 * TS:(par_f + 1) * 8 * TS]
    SP_WOF = [S.c["sp"] + 1 + m for m in range(8)]
    SA_WOFE = [S.c["sa"] + 1 + m for m in range(8)]

    @block.tensor
    def _(t):
        for mt in range(8):
            if mt == 0:
                t.wait_ge(S.h["sv"], prev["SV_YFN"])
                t.wait_ge(S.h["sa"], prev["SA_WOEV7"])
            if mt >= 2:
                t.wait_ge(S.h["sa"], SA_WOFE[mt - 2])
            for kt in range(8):
                _li = t.matmul(ps_wo[mt % 2][:], wout_t(kt, mt),
                               yfn_f[:, kt * TS:(kt + 1) * TS],
                               start=(kt == 0), stop=(kt == 7))
            S.inc(_li, "sp")

    @block.scalar
    def _(s):
        for mt in range(8):
            s.wait_ge(S.h["sp"], SP_WOF[mt])
            if mt == 0:
                s.wait_ge(S.h["qo"], t_powq[nSU - 1])
            _li = s.activation(out=post[:, mt * TS:(mt + 1) * TS],
                               in_=ps_wo[mt % 2][:], func=AF.Copy)
            S.inc(_li, "sa")
        s.wait_ge(S.h["sa"], S.c["sa"])
        h = get_h(s, "scalar")
        srcp = post[:].rearrange("p (d t) -> p d t", d=8)
        tf0 = (nSU - 1) * TS
        with s.If_eq(h, 0):
            s.dma_start(out=shpo.rearrange("w d p l -> p w d l")[
                :, 0, :, tf0:tf0 + TS], in_=srcp).then_inc(S.h["qo"], 16)
        with s.Else():
            s.dma_start(out=shpo.rearrange("w d p l -> p w d l")[
                :, 1, :, tf0:tf0 + TS], in_=srcp).then_inc(S.h["qo"], 16)
        S.c["qo"] += 16
    assert S.c["sp"] == SP_WOF[7]
    assert S.c["sa"] == SA_WOFE[7]

    QO_B = S.c["qo"]

    @block.gpsimd
    def _(g):
        g.wait_ge(S.h["qo"], QO_B)
        S.inc(g.collective_compute(
            "AllReduce", ALU.add, replica_groups=PAIRS,
            ins=[bi2], outs=[bo2]), "cc", 1)
    T_BAR2 = S.c["cc"]

    if stop_after == "B":
        dpo = nc.dram_tensor("dbg_po", [2, 8, P, L], F32,
                             kind="ExternalOutput").ap()
        ddl = nc.dram_tensor("dbg_dlt", [P, 8 * TS], F32,
                             kind="ExternalOutput").ap()
        dbs = nc.dram_tensor("dbg_bsb", [NST, TS], F32,
                             kind="ExternalOutput").ap()
        dbb = nc.dram_tensor("dbg_bbc", [P, NST * TS], F32,
                             kind="ExternalOutput").ap()
        dya = nc.dram_tensor("dbg_yac", [P, 8 * TS], F32,
                             kind="ExternalOutput").ap()
        dyf = nc.dram_tensor("dbg_yfn", [P, 8 * TS], F32,
                             kind="ExternalOutput").ap()
        qo_f, qs_f, qg_f = S.c["qo"], S.c["qs"], S.c["qg"]

        @block.gpsimd
        def _(g):
            g.wait_ge(S.h["qo"], qo_f)
            g.wait_ge(S.h["qs"], qs_f)
            g.wait_ge(S.h["qg"], qg_f)
            g.wait_ge(S.h["cc"], T_BAR2)
            g.dma_start(out=dpo, in_=shpo).then_inc(S.h["qg"], 16)
            g.dma_start(out=ddl, in_=dlt[:]).then_inc(S.h["qg"], 16)
            g.dma_start(out=dbs, in_=bsb[:]).then_inc(S.h["qg"], 16)
            g.dma_start(out=dbb, in_=bbc[:]).then_inc(S.h["qg"], 16)
            g.dma_start(out=dya, in_=yac[:]).then_inc(S.h["qg"], 16)
            g.dma_start(out=dyf, in_=yfn[:]).then_inc(S.h["qg"], 16)
            g.wait_ge(S.h["qg"], qg_f + 96)
        es_b.close()
        es_mw.close()
        es.close()
        return nc

    es_b.close()
    es_mw.close()

    if False:
        pass

    es_c = ExitStack()

    def sbc(name, shape, dtype):
        return es_c.enter_context(nc.sbuf_tensor(name, shape, dtype))

    xnh = sbc("xnh", [P, 8 * TH], BF16)
    racc = sbc("racc", [P, 8 * TH], F32)
    cft = sbc("cft", [NE + 1, TH], BF16)
    lgb = sbc("lgb", [NE, 512], F32)
    spb = sbc("spb", [NE, 512], F32)
    acb = sbc("acb", [NE, 512], F32)
    sptok = sbc("sptok", [P, 4 * NE], F32)
    actok = sbc("actok", [P, 4 * NE], F32)
    cfe = sbc("cfe", [P, 4 * (NE + 1)], F32)
    m1b = sbc("m1b", [P, 3], F32)
    rwT = sbc("rwT", [P, 8 * NE], BF16)
    rb2x = sbc("rb2x", [NE + 1, D], BF16)
    w1e = sbc("w1e", [P, 2 * 8 * HF], BF16)
    w2e = sbc("w2e", [P, 2 * 4 * D], BF16)
    hbuf = sbc("hbuf", [P, 4 * 512], BF16)
    cfbb = sbc("cfbb", [P, 512], BF16)
    rtmp = sbc("rtmp", [P, 2 * 512], BF16)
    pobuf = sbc("pobuf", [P, 2 * 8 * 512], BF16)
    xrl = sbc("xrl", [P, 4 * D], F32)
    otok = sbc("otok", [P, 2 * D], F32)

    @block.gpsimd
    def _(g):
        g.wait_ge(S.h["cc"], T_BAR2)
        S.inc(g.dma_start(out=rwT[:].rearrange("p (k e) -> p k e", k=8),
                          in_=drw.rearrange("(k p) e -> p k e", p=P)), "qg", 16)
        S.inc(g.dma_start(out=rb2x[:], in_=db2), "qg", 16)
    QG_C0 = S.c["qg"]

    @block.sync
    def _(sy):
        sy.wait_ge(S.h["cc"], T_BAR2)
        h = get_h(sy, "sync")
        with sy.If_eq(h, 0):
            sy.dma_start(out=xnh[:].rearrange("p (d t) -> p d t", d=8),
                         in_=pxnt.rearrange("d p l -> p d l")[
                             :, :, 0:TH]).then_inc(S.h["qs"], 16)
        with sy.Else():
            sy.dma_start(out=xnh[:].rearrange("p (d t) -> p d t", d=8),
                         in_=pxnt.rearrange("d p l -> p d l")[
                             :, :, TH:L]).then_inc(S.h["qs"], 16)
        S.c["qs"] += 16
    T_XNH = S.c["qs"]

    es_c1 = ExitStack()
    ps_lg = es_c1.enter_context(nc.psum_tensor("ps_lg", [NE, 512], F32))
    ps_tp = es_c1.enter_context(nc.psum_tensor("ps_tp", [P, 64], F32))
    ps_cf = es_c1.enter_context(nc.psum_tensor("ps_cf", [NE + 1, 512], F32))

    t_rt_done = 0
    t_top2 = [0] * nTB
    for tb in range(nTB):
        @block.tensor
        def _(t, tb=tb):
            t.wait_ge(S.h["qs"], T_XNH)
            if tb == 0:
                t.wait_ge(S.h["qg"], QG_C0)
            else:
                t.wait_ge(S.h["sv"], t_top2[tb - 1])
            for kt in range(8):
                _li = t.matmul(ps_lg[:], rwT[:, kt * NE:(kt + 1) * NE],
                         xnh[:, kt * TH + tb * 512:kt * TH + (tb + 1) * 512],
                         start=(kt == 0), stop=(kt == 7))
            S.inc(_li, "sp")
        t_lgm = S.c["sp"]

        @block.vector
        def _(v, t_lgm=t_lgm):
            v.wait_ge(S.h["sp"], t_lgm)
            _li = v.tensor_scalar(out=lgb[:], in0=ps_lg[:], scalar1=ebv[:, 0:1],
                            scalar2=None, op0=ALU.add)
            S.inc(_li, "sv")
        t_lg = S.c["sv"]

        @block.scalar
        def _(s, t_lg=t_lg):
            s.wait_ge(S.h["sv"], t_lg)
            s.activation(out=spb[:], in_=lgb[:], func=AF.Abs)
            s.activation(out=spb[:], in_=spb[:], func=AF.Exp,
                         scale=cst[0:NE, C_MONE:C_MONE + 1])
            _li = s.activation(out=spb[:], in_=spb[:], func=AF.Ln,
                         bias=cst[0:NE, C_ONE:C_ONE + 1])
            S.inc(_li, "sa")
        t_sp1 = S.c["sa"]

        @block.vector
        def _(v, t_sp1=t_sp1):
            v.wait_ge(S.h["sa"], t_sp1)
            v.tensor_relu(acb[:], lgb[:])
            v.tensor_tensor(spb[:], spb[:], acb[:], ALU.add)
            _li = v.tensor_scalar_max(spb[:], spb[:], cst[0:NE, C_TINY:C_TINY + 1])
            S.inc(_li, "sv")
        t_sp2 = S.c["sv"]

        @block.scalar
        def _(s, t_sp2=t_sp2):
            s.wait_ge(S.h["sv"], t_sp2)
            s.activation(out=acb[:], in_=spb[:], func=AF.Ln)
            _li = s.activation(out=acb[:], in_=acb[:], func=AF.Exp,
                         scale=cst[0:NE, C_HALF:C_HALF + 1])
            S.inc(_li, "sa")
        t_act = S.c["sa"]

        @block.tensor
        def _(t, t_act=t_act, tb=tb):
            t.wait_ge(S.h["sa"], t_act)
            if tb > 0:
                t.wait_ge(S.h["sa"], S.c["sa"])
            for tt in range(4):
                t.transpose(ps_tp[:, tt * 8:(tt + 1) * 8],
                            spb[:, tt * P:(tt + 1) * P], idf[0:NE, 0:NE])
                _li = t.transpose(ps_tp[:, 32 + tt * 8:32 + (tt + 1) * 8],
                            acb[:, tt * P:(tt + 1) * P], idf[0:NE, 0:NE])
            S.inc(_li, "sp")
        t_tpm = S.c["sp"]

        @block.scalar
        def _(s, t_tpm=t_tpm):
            s.wait_ge(S.h["sp"], t_tpm)
            s.activation(out=sptok[:], in_=ps_tp[:, 0:32], func=AF.Copy)
            _li = s.activation(out=actok[:], in_=ps_tp[:, 32:64], func=AF.Copy)
            S.inc(_li, "sa")
        t_tpe = S.c["sa"]

        @block.vector
        def _(v, t_tpe=t_tpe):
            v.wait_ge(S.h["sa"], t_tpe)
            def rtw(inst):
                S.inc(inst, "sv")
                v.wait_ge(S.h["sv"], S.c["sv"])
            for tt in range(4):
                sp_t = sptok[:, tt * NE:(tt + 1) * NE]
                ac_t = actok[:, tt * NE:(tt + 1) * NE]
                cf_t = cfe[:, tt * (NE + 1):(tt + 1) * (NE + 1)]
                rtw(v.reduce_max(m1b[:, 0:1], sp_t, axis=mybir.AxisListType.X))
                rtw(v.tensor_scalar(out=cf_t[:, 0:NE], in0=sp_t,
                                    scalar1=m1b[:, 0:1], scalar2=None,
                                    op0=ALU.is_equal))
                rtw(v.tensor_scalar_mul(cf_t[:, 0:NE], cf_t[:, 0:NE], 1e30))
                rtw(v.tensor_tensor(cf_t[:, 0:NE], sp_t, cf_t[:, 0:NE],
                                    ALU.subtract))
                rtw(v.reduce_max(m1b[:, 1:2], cf_t[:, 0:NE],
                                 axis=mybir.AxisListType.X))
                rtw(v.tensor_scalar(out=cf_t[:, 0:NE], in0=sp_t,
                                    scalar1=m1b[:, 1:2], scalar2=None,
                                    op0=ALU.is_ge))
                v.tensor_tensor(cf_t[:, 0:NE], cf_t[:, 0:NE], ac_t, ALU.mult)
                _li = v.memset(cf_t[:, NE:NE + 1], 1.0)
            S.inc(_li, "sv")
        t_top2[tb] = S.c["sv"]

        @block.tensor
        def _(t, tb=tb):
            t.wait_ge(S.h["sv"], t_top2[tb])
            if tb > 0:
                t.wait_ge(S.h["sa"], S.c["sa"])
            for tt in range(4):
                _li = t.transpose(ps_cf[:, tt * P:(tt + 1) * P],
                            cfe[:, tt * (NE + 1):(tt + 1) * (NE + 1)], idf[:])
            S.inc(_li, "sp")
        t_cfm = S.c["sp"]

        @block.scalar
        def _(s, t_cfm=t_cfm, tb=tb):
            s.wait_ge(S.h["sp"], t_cfm)
            _li = s.activation(out=cft[:, tb * 512:(tb + 1) * 512], in_=ps_cf[:],
                         func=AF.Copy)
            S.inc(_li, "sa")
        t_rt_done = S.c["sa"]

    es_c1.close()

    es_c2 = ExitStack()
    ps_h = [es_c2.enter_context(nc.psum_tensor(f"ps_h{i}", [P, 512], F32))
            for i in range(2)]
    ps_rt = [es_c2.enter_context(nc.psum_tensor(f"ps_rt{i}", [P, 512], F32))
             for i in range(2)]
    ps_cb = es_c2.enter_context(nc.psum_tensor("ps_cb", [P, 512], F32))

    e_order = [8] + list(range(8))
    sp0e, sa0e, sv0e = S.c["sp"], S.c["sa"], S.c["sv"]
    SP_MM1 = {}
    SP_CFB = {}
    SP_MM2 = {}
    SA_M1E = {}
    SV_CFB = {}
    SV_RT = {}
    spv, sav, svv = sp0e, sa0e, sv0e
    for ei, e in enumerate(e_order):
        for tb in range(nTB):
            for mt1 in range(4):
                spv += 1; SP_MM1[(ei, tb, mt1)] = spv
                sav += 1; SA_M1E[(ei, tb, mt1)] = sav
            if e != 8:
                spv += 1; SP_CFB[(ei, tb)] = spv
                svv += 1; SV_CFB[(ei, tb)] = svv
            for mt in range(8):
                spv += 1; SP_MM2[(ei, tb, mt)] = spv
                svv += 1; SV_RT[(ei, tb, mt)] = svv
    SP_E_END, SA_E_END, SV_E_END = spv, sav, svv

    def prev_mm2_last(ei, tb):
        if tb > 0:
            return SP_MM2[(ei, tb - 1, 7)]
        if ei > 0:
            return SP_MM2[(ei - 1, nTB - 1, 7)]
        return None

    t_wld = [0] * 9

    @block.gpsimd
    def _(g):
        for ei, e in enumerate(e_order):
            ep = ei % 2
            if ei >= 2:
                g.wait_ge(S.h["sp"], SP_MM2[(ei - 2, nTB - 1, 7)])
            S.inc(g.dma_start(
                out=w1e[:, ep * 8 * HF:(ep + 1) * 8 * HF].rearrange(
                    "p (k c) -> p k c", k=8),
                in_=dw1[e].rearrange("(k p) c -> p k c", p=P)), "qg", 16)
            S.inc(g.dma_start(
                out=w2e[:, ep * 4 * D:(ep + 1) * 4 * D].rearrange(
                    "p (k c) -> p k c", k=4),
                in_=dw2[e].rearrange("(k p) c -> p k c", p=P)), "qg", 16)
            t_wld[ei] = S.c["qg"]

    @block.tensor
    def _(t):
        t.wait_ge(S.h["sa"], t_rt_done)
        for ei, e in enumerate(e_order):
            ep = ei % 2
            for tb in range(nTB):
                if tb == 0:
                    t.wait_ge(S.h["qg"], t_wld[ei])
                for mt1 in range(4):
                    g_prev = (ei * nTB + tb) * 4 + mt1 - 2
                    if g_prev >= 0:
                        pei, r = divmod(g_prev, nTB * 4)
                        ptb, pmt1 = divmod(r, 4)
                        t.wait_ge(S.h["sa"], SA_M1E[(pei, ptb, pmt1)])
                    for kt in range(8):
                        _li = t.matmul(
                            ps_h[mt1 % 2][:],
                            w1e[:, ep * 8 * HF + kt * HF + mt1 * P:
                                ep * 8 * HF + kt * HF + (mt1 + 1) * P],
                            xnh[:, kt * TH + tb * 512:kt * TH + (tb + 1) * 512],
                            start=(kt == 0), stop=(kt == 7))
                    S.inc(_li, "sp")
                if e != 8:
                    keys = sorted(SV_CFB)
                    my_i = keys.index((ei, tb))
                    if my_i > 0:
                        t.wait_ge(S.h["sv"], SV_CFB[keys[my_i - 1]])
                    _li = t.matmul(ps_cb[:], e9[:, e * P:(e + 1) * P],
                                   cft[:, tb * 512:(tb + 1) * 512],
                                   start=True, stop=True)
                    S.inc(_li, "sp")
                for mt in range(8):
                    if mt == 0:
                        if e != 8:
                            t.wait_ge(S.h["sv"], SV_CFB[(ei, tb)])
                        else:
                            t.wait_ge(S.h["sa"], SA_M1E[(ei, tb, 3)])
                    g2 = (ei * nTB + tb) * 8 + mt - 2
                    if g2 >= 0:
                        pei, r = divmod(g2, nTB * 8)
                        ptb, pmt = divmod(r, 8)
                        t.wait_ge(S.h["sv"], SV_RT[(pei, ptb, pmt)])
                    for kt in range(4):
                        _li = t.matmul(
                            ps_rt[mt % 2][:],
                            w2e[:, ep * 4 * D + kt * D + mt * P:
                                ep * 4 * D + kt * D + (mt + 1) * P],
                            hbuf[:, kt * 512:(kt + 1) * 512],
                            start=(kt == 0), stop=(kt == 3 and e != 8))
                    if e == 8:
                        _li = t.matmul(ps_rt[mt % 2][:],
                                       rb2x[:, mt * P:(mt + 1) * P],
                                       cft[:, tb * 512:(tb + 1) * 512],
                                       start=False, stop=True)
                    S.inc(_li, "sp")
    assert S.c["sp"] == SP_E_END, (S.c["sp"], SP_E_END)

    @block.scalar
    def _(s):
        for ei, e in enumerate(e_order):
            for tb in range(nTB):
                pm = prev_mm2_last(ei, tb)
                for mt1 in range(4):
                    s.wait_ge(S.h["sp"], SP_MM1[(ei, tb, mt1)])
                    if pm is not None and mt1 == 0:
                        s.wait_ge(S.h["sp"], pm)
                    _li = s.activation(out=hbuf[:, mt1 * 512:(mt1 + 1) * 512],
                                       in_=ps_h[mt1 % 2][:], func=AF.Silu,
                                       bias=b1v[:, e * 4 + mt1:e * 4 + mt1 + 1])
                    S.inc(_li, "sa")
    assert S.c["sa"] == SA_E_END, (S.c["sa"], SA_E_END)

    @block.vector
    def _(v):
        for ei, e in enumerate(e_order):
            for tb in range(nTB):
                if e != 8:
                    v.wait_ge(S.h["sp"], SP_CFB[(ei, tb)])
                    v.wait_ge(S.h["sa"], SA_M1E[(ei, tb, 3)])
                    v.tensor_copy(cfbb[:], ps_cb[:])
                    for mt1 in range(4):
                        _li = v.tensor_tensor(
                            hbuf[:, mt1 * 512:(mt1 + 1) * 512],
                            hbuf[:, mt1 * 512:(mt1 + 1) * 512],
                            cfbb[:], ALU.mult)
                    S.inc(_li, "sv")
                for mt in range(8):
                    v.wait_ge(S.h["sp"], SP_MM2[(ei, tb, mt)])
                    dst = racc[:, mt * TH + tb * 512:mt * TH + (tb + 1) * 512]
                    if e == 8:
                        _li = v.tensor_copy(dst, ps_rt[mt % 2][:])
                    else:
                        _li = v.tensor_tensor(dst, dst, ps_rt[mt % 2][:],
                                              ALU.add)
                    S.inc(_li, "sv")
    assert S.c["sv"] == SV_E_END, (S.c["sv"], SV_E_END)

    T_EXP_DVE = SV_E_END

    es_c2.close()

    es_c3 = ExitStack()
    ps_ot = [es_c3.enter_context(nc.psum_tensor(f"ps_ot{i}", [P, D], F32))
             for i in range(2)]

    t_po_qs = [0] * nTB
    t_po_qg = [0] * nTB
    t_xr_qs = [0] * nTB
    t_xr_qo = [0] * nTB
    t_accs = [0] * nTB
    t_fin, t_finev, t_oadd, t_odma = [], [], [], []

    def _po_dma(eng, qn, w, tb, h):
        for hh in range(2):
            ctx = eng.If_eq(h, 0) if hh == 0 else eng.Else()
            with ctx:
                eng.dma_start(
                    out=pobuf[:, w * 8 * 512:(w + 1) * 8 * 512].rearrange(
                        "p (d t) -> p d t", d=8),
                    in_=shpo.rearrange("w d p l -> p w d l")[
                        :, w, :, hh * TH + tb * 512:hh * TH + (tb + 1) * 512]
                ).then_inc(S.h[qn], 16)
        S.c[qn] += 16

    def _xr_dma(eng, qn, i, tb, h):
        for hh in range(2):
            ctx = eng.If_eq(h, 0) if hh == 0 else eng.Else()
            with ctx:
                eng.dma_start(out=xrl[:, i * D:(i + 1) * D],
                              in_=dxin[hh * TH + tb * 512 + i * 128:
                                       hh * TH + tb * 512 + (i + 1) * 128, :]
                              ).then_inc(S.h[qn], 16)
        S.c[qn] += 16

    def emit_loads(tb):
        if tb == 0:
            @block.sync
            def _(sy):
                h = get_h(sy, "sync")
                for w in range(2):
                    _po_dma(sy, "qs", w, 0, h)
                for i in range(4):
                    _xr_dma(sy, "qs", i, 0, h)
            t_po_qg[0] = S.c["qg"]
            t_xr_qo[0] = S.c["qo"]
        else:
            @block.sync
            def _(sy, tb=tb):
                sy.wait_ge(S.h["sv"], t_accs[tb - 1])
                h = get_h(sy, "sync")
                _po_dma(sy, "qs", 0, tb, h)
            t_po_qs[tb] = S.c["qs"]

            @block.sync
            def _(sy, tb=tb):
                sy.wait_ge(S.h["sg"], t_oadd[(tb - 1) * 4 + 3])
                h = get_h(sy, "sync")
                for i in range(2):
                    _xr_dma(sy, "qs", i, tb, h)

            @block.gpsimd
            def _(g, tb=tb):
                g.wait_ge(S.h["sv"], t_accs[tb - 1])
                h = get_h(g, "gpsimd")
                _po_dma(g, "qg", 1, tb, h)
            t_po_qg[tb] = S.c["qg"]

            @block.scalar
            def _(s, tb=tb):
                s.wait_ge(S.h["sg"], t_oadd[(tb - 1) * 4 + 3])
                h = get_h(s, "scalar")
                for i in range(2, 4):
                    _xr_dma(s, "qo", i, tb, h)
            t_xr_qo[tb] = S.c["qo"]
        if tb == 0:
            t_po_qs[0] = S.c["qs"]
        t_xr_qs[tb] = S.c["qs"]

    emit_loads(0)
    for tb in range(nTB):
        @block.vector
        def _(v, tb=tb):
            v.wait_ge(S.h["qs"], t_po_qs[tb])
            if tb >= 1:
                v.wait_ge(S.h["qg"], t_po_qg[tb])
            for mt in range(8):
                dst = racc[:, mt * TH + tb * 512:mt * TH + (tb + 1) * 512]
                for w in range(2):
                    _li = v.tensor_tensor(
                        dst, dst,
                        pobuf[:, (w * 8 + mt) * 512:(w * 8 + mt + 1) * 512],
                        ALU.add)
            S.inc(_li, "sv")
        t_accs[tb] = S.c["sv"]

        for tt in range(4):
            fi = len(t_fin)

            @block.tensor
            def _(t, tb=tb, tt=tt, fi=fi):
                t.wait_ge(S.h["sv"], t_accs[tb])
                if fi >= 2:
                    t.wait_ge(S.h["sa"], t_finev[fi - 2])
                for mt in range(8):
                    _li = t.transpose(
                        ps_ot[fi % 2][:, mt * P:(mt + 1) * P],
                        racc[:, mt * TH + tb * 512 + tt * P:
                             mt * TH + tb * 512 + (tt + 1) * P], idf[:])
                S.inc(_li, "sp")
            t_fin.append(S.c["sp"])

            @block.scalar
            def _(s, fi=fi):
                s.wait_ge(S.h["sp"], t_fin[fi])
                if fi >= 2:
                    s.wait_ge(S.h["sg"], t_oadd[fi - 2])
                    s.wait_ge(S.h["qo"], t_odma[fi - 2])
                _li = s.activation(out=otok[:, (fi % 2) * D:((fi % 2) + 1) * D],
                             in_=ps_ot[fi % 2][:], func=AF.Copy)
                S.inc(_li, "sa")
            t_finev.append(S.c["sa"])

            @block.gpsimd
            def _(g, tb=tb, tt=tt, fi=fi):
                g.wait_ge(S.h["sa"], t_finev[fi])
                g.wait_ge(S.h["qs"], t_xr_qs[tb])
                if tt >= 2:
                    g.wait_ge(S.h["qo"], t_xr_qo[tb])
                _li = g.tensor_tensor(otok[:, (fi % 2) * D:((fi % 2) + 1) * D],
                                otok[:, (fi % 2) * D:((fi % 2) + 1) * D],
                                xrl[:, tt * D:(tt + 1) * D], ALU.add)
                S.inc(_li, "sg")
            t_oadd.append(S.c["sg"])

            @block.scalar
            def _(s, tb=tb, tt=tt, fi=fi):
                s.wait_ge(S.h["sg"], t_oadd[fi])
                S.inc(s.dma_start(
                    out=dout[tb * 512 + tt * P:tb * 512 + (tt + 1) * P, :],
                    in_=otok[:, (fi % 2) * D:((fi % 2) + 1) * D]), "qo", 16)
            t_odma.append(S.c["qo"])
        if tb + 1 < nTB:
            emit_loads(tb + 1)

    QO_END = S.c["qo"]
    QS_END = S.c["qs"]
    QG_END = S.c["qg"]

    @block.gpsimd
    def _(g):
        g.wait_ge(S.h["qo"], QO_END)
        g.wait_ge(S.h["qs"], QS_END)
        g.wait_ge(S.h["qg"], QG_END)

    es_c3.close()
    es_c.close()
    es.close()
    return nc


_NC_CACHE = {}


def _get_nc(L):
    if L not in _NC_CACHE:
        _NC_CACHE[L] = build_program(L)
    return _NC_CACHE[L]


def make_in_maps(inputs, L):
    import ml_dtypes
    x = np.asarray(inputs["L0_attn_x"], np.float32)
    w_in = np.asarray(inputs["L0_attn_W_in"], np.float32)
    conv_w = np.asarray(inputs["L0_attn_conv_w"], np.float32)
    conv_b = np.asarray(inputs["L0_attn_conv_b"], np.float32)
    w_xp = np.asarray(inputs["L0_attn_W_xproj"], np.float32)
    w_dt = np.asarray(inputs["L0_attn_W_dt"], np.float32)
    b_dt = np.asarray(inputs["L0_attn_b_dt"], np.float32)
    a_log = np.asarray(inputs["L0_attn_A_log"], np.float32)
    d_p = np.asarray(inputs["L0_attn_Dp"], np.float32)
    w_out = np.asarray(inputs["L0_attn_W_out"], np.float32)
    r_w = np.asarray(inputs["L0_ffn_routing_W"], np.float32)
    e_b = np.asarray(inputs["L0_ffn_expert_bias"], np.float32)
    sW1 = np.asarray(inputs["L0_ffn_sW1"], np.float32)
    sb1 = np.asarray(inputs["L0_ffn_sb1"], np.float32)
    sW2 = np.asarray(inputs["L0_ffn_sW2"], np.float32)
    sb2 = np.asarray(inputs["L0_ffn_sb2"], np.float32)
    rW1 = np.asarray(inputs["L0_ffn_rW1"], np.float32)
    rb1 = np.asarray(inputs["L0_ffn_rb1"], np.float32)
    rW2 = np.asarray(inputs["L0_ffn_rW2"], np.float32)
    rb2 = np.asarray(inputs["L0_ffn_rb2"], np.float32)

    w1 = np.ascontiguousarray(np.concatenate([rW1, sW1[None]], 0))
    b1 = np.ascontiguousarray(np.concatenate([rb1, sb1[None]], 0))
    w2 = np.ascontiguousarray(np.concatenate([rW2, sW2[None]], 0))
    b2 = np.ascontiguousarray(np.concatenate([rb2, sb2[None]], 0))
    identf = np.eye(P, dtype=np.float32)
    identb = np.eye(P, dtype=np.float32).astype(ml_dtypes.bfloat16)
    e16m = np.zeros((NST, NST * P), np.float32)
    for n in range(NST):
        e16m[n, n * P:(n + 1) * P] = 1.0
    e16m = e16m.astype(ml_dtypes.bfloat16)
    e9m = np.zeros((NE + 1, (NE + 1) * P), np.float32)
    for n in range(NE + 1):
        e9m[n, n * P:(n + 1) * P] = 1.0
    e9m = e9m.astype(ml_dtypes.bfloat16)

    c = np.ascontiguousarray
    in_maps = []
    for core in range(8):
        b, h = core // 2, core % 2
        own = slice(h * HD, (h + 1) * HD)
        pair = slice((1 - h) * HD, (2 - h) * HD)
        in_maps.append({
            "x": c(x[b, :L]),
            "w_in": c(np.concatenate(
                [w_in[:, own], w_in[:, DI + h * HD:DI + (h + 1) * HD]], 1)),
            "conv_w": c(conv_w[own, 0, :]),
            "conv_b": c(conv_b[own].reshape(HD, 1)),
            "w_x2": c(np.concatenate(
                [w_xp[own, NST:2 * NST], w_xp[pair, NST:2 * NST]], 0)),
            "w_dt": c(np.concatenate([w_dt[own, own], w_dt[pair, own]], 0)),
            "b_dt": c(b_dt[own].reshape(HD, 1)),
            "a_log": c(a_log[own]),
            "d_p": c(d_p[own].reshape(HD, 1)),
            "w_out": c(w_out[own, :]),
            "r_w": c(r_w.T),
            "e_b": c(e_b.reshape(NE, 1)),
            "w1": w1, "b1": b1, "w2": w2, "b2": b2,
            "identf": identf, "identb": identb, "e16": e16m, "e9": e9m,
        })
    return in_maps


def kernel(**inputs):
    L = 4096
    in_maps = make_in_maps(inputs, L)
    nc = _get_nc(L)
    res = run_bass_kernel_spmd(nc, in_maps, core_ids=list(range(8)))
    out = np.empty((4, L, D), np.float32)
    for core in range(8):
        b, h = core // 2, core % 2
        out[b, h * (L // 2):(h + 1) * (L // 2), :] = res.results[core]["out"]
    return out


# version 11
# speedup vs baseline: 1.1791x; 1.0231x over previous
import numpy as np
from contextlib import ExitStack

import concourse.bass as bass
import concourse.mybir as mybir
from concourse.bass_utils import run_bass_kernel_spmd

F32 = mybir.dt.float32
BF16 = mybir.dt.bfloat16
AF = mybir.ActivationFunctionType
ALU = mybir.AluOpType

P = 128
D = 1024
DI = 2048
HD = 1024
NST = 16
NE = 8
HF = 512
TS = 512
PAIRS = [[0, 1], [2, 3], [4, 5], [6, 7]]


class Sems:
    def __init__(self, es, nc, names):
        self.h = {n: es.enter_context(nc.semaphore(n)) for n in names}
        self.c = {n: 0 for n in names}

    def inc(self, inst, name, k=1):
        inst.then_inc(self.h[name], k)
        self.c[name] += k
        return self.c[name]


def build_program(L, stop_after=None):
    nCH = L // 512
    nSU = L // TS
    TH = L // 2
    nTB = TH // 512

    nc = bass.Bass("TRN2", target_bir_lowering=False, num_devices=8,
                   detect_race_conditions=False)

    dxin = nc.dram_tensor("x", [L, D], F32, kind="ExternalInput").ap()
    dwin = nc.dram_tensor("w_in", [D, DI], F32, kind="ExternalInput").ap()
    dcw = nc.dram_tensor("conv_w", [HD, 4], F32, kind="ExternalInput").ap()
    dcb = nc.dram_tensor("conv_b", [HD, 1], F32, kind="ExternalInput").ap()
    dwx2 = nc.dram_tensor("w_x2", [DI, NST], F32, kind="ExternalInput").ap()
    dwdt = nc.dram_tensor("w_dt", [DI, HD], F32, kind="ExternalInput").ap()
    dbdt = nc.dram_tensor("b_dt", [HD, 1], F32, kind="ExternalInput").ap()
    dalog = nc.dram_tensor("a_log", [HD, NST], F32, kind="ExternalInput").ap()
    ddp = nc.dram_tensor("d_p", [HD, 1], F32, kind="ExternalInput").ap()
    dwout = nc.dram_tensor("w_out", [HD, D], F32, kind="ExternalInput").ap()
    drw = nc.dram_tensor("r_w", [D, NE], F32, kind="ExternalInput").ap()
    deb = nc.dram_tensor("e_b", [NE, 1], F32, kind="ExternalInput").ap()
    dw1 = nc.dram_tensor("w1", [NE + 1, D, HF], F32, kind="ExternalInput").ap()
    db1 = nc.dram_tensor("b1", [NE + 1, HF], F32, kind="ExternalInput").ap()
    dw2 = nc.dram_tensor("w2", [NE + 1, HF, D], F32, kind="ExternalInput").ap()
    db2 = nc.dram_tensor("b2", [NE + 1, D], F32, kind="ExternalInput").ap()
    didf = nc.dram_tensor("identf", [P, P], F32, kind="ExternalInput").ap()
    didb = nc.dram_tensor("identb", [P, P], BF16, kind="ExternalInput").ap()
    de16 = nc.dram_tensor("e16", [NST, NST * P], BF16, kind="ExternalInput").ap()
    de9 = nc.dram_tensor("e9", [NE + 1, (NE + 1) * P], BF16,
                         kind="ExternalInput").ap()
    dout = nc.dram_tensor("out", [TH, D], F32, kind="ExternalOutput").ap()

    shxc = nc.dram_tensor("sh_xc", [16, P, L], BF16, kind="Internal",
                          addr_space="Shared").ap()
    shpo = nc.dram_tensor("sh_po", [2, 8, P, L], BF16, kind="Internal",
                          addr_space="Shared").ap()
    pz = nc.dram_tensor("p_z", [8, P, L], BF16, kind="Internal").ap()
    pxnt = nc.dram_tensor("p_xnt", [8, P, L], BF16, kind="Internal").ap()
    bi1 = nc.dram_tensor("bar_i1", [1, 16], F32, kind="Internal").ap()
    bo1 = nc.dram_tensor("bar_o1", [1, 16], F32, kind="Internal").ap()
    bi2 = nc.dram_tensor("bar_i2", [1, 16], F32, kind="Internal").ap()
    bo2 = nc.dram_tensor("bar_o2", [1, 16], F32, kind="Internal").ap()

    es = ExitStack()

    def sb(name, shape, dtype):
        return es.enter_context(nc.sbuf_tensor(name, shape, dtype))

    S = Sems(es, nc, ["qs", "qg", "qo", "sa", "sv", "sp", "sg", "cc"])
    block = es.enter_context(nc.Block())
    hreg = {}

    def get_h(eng, ename):
        if ename not in hreg:
            hreg[ename] = eng.to_reg(eng.partition_id() % 2)
        return hreg[ename]

    wx2 = sb("wx2_s", [P, 16 * NST], BF16)
    cwv = sb("cwv", [P, 8 * 4], F32)
    cbv = sb("cbv", [P, 8], F32)
    bdtv = sb("bdtv", [P, 8], F32)
    dpv = sb("dpv", [P, 8], F32)
    alog = sb("alog", [P, 8 * NST], F32)
    aneg = sb("aneg", [P, 8 * NST], F32)
    idf = sb("idf", [P, P], F32)
    idb = sb("idb", [P, P], BF16)
    e16 = sb("e16s", [NST, NST * P], BF16)
    e9 = sb("e9s", [NE + 1, (NE + 1) * P], BF16)
    cst = sb("cst", [P, 8], F32)
    zb = sb("zb", [1, 16], F32)
    C_EPS, C_MH, C_ONE, C_MONE, C_HALF, C_TINY = range(6)
    carry = sb("carry", [P, NST * 8], F32)
    ebv = sb("ebv", [NE, 1], F32)
    b1v = sb("b1v", [P, (NE + 1) * 4], F32)

    def cc_(i):
        return cst[:, i:i + 1]

    es_mw = ExitStack()
    es_wi = ExitStack()
    wdt = es_mw.enter_context(nc.sbuf_tensor("wdt_s", [P, 16 * HD], BF16))
    wout = es_mw.enter_context(nc.sbuf_tensor("wout_s", [P, 8 * D], BF16))
    win = es_wi.enter_context(nc.sbuf_tensor("win_s", [P, 8 * DI], BF16))

    def win_t(kt, mt):
        return win[:, kt * DI + mt * P:kt * DI + (mt + 1) * P]

    def wdt_t(kt, mt):
        return wdt[:, kt * HD + mt * P:kt * HD + (mt + 1) * P]

    def wout_t(kt, mt):
        return wout[:, kt * D + mt * P:kt * D + (mt + 1) * P]

    @block.gpsimd
    def _(g):
        S.inc(g.dma_start(out=win[:].rearrange("p (k c) -> p k c", k=8),
                          in_=dwin.rearrange("(k p) c -> p k c", p=P)), "qg", 16)
        S.inc(g.dma_start(out=wdt[:].rearrange("p (k c) -> p k c", k=16),
                          in_=dwdt.rearrange("(k p) c -> p k c", p=P)), "qg", 16)
        S.inc(g.dma_start(out=wout[:].rearrange("p (k c) -> p k c", k=8),
                          in_=dwout.rearrange("(k p) c -> p k c", p=P)), "qg", 16)
        S.inc(g.dma_start(out=wx2[:].rearrange("p (k n) -> p k n", k=16),
                          in_=dwx2.rearrange("(k p) n -> p k n", p=P)), "qg", 16)
        S.inc(g.dma_start(out=idb[:], in_=didb), "qg", 16)
        S.inc(g.dma_start(out=e16[:], in_=de16), "qg", 16)
        S.inc(g.dma_start(out=e9[:], in_=de9), "qg", 16)
        g.memset(zb[:], 0.0)
        g.memset(cst[:, C_EPS:C_EPS + 1], 1e-5)
        g.memset(cst[:, C_MH:C_MH + 1], -0.5)
        g.memset(cst[:, C_ONE:C_ONE + 1], 1.0)
        g.memset(cst[:, C_MONE:C_MONE + 1], -1.0)
        g.memset(cst[:, C_HALF:C_HALF + 1], 0.5)
        S.inc(g.memset(cst[:, C_TINY:C_TINY + 1], 1e-30), "sg", 1)
        g.wait_ge(S.h["sg"], 1)
        S.inc(g.dma_start(out=bi1, in_=zb[0:1, :]), "qg", 16)
        S.inc(g.dma_start(out=bi2, in_=zb[0:1, :]), "qg", 16)
    QG_W = S.c["qg"]

    @block.sync
    def _(sy):
        S.inc(sy.dma_start(out=cwv[:].rearrange("p (d f) -> p d f", d=8),
                           in_=dcw.rearrange("(d p) f -> p d f", p=P)), "qs", 16)
        S.inc(sy.dma_start(out=cbv[:],
                           in_=dcb.rearrange("(d p) o -> p (d o)", p=P), allow_slow_non_contiguous=True), "qs", 16)
        S.inc(sy.dma_start(out=bdtv[:],
                           in_=dbdt.rearrange("(d p) o -> p (d o)", p=P), allow_slow_non_contiguous=True), "qs", 16)
        S.inc(sy.dma_start(out=dpv[:],
                           in_=ddp.rearrange("(d p) o -> p (d o)", p=P), allow_slow_non_contiguous=True), "qs", 16)
        S.inc(sy.dma_start(out=alog[:].rearrange("p (d n) -> p d n", d=8),
                           in_=dalog.rearrange("(d p) n -> p d n", p=P)), "qs", 16)
        S.inc(sy.dma_start(out=idf[:], in_=didf), "qs", 16)
        S.inc(sy.dma_start(out=ebv[:], in_=deb), "qs", 16)
        S.inc(sy.dma_start(out=b1v[:].rearrange("p (e m) -> p e m", e=NE + 1),
                           in_=db1.rearrange("e (m p) -> p e m", p=P), allow_slow_non_contiguous=True), "qs", 16)
    QS_SETUP = S.c["qs"]

    @block.scalar
    def _(s):
        s.wait_ge(S.h["qs"], QS_SETUP)
        s.wait_ge(S.h["sg"], 1)
        s.activation(out=cst[:, 6:7], in_=cst[:, C_ONE:C_ONE + 1], func=AF.Ln)
        s.activation(out=cst[:, 6:7], in_=cst[:, C_ONE:C_ONE + 1], func=AF.Silu)
        S.inc(s.activation(out=aneg[:], in_=alog[:], func=AF.Exp), "sa")
    t_aexp = S.c["sa"]

    @block.vector
    def _(v):
        v.wait_ge(S.h["sa"], t_aexp)
        S.inc(v.tensor_scalar_mul(aneg[:], aneg[:], -1.0), "sv")
    T_ANEG = S.c["sv"]

    if stop_after == "S":
        qs_f, qg_f = S.c["qs"], S.c["qg"]

        @block.gpsimd
        def _(g):
            g.wait_ge(S.h["qs"], qs_f)
            g.wait_ge(S.h["qg"], qg_f)
            g.wait_ge(S.h["sv"], T_ANEG)
        es_mw.close()
        es.close()
        return nc

    es_a = ExitStack()

    def sba(name, shape, dtype):
        return es_a.enter_context(nc.sbuf_tensor(name, shape, dtype))

    xa = sba("xa", [P, 2 * 4 * D], BF16)
    stats = sba("stats", [P, 48], F32)
    mvb = sba("mvb", [P, 8], F32)
    istd = sba("istd", [P, 2 * 4], F32)
    negm = sba("negm", [P, 2 * 4], F32)
    xn = sba("xn", [P, 2 * 4 * D], BF16)
    xnt = sba("xnt", [P, 2 * 8 * 512], BF16)
    xcp = sba("xcp", [P, 2 * 8 * 515], BF16)
    cvt = sba("cvt", [P, 8 * 512], BF16)
    cvtm = sba("cvtm", [P, 512], BF16)
    xco = sba("xco", [P, 2 * 8 * 512], BF16)
    zso = sba("zso", [P, 2 * 8 * 512], BF16)
    ps_tr = [es_a.enter_context(nc.psum_tensor(f"ps_tr{i}", [P, 512], BF16))
             for i in range(4)]
    ps_xz = [es_a.enter_context(nc.psum_tensor(f"ps_xz{i}", [P, 512], F32))
             for i in range(4)]

    t_ln = [0] * nCH
    t_tr = [0] * nCH
    t_mm = [0] * nCH
    t_cv = [0] * nCH
    t_sil = [0] * nCH
    t_silq = [0] * nCH
    t_trg, t_trgev, t_mmg = [], [], []
    ec_tok = {}


    def a_views(k):
        pb = k % 2
        return dict(
            pb=pb,
            xa_v=xa[:, pb * 4 * D:(pb + 1) * 4 * D],
            xn_v=xn[:, pb * 4 * D:(pb + 1) * 4 * D],
            xnt_v=xnt[:, pb * 8 * 512:(pb + 1) * 8 * 512],
            xcp_v=xcp[:, pb * 8 * 515:(pb + 1) * 8 * 515],
            xcp_p=xcp[:, (1 - pb) * 8 * 515:(2 - pb) * 8 * 515],
            cvt_v=cvt[:, 0:8 * 512],
            xco_v=xco[:, pb * 8 * 512:(pb + 1) * 8 * 512],
            zso_v=zso[:, pb * 8 * 512:(pb + 1) * 8 * 512],
            istd_v=istd[:, pb * 4:(pb + 1) * 4],
            negm_v=negm[:, pb * 4:(pb + 1) * 4])

    def emit_load(k):
        vw = a_views(k)

        @block.gpsimd
        def _(g, k=k, xa_v=vw["xa_v"]):
            if k >= 2:
                g.wait_ge(S.h["sv"], t_ln[k - 2])
            for i in range(4):
                t0 = k * 512 + i * 128
                S.inc(g.dma_start(out=xa_v[:, i * D:(i + 1) * D],
                                  in_=dxin[t0:t0 + 128, :]), "qg", 16)
        return S.c["qg"]

    def emit_stats(k, t_ldx):
        vw = a_views(k)

        @block.vector
        def _(v, t_ldx=t_ldx, xa_v=vw["xa_v"], pb=vw["pb"]):
            v.wait_ge(S.h["qg"], t_ldx)
            for i in range(4):
                v.bn_stats(stats[:, i * 12:i * 12 + 6],
                           xa_v[:, i * D:i * D + 512])
                _li = v.bn_stats(stats[:, i * 12 + 6:i * 12 + 12],
                                 xa_v[:, i * D + 512:(i + 1) * D])
            S.inc(_li, "sv")
            v.wait_ge(S.h["sv"], S.c["sv"])
            for i in range(4):
                _li = v.bn_aggr(mvb[:, 2 * i:2 * i + 2],
                                stats[:, i * 12:(i + 1) * 12])
            S.inc(_li, "sv")
            v.wait_ge(S.h["sv"], S.c["sv"])
            for i in range(4):
                v.tensor_copy(istd[:, pb * 4 + i:pb * 4 + i + 1],
                              mvb[:, 2 * i + 1:2 * i + 2])
                _li = v.tensor_copy(negm[:, pb * 4 + i:pb * 4 + i + 1],
                                    mvb[:, 2 * i:2 * i + 1])
            S.inc(_li, "sv")
        return S.c["sv"]

    def emit_istd(k, t_bn):
        vw = a_views(k)

        @block.scalar
        def _(s, t_bn=t_bn, istd_v=vw["istd_v"]):
            s.wait_ge(S.h["sv"], t_bn)
            for i in range(4):
                s.activation(out=istd_v[:, i:i + 1], in_=istd_v[:, i:i + 1],
                             func=AF.Ln, bias=cc_(C_EPS))
                _li = s.activation(out=istd_v[:, i:i + 1],
                                   in_=istd_v[:, i:i + 1],
                                   func=AF.Exp, scale=cc_(C_MH))
            S.inc(_li, "sa")
        return S.c["sa"]

    def emit_norm(k, t_istd):
        vw = a_views(k)

        @block.vector
        def _(v, k=k, t_istd=t_istd, xa_v=vw["xa_v"], xn_v=vw["xn_v"],
              istd_v=vw["istd_v"], negm_v=vw["negm_v"]):
            v.wait_ge(S.h["sa"], t_istd)
            if k >= 2:
                v.wait_ge(S.h["sp"], t_tr[k - 2])
            for i in range(4):
                _li = v.tensor_scalar(out=negm_v[:, i:i + 1],
                                      in0=negm_v[:, i:i + 1],
                                      scalar1=istd_v[:, i:i + 1],
                                      scalar2=-1.0,
                                      op0=ALU.mult, op1=ALU.mult)
            S.inc(_li, "sv")
            v.wait_ge(S.h["sv"], S.c["sv"])
            for i in range(4):
                _li = v.tensor_scalar(out=xn_v[:, i * D:(i + 1) * D],
                                in0=xa_v[:, i * D:(i + 1) * D],
                                scalar1=negm_v[:, i:i + 1],
                                scalar2=istd_v[:, i:i + 1],
                                op0=ALU.add, op1=ALU.mult)
            S.inc(_li, "sv")
        t_ln[k] = S.c["sv"]

    def emit_pe(k):
        vw = a_views(k)
        ec_base = S.c["sv"] + (3 if k < nCH - 1 else 0)
        sa_off = 1 if k < nCH - 1 else 0
        tok = dict(trev=[ec_base + 1 + g for g in range(4)],
                   xzev=[ec_base + 5 + g for g in range(4)],
                   cv=ec_base + 9,
                   zsil=[S.c["sa"] + sa_off + 1 + g for g in range(4)],
                   silcv=S.c["sa"] + sa_off + 5)
        ec_tok[k] = tok

        for g_ in range(4):
            @block.tensor
            def _(t, k=k, g_=g_, xn_v=vw["xn_v"]):
                if g_ == 0:
                    t.wait_ge(S.h["sv"], t_ln[k])
                    if k == 0:
                        t.wait_ge(S.h["qg"], QG_W)
                if g_ >= 2:
                    t.wait_ge(S.h["sv"], tok["trev"][g_ - 2])
                elif k >= 1:
                    t.wait_ge(S.h["sv"], ec_tok[k - 1]["trev"][g_ + 2])
                for j in range(2):
                    dt = g_ * 2 + j
                    pt = ps_tr[(g_ % 2) * 2 + j]
                    for i in range(4):
                        _li = t.transpose(pt[:, i * P:(i + 1) * P],
                                    xn_v[:, i * D + dt * P:i * D + (dt + 1) * P],
                                    idb[:])
                S.inc(_li, "sp")
            t_trg.append(S.c["sp"])
        t_tr[k] = S.c["sp"]

        for g_ in range(8):
            @block.tensor
            def _(t, k=k, g_=g_, xnt_v=vw["xnt_v"]):
                if g_ == 0:
                    t.wait_ge(S.h["sv"], tok["trev"][3])
                    if k == 0:
                        t.wait_ge(S.h["sv"], T_ANEG)
                    if k >= 2:
                        t.wait_ge(S.h["sa"], t_sil[k - 2])
                    if k >= 1:
                        t.wait_ge(S.h["sv"], ec_tok[k - 1]["cv"])
                if g_ in (2, 3, 4, 5):
                    t.wait_ge(S.h["sv"], tok["xzev"][g_ - 2])
                elif g_ in (6, 7):
                    t.wait_ge(S.h["sa"], tok["zsil"][g_ - 6])
                elif k >= 1:
                    t.wait_ge(S.h["sa"], ec_tok[k - 1]["zsil"][g_ + 2])
                for j in range(2):
                    mt = g_ * 2 + j
                    pj = ps_xz[(g_ % 2) * 2 + j]
                    for kt in range(8):
                        _li = t.matmul(pj[:], win_t(kt, mt),
                                 xnt_v[:, kt * 512:(kt + 1) * 512],
                                 start=(kt == 0), stop=(kt == 7))
                S.inc(_li, "sp")
            t_mmg.append(S.c["sp"])
        t_mm[k] = S.c["sp"]

    def emit_ec(k):
        vw = a_views(k)
        for g_ in range(4):
            @block.vector
            def _(v, k=k, g_=g_, xnt_v=vw["xnt_v"]):
                v.wait_ge(S.h["sp"], t_trg[k * 4 + g_])
                if k >= 2 and g_ == 0:
                    v.wait_ge(S.h["qo"], t_silq[k - 2])
                    v.wait_ge(S.h["sp"], t_mm[k - 2])
                for j in range(2):
                    dt = g_ * 2 + j
                    _li = v.tensor_copy(xnt_v[:, dt * 512:(dt + 1) * 512],
                                        ps_tr[(g_ % 2) * 2 + j][:])
                S.inc(_li, "sv")
            t_trgev.append(S.c["sv"])
        for g_ in range(4):
            @block.vector
            def _(v, k=k, g_=g_, xcp_v=vw["xcp_v"]):
                v.wait_ge(S.h["sp"], t_mmg[k * 8 + g_])
                for j in range(2):
                    mt = g_ * 2 + j
                    pj = ps_xz[(g_ % 2) * 2 + j]
                    _li = v.tensor_copy(
                        xcp_v[:, mt * 515 + 3:mt * 515 + 515], pj[:])
                S.inc(_li, "sv")

        @block.vector
        def _(v, k=k, xcp_v=vw["xcp_v"], xcp_p=vw["xcp_p"],
              cvt_v=vw["cvt_v"]):
            if k >= 1:
                v.wait_ge(S.h["sa"], t_sil[k - 1])
            for dt in range(8):
                if k == 0:
                    v.memset(xcp_v[:, dt * 515:dt * 515 + 3], 0.0)
                else:
                    v.tensor_copy(xcp_v[:, dt * 515:dt * 515 + 3],
                                  xcp_p[:, dt * 515 + 512:dt * 515 + 515])
                cv = cvt_v[:, dt * 512:(dt + 1) * 512]
                v.tensor_scalar_mul(cv, xcp_v[:, dt * 515:dt * 515 + 512],
                                    cwv[:, dt * 4:dt * 4 + 1])
                for j in range(1, 4):
                    v.tensor_scalar_mul(cvtm[:],
                                        xcp_v[:, dt * 515 + j:dt * 515 + j + 512],
                                        cwv[:, dt * 4 + j:dt * 4 + j + 1])
                    _li = v.tensor_tensor(cv, cv, cvtm[:], ALU.add)
            S.inc(_li, "sv")
        t_cv[k] = S.c["sv"]
        assert t_cv[k] == ec_tok[k]["cv"], (k, t_cv[k], ec_tok[k]["cv"])
        assert t_trgev[k * 4:k * 4 + 4] == ec_tok[k]["trev"], k

    def emit_sil(k):
        vw = a_views(k)

        @block.scalar
        def _(s, k=k, xnt_v=vw["xnt_v"]):
            s.wait_ge(S.h["sv"], t_trgev[k * 4 + 3])
            S.inc(s.dma_start(
                out=pxnt.rearrange("d p l -> p d l")[:, :, k * 512:(k + 1) * 512],
                in_=xnt_v[:].rearrange("p (d t) -> p d t", d=8)), "qo", 16)

        for g_ in range(4, 8):
            @block.scalar
            def _(s, k=k, g_=g_, zso_v=vw["zso_v"]):
                s.wait_ge(S.h["sp"], t_mmg[k * 8 + g_])
                for j in range(2):
                    z = g_ * 2 + j - 8
                    pj = ps_xz[(g_ % 2) * 2 + j]
                    _li = s.activation(out=zso_v[:, z * 512:(z + 1) * 512],
                                 in_=pj[:], func=AF.Silu)
                S.inc(_li, "sa")
        assert S.c["sa"] == ec_tok[k]["zsil"][3], k

        @block.scalar
        def _(s, k=k, zso_v=vw["zso_v"]):
            s.wait_ge(S.h["sa"], S.c["sa"])
            S.inc(s.dma_start(
                out=pz.rearrange("d p l -> p d l")[:, :, k * 512:(k + 1) * 512],
                in_=zso_v[:].rearrange("p (d t) -> p d t", d=8)), "qo", 16)

        @block.scalar
        def _(s, k=k, cvt_v=vw["cvt_v"], xco_v=vw["xco_v"]):
            s.wait_ge(S.h["sv"], t_cv[k])
            if k >= 2:
                s.wait_ge(S.h["qo"], t_silq[k - 2])
            for dt in range(8):
                _li = s.activation(out=xco_v[:, dt * 512:(dt + 1) * 512],
                                   in_=cvt_v[:, dt * 512:(dt + 1) * 512],
                                   func=AF.Silu, bias=cbv[:, dt:dt + 1])
            S.inc(_li, "sa")
            s.wait_ge(S.h["sa"], S.c["sa"])
            h = get_h(s, "scalar")
            src = xco_v[:].rearrange("p (d t) -> p d t", d=8)
            with s.If_eq(h, 0):
                s.dma_start(out=shxc.rearrange("d p l -> p d l")[
                    :, 0:8, k * 512:(k + 1) * 512],
                    in_=src).then_inc(S.h["qo"], 16)
            with s.Else():
                s.dma_start(out=shxc.rearrange("d p l -> p d l")[
                    :, 8:16, k * 512:(k + 1) * 512],
                    in_=src).then_inc(S.h["qo"], 16)
            S.c["qo"] += 16
        t_sil[k] = S.c["sa"]
        t_silq[k] = S.c["qo"]
        assert t_sil[k] == ec_tok[k]["silcv"], k

    for k in range(nCH):
        t_ldx = emit_load(k)
        t_bn = emit_stats(k, t_ldx)
        if k >= 1:
            emit_ec(k - 1)
        t_istd = emit_istd(k, t_bn)
        if k >= 1:
            emit_sil(k - 1)
        emit_norm(k, t_istd)
        emit_pe(k)
    emit_ec(nCH - 1)
    emit_sil(nCH - 1)

    QO_A = S.c["qo"]
    QS_A = S.c["qs"]

    @block.gpsimd
    def _(g):
        g.wait_ge(S.h["qo"], QO_A)
        g.wait_ge(S.h["qs"], QS_A)
        g.wait_ge(S.h["sa"], t_sil[nCH - 1])
        g.wait_ge(S.h["qg"], QG_W)
        S.inc(g.collective_compute(
            "AllReduce", ALU.add, replica_groups=PAIRS,
            ins=[bi1], outs=[bo1]), "cc", 1)
    T_BAR1 = S.c["cc"]

    if stop_after == "A":
        dxc = nc.dram_tensor("dbg_xc", [16, P, L], F32, kind="ExternalOutput").ap()
        dzz = nc.dram_tensor("dbg_z", [8, P, L], F32, kind="ExternalOutput").ap()
        dxn = nc.dram_tensor("dbg_xnt", [8, P, L], F32, kind="ExternalOutput").ap()
        dst_ = nc.dram_tensor("dbg_stats", [P, 32], F32, kind="ExternalOutput").ap()
        dxa = nc.dram_tensor("dbg_xa", [P, 4 * D], F32, kind="ExternalOutput").ap()
        qo_f, qs_f, qg_f = S.c["qo"], S.c["qs"], S.c["qg"]

        @block.gpsimd
        def _(g):
            g.wait_ge(S.h["qo"], qo_f)
            g.wait_ge(S.h["qs"], qs_f)
            g.wait_ge(S.h["qg"], qg_f)
            g.wait_ge(S.h["cc"], T_BAR1)
            g.dma_start(out=dxc, in_=shxc).then_inc(S.h["qg"], 16)
            g.dma_start(out=dzz, in_=pz).then_inc(S.h["qg"], 16)
            g.dma_start(out=dxn, in_=pxnt).then_inc(S.h["qg"], 16)
            g.dma_start(out=dst_[:, 0:8], in_=istd[:]).then_inc(S.h["qg"], 16)
            g.dma_start(out=dst_[:, 8:16], in_=negm[:]).then_inc(S.h["qg"], 16)
            g.dma_start(out=dxa, in_=xa[:, 0:4 * D]).then_inc(S.h["qg"], 16)
            g.wait_ge(S.h["qg"], qg_f + 96)
        es_a.close()
        es_wi.close()
        es_mw.close()
        es.close()
        return nc

    es_a.close()
    es_wi.close()

    es_b = ExitStack()

    def sbb(name, shape, dtype):
        return es_b.enter_context(nc.sbuf_tensor(name, shape, dtype))

    xcs2 = sbb("xcs2", [P, 2 * 16 * TS], BF16)
    zss2 = sbb("zss2", [P, 8 * TS], BF16)
    dlt = sbb("dlt", [P, 8 * TS], BF16)
    dxb = sbb("dxb", [P, 8 * TS], BF16)
    dab = sbb("dab", [P, 2 * 8 * TS], BF16)
    ubf = sbb("ubf", [P, TS], BF16)
    sbf = sbb("sbf", [P, TS], BF16)
    pbf2 = sbb("pbf2", [P, TS], BF16)
    yac = sbb("yac", [P, 8 * TS], BF16)
    yfn2 = sbb("yfn2", [P, 2 * 8 * TS], BF16)
    bsb = sbb("bsb", [NST, TS], BF16)
    bbc = sbb("bbc", [P, NST * TS], BF16)
    pre = sbb("pre", [P, 2 * TS], F32)
    spa = sbb("spa", [P, TS], F32)
    spr = sbb("spr", [P, 2 * TS], BF16)
    post = sbb("post", [P, 8 * TS], BF16)
    ps_dt = [es_b.enter_context(nc.psum_tensor(f"ps_dt{i}", [P, TS], F32))
             for i in range(2)]
    ps_bt = es_b.enter_context(nc.psum_tensor("ps_bt", [NST, TS], F32))
    ps_bb = es_b.enter_context(nc.psum_tensor("ps_bb", [P, TS], F32))
    ps_wo = [es_b.enter_context(nc.psum_tensor(f"ps_wo{i}", [P, TS], F32))
             for i in range(2)]

    t_ldb = [0] * nSU
    t_ldz = [0] * nSU
    t_powq = [0] * nSU
    prev = None

    prev2 = None
    for s_ in range(nSU):
        t0 = s_ * TS
        par = s_ % 2
        xcs = xcs2[:, par * 16 * TS:(par + 1) * 16 * TS]
        zss = zss2[:, 0:8 * TS]
        yfn = yfn2[:, par * 8 * TS:(par + 1) * 8 * TS]
        yfn_p = yfn2[:, (1 - par) * 8 * TS:(2 - par) * 8 * TS]
        sp0, sa0, sv0 = S.c["sp"], S.c["sa"], S.c["sv"]

        has_wo = s_ >= 1
        SP_WX2 = sp0 + 1
        SP_WDT = [SP_WX2 + 1 + m for m in range(8)]
        SP_BB = [SP_WX2 + 9 + n for n in range(16)]
        SP_WO = [SP_WX2 + 25 + m for m in range(8)] if has_wo else []
        SA_BSB = sa0 + 1
        SA_SP = [SA_BSB + 1 + m for m in range(8)]
        SA_DA = [SA_BSB + 9 + n for n in range(16)]
        SA_WOEV = [SA_BSB + 25 + m for m in range(8)] if has_wo else []
        seq = []
        for m in range(8):
            if m >= 2:
                seq.append(("d", m - 2))
            seq.append(("p", m))
        seq += [("d", 6), ("d", 7)]
        SV_PRE, SV_DLT = {}, {}
        for i, (kind, m) in enumerate(seq):
            (SV_PRE if kind == "p" else SV_DLT)[m] = sv0 + i + 1
        SV_DX = sv0 + 17
        SV_BBEV = [SV_DX + 1 + n for n in range(16)]
        SV_SCAN = [SV_DX + 17 + n for n in range(16)]
        SV_YFN = SV_DX + 33

        @block.sync
        def _(sy, s_=s_, t0=t0, xcs=xcs, zss=zss):
            if s_ == 0:
                sy.wait_ge(S.h["qo"], QO_A)
            if s_ >= 2:
                sy.wait_ge(S.h["sv"], prev2["SV_YFN"])
                sy.wait_ge(S.h["sp"], prev2["SP_WDT7"])
            h = get_h(sy, "sync")
            xv = shxc.rearrange("d p l -> p d l")
            with sy.If_eq(h, 0):
                sy.dma_start(out=xcs[:, 0:8 * TS].rearrange(
                    "p (d t) -> p d t", d=8),
                    in_=xv[:, 0:8, t0:t0 + TS]).then_inc(S.h["qs"], 16)
            with sy.Else():
                sy.dma_start(out=xcs[:, 0:8 * TS].rearrange(
                    "p (d t) -> p d t", d=8),
                    in_=xv[:, 8:16, t0:t0 + TS]).then_inc(S.h["qs"], 16)
            S.c["qs"] += 16
            if s_ == 0:
                sy.wait_ge(S.h["cc"], T_BAR1)
            with sy.If_eq(h, 0):
                sy.dma_start(out=xcs[:, 8 * TS:16 * TS].rearrange(
                    "p (d t) -> p d t", d=8),
                    in_=xv[:, 8:16, t0:t0 + TS]).then_inc(S.h["qs"], 16)
            with sy.Else():
                sy.dma_start(out=xcs[:, 8 * TS:16 * TS].rearrange(
                    "p (d t) -> p d t", d=8),
                    in_=xv[:, 0:8, t0:t0 + TS]).then_inc(S.h["qs"], 16)
            S.c["qs"] += 16
        t_ldb[s_] = S.c["qs"]

        @block.sync
        def _(sy, s_=s_, t0=t0, zss=zss, prev=prev):
            if s_ >= 1:
                sy.wait_ge(S.h["sv"], prev["SV_YFN"])
            S.inc(sy.dma_start(out=zss[:].rearrange("p (d t) -> p d t", d=8),
                               in_=pz.rearrange("d p l -> p d l")[
                                   :, :, t0:t0 + TS]), "qs", 16)
        t_ldz[s_] = S.c["qs"]

        @block.tensor
        def _(t, s_=s_, SP_WDT=SP_WDT, SV_PRE=SV_PRE, SV_BBEV=SV_BBEV,
              SV_YFN=SV_YFN, SA_BSB=SA_BSB, SA_WOEV=SA_WOEV, SA_DA=SA_DA,
              prev=prev, xcs=xcs, yfn=yfn, yfn_p=yfn_p):
            t.wait_ge(S.h["qs"], t_ldb[s_])
            if s_ == 0:
                t.wait_ge(S.h["cc"], T_BAR1)
                t.wait_ge(S.h["qg"], QG_W)
            else:
                t.wait_ge(S.h["sa"], prev["SA_BSB"])
            for kt in range(16):
                _li = t.matmul(ps_bt[:], wx2[:, kt * NST:(kt + 1) * NST],
                               xcs[:, kt * TS:(kt + 1) * TS],
                               start=(kt == 0), stop=(kt == 15))
            S.inc(_li, "sp")
            for mt in range(8):
                if mt >= 2:
                    t.wait_ge(S.h["sv"], SV_PRE[mt - 2])
                if mt == 0 and s_ >= 1:
                    t.wait_ge(S.h["sa"], prev["SA_DA15"])
                for kt in range(16):
                    _li = t.matmul(ps_dt[mt % 2][:], wdt_t(kt, mt),
                                   xcs[:, kt * TS:(kt + 1) * TS],
                                   start=(kt == 0), stop=(kt == 15))
                S.inc(_li, "sp")
            for n in range(NST):
                if n == 0:
                    t.wait_ge(S.h["sa"], SA_BSB)
                else:
                    t.wait_ge(S.h["sv"], SV_BBEV[n - 1])
                _li = t.matmul(ps_bb[:], e16[:, n * P:(n + 1) * P], bsb[:],
                               start=True, stop=True)
                S.inc(_li, "sp")
            if s_ >= 1:
                for mt in range(8):
                    if mt == 0:
                        t.wait_ge(S.h["sv"], prev["SV_YFN"])
                        if s_ >= 2:
                            t.wait_ge(S.h["sa"], prev["SA_WOEV7"])
                    if mt >= 2:
                        t.wait_ge(S.h["sa"], SA_WOEV[mt - 2])
                    for kt in range(8):
                        _li = t.matmul(ps_wo[mt % 2][:], wout_t(kt, mt),
                                       yfn_p[:, kt * TS:(kt + 1) * TS],
                                       start=(kt == 0), stop=(kt == 7))
                    S.inc(_li, "sp")

        @block.scalar
        def _(s, s_=s_, t0=t0, SP_WX2=SP_WX2, SP_WDT=SP_WDT, SP_BB=SP_BB,
              SP_WO=SP_WO, SV_PRE=SV_PRE, SV_DLT=SV_DLT, SV_DX=SV_DX,
              SV_SCAN=SV_SCAN, SA_WOEV=SA_WOEV, prev=prev):
            s.wait_ge(S.h["sp"], SP_WX2)
            if s_ >= 1:
                s.wait_ge(S.h["sp"], prev["SP_BB15"])
            _li = s.activation(out=bsb[:], in_=ps_bt[:], func=AF.Copy)
            S.inc(_li, "sa")
            for mt in range(8):
                s.wait_ge(S.h["sv"], SV_PRE[mt])
                if mt >= 2:
                    s.wait_ge(S.h["sv"], SV_DLT[mt - 2])
                pv = pre[:, (mt % 2) * TS:(mt % 2 + 1) * TS]
                s.activation(out=spa[:], in_=pv, func=AF.Abs)
                s.activation(out=spa[:], in_=spa[:], func=AF.Exp,
                             scale=cc_(C_MONE))
                _li = s.activation(out=spr[:, (mt % 2) * TS:(mt % 2 + 1) * TS],
                                   in_=spa[:], func=AF.Ln, bias=cc_(C_ONE))
                S.inc(_li, "sa")
            for n in range(NST):
                if n == 0:
                    s.wait_ge(S.h["sv"], SV_DX)
                elif n >= 2:
                    s.wait_ge(S.h["sv"], SV_SCAN[n - 2])
                np_ = n % 2
                for dt in range(8):
                    _li = s.activation(
                        out=dab[:, (np_ * 8 + dt) * TS:(np_ * 8 + dt + 1) * TS],
                        in_=dlt[:, dt * TS:(dt + 1) * TS], func=AF.Exp,
                        scale=aneg[:, dt * NST + n:dt * NST + n + 1])
                S.inc(_li, "sa")
            if s_ >= 1:
                for mt in range(8):
                    s.wait_ge(S.h["sp"], SP_WO[mt])
                    if s_ >= 2 and mt == 0:
                        s.wait_ge(S.h["qo"], t_powq[s_ - 1])
                    _li = s.activation(out=post[:, mt * TS:(mt + 1) * TS],
                                       in_=ps_wo[mt % 2][:], func=AF.Copy)
                    S.inc(_li, "sa")
                s.wait_ge(S.h["sa"], S.c["sa"])
                h = get_h(s, "scalar")
                srcp = post[:].rearrange("p (d t) -> p d t", d=8)
                tp0 = (s_ - 1) * TS
                with s.If_eq(h, 0):
                    s.dma_start(out=shpo.rearrange("w d p l -> p w d l")[
                        :, 0, :, tp0:tp0 + TS], in_=srcp).then_inc(S.h["qo"], 16)
                with s.Else():
                    s.dma_start(out=shpo.rearrange("w d p l -> p w d l")[
                        :, 1, :, tp0:tp0 + TS], in_=srcp).then_inc(S.h["qo"], 16)
                S.c["qo"] += 16
        t_powq[s_] = S.c["qo"]

        @block.vector
        def _(v, s_=s_, seq=seq, SP_WDT=SP_WDT, SP_BB=SP_BB, SA_SP=SA_SP,
              SA_DA=SA_DA, prev=prev, xcs=xcs, zss=zss, yfn=yfn):
            for kind, m in seq:
                if kind == "p":
                    v.wait_ge(S.h["sp"], SP_WDT[m])
                    if m >= 2:
                        v.wait_ge(S.h["sa"], SA_SP[m - 2])
                    _li = v.tensor_scalar(
                        out=pre[:, (m % 2) * TS:(m % 2 + 1) * TS],
                        in0=ps_dt[m % 2][:], scalar1=bdtv[:, m:m + 1],
                        scalar2=None, op0=ALU.add)
                    S.inc(_li, "sv")
                else:
                    v.wait_ge(S.h["sa"], SA_SP[m])
                    pv = pre[:, (m % 2) * TS:(m % 2 + 1) * TS]
                    dv = dlt[:, m * TS:(m + 1) * TS]
                    v.tensor_relu(dv, pv)
                    _li = v.tensor_tensor(
                        dv, dv, spr[:, (m % 2) * TS:(m % 2 + 1) * TS], ALU.add)
                    S.inc(_li, "sv")
            for dt in range(8):
                _li = v.tensor_tensor(dxb[:, dt * TS:(dt + 1) * TS],
                                      dlt[:, dt * TS:(dt + 1) * TS],
                                      xcs[:, dt * TS:(dt + 1) * TS], ALU.mult)
            S.inc(_li, "sv")
            for n in range(NST):
                v.wait_ge(S.h["sp"], SP_BB[n])
                _li = v.tensor_copy(bbc[:, n * TS:(n + 1) * TS], ps_bb[:])
                S.inc(_li, "sv")
            for n in range(NST):
                v.wait_ge(S.h["sa"], SA_DA[n])
                np_ = n % 2
                bb_v = bbc[:, n * TS:(n + 1) * TS]
                for dt in range(8):
                    da_v = dab[:, (np_ * 8 + dt) * TS:(np_ * 8 + dt + 1) * TS]
                    y_v = yac[:, dt * TS:(dt + 1) * TS]
                    v.tensor_tensor(ubf[:], dxb[:, dt * TS:(dt + 1) * TS],
                                    bb_v, ALU.mult)
                    ini = 0.0 if s_ == 0 else carry[:, n * 8 + dt:n * 8 + dt + 1]
                    v.tensor_tensor_scan(sbf[:], da_v, ubf[:], ini,
                                         ALU.mult, ALU.add)
                    if n == 0:
                        v.tensor_tensor(y_v, sbf[:], bb_v, ALU.mult)
                    else:
                        v.tensor_tensor(pbf2[:, 0:TS], sbf[:], bb_v, ALU.mult)
                        v.tensor_tensor(y_v, y_v, pbf2[:, 0:TS], ALU.add)
                    _li = v.tensor_copy(carry[:, n * 8 + dt:n * 8 + dt + 1],
                                        sbf[:, TS - 1:TS])
                S.inc(_li, "sv")
            if s_ >= 2:
                v.wait_ge(S.h["sp"], prev["SP_WO7"])
            v.wait_ge(S.h["qs"], t_ldz[s_])
            for dt in range(8):
                v.scalar_tensor_tensor(
                    out=yfn[:, dt * TS:(dt + 1) * TS],
                    in0=xcs[:, dt * TS:(dt + 1) * TS],
                    scalar=dpv[:, dt:dt + 1],
                    in1=yac[:, dt * TS:(dt + 1) * TS],
                    op0=ALU.mult, op1=ALU.add)
                _li = v.tensor_tensor(yfn[:, dt * TS:(dt + 1) * TS],
                                      yfn[:, dt * TS:(dt + 1) * TS],
                                      zss[:, dt * TS:(dt + 1) * TS], ALU.mult)
            S.inc(_li, "sv")

        assert S.c["sp"] == (SP_WO[7] if has_wo else SP_BB[15]), S.c["sp"]
        assert S.c["sa"] == (SA_WOEV[7] if has_wo else SA_DA[15]), S.c["sa"]
        assert S.c["sv"] == SV_YFN, (S.c["sv"], SV_YFN)
        prev2 = prev
        prev = {"SV_YFN": SV_YFN, "SP_WDT7": SP_WDT[7], "SA_BSB": SA_BSB,
                "SA_DA15": SA_DA[15],
                "SA_WOEV7": SA_WOEV[7] if has_wo else
                (prev["SA_WOEV7"] if prev else 0),
                "SP_BB15": SP_BB[15],
                "SP_WO7": SP_WO[7] if has_wo else
                (prev["SP_WO7"] if prev else 0)}


    par_f = (nSU - 1) % 2
    yfn_f = yfn2[:, par_f * 8 * TS:(par_f + 1) * 8 * TS]
    SP_WOF = [S.c["sp"] + 1 + m for m in range(8)]
    SA_WOFE = [S.c["sa"] + 1 + m for m in range(8)]

    @block.tensor
    def _(t):
        for mt in range(8):
            if mt == 0:
                t.wait_ge(S.h["sv"], prev["SV_YFN"])
                t.wait_ge(S.h["sa"], prev["SA_WOEV7"])
            if mt >= 2:
                t.wait_ge(S.h["sa"], SA_WOFE[mt - 2])
            for kt in range(8):
                _li = t.matmul(ps_wo[mt % 2][:], wout_t(kt, mt),
                               yfn_f[:, kt * TS:(kt + 1) * TS],
                               start=(kt == 0), stop=(kt == 7))
            S.inc(_li, "sp")

    @block.scalar
    def _(s):
        for mt in range(8):
            s.wait_ge(S.h["sp"], SP_WOF[mt])
            if mt == 0:
                s.wait_ge(S.h["qo"], t_powq[nSU - 1])
            _li = s.activation(out=post[:, mt * TS:(mt + 1) * TS],
                               in_=ps_wo[mt % 2][:], func=AF.Copy)
            S.inc(_li, "sa")
        s.wait_ge(S.h["sa"], S.c["sa"])
        h = get_h(s, "scalar")
        srcp = post[:].rearrange("p (d t) -> p d t", d=8)
        tf0 = (nSU - 1) * TS
        with s.If_eq(h, 0):
            s.dma_start(out=shpo.rearrange("w d p l -> p w d l")[
                :, 0, :, tf0:tf0 + TS], in_=srcp).then_inc(S.h["qo"], 16)
        with s.Else():
            s.dma_start(out=shpo.rearrange("w d p l -> p w d l")[
                :, 1, :, tf0:tf0 + TS], in_=srcp).then_inc(S.h["qo"], 16)
        S.c["qo"] += 16
    assert S.c["sp"] == SP_WOF[7]
    assert S.c["sa"] == SA_WOFE[7]

    QO_B = S.c["qo"]

    @block.gpsimd
    def _(g):
        g.wait_ge(S.h["qo"], QO_B)
        S.inc(g.collective_compute(
            "AllReduce", ALU.add, replica_groups=PAIRS,
            ins=[bi2], outs=[bo2]), "cc", 1)
    T_BAR2 = S.c["cc"]

    if stop_after == "B":
        dpo = nc.dram_tensor("dbg_po", [2, 8, P, L], F32,
                             kind="ExternalOutput").ap()
        ddl = nc.dram_tensor("dbg_dlt", [P, 8 * TS], F32,
                             kind="ExternalOutput").ap()
        dbs = nc.dram_tensor("dbg_bsb", [NST, TS], F32,
                             kind="ExternalOutput").ap()
        dbb = nc.dram_tensor("dbg_bbc", [P, NST * TS], F32,
                             kind="ExternalOutput").ap()
        dya = nc.dram_tensor("dbg_yac", [P, 8 * TS], F32,
                             kind="ExternalOutput").ap()
        dyf = nc.dram_tensor("dbg_yfn", [P, 8 * TS], F32,
                             kind="ExternalOutput").ap()
        qo_f, qs_f, qg_f = S.c["qo"], S.c["qs"], S.c["qg"]

        @block.gpsimd
        def _(g):
            g.wait_ge(S.h["qo"], qo_f)
            g.wait_ge(S.h["qs"], qs_f)
            g.wait_ge(S.h["qg"], qg_f)
            g.wait_ge(S.h["cc"], T_BAR2)
            g.dma_start(out=dpo, in_=shpo).then_inc(S.h["qg"], 16)
            g.dma_start(out=ddl, in_=dlt[:]).then_inc(S.h["qg"], 16)
            g.dma_start(out=dbs, in_=bsb[:]).then_inc(S.h["qg"], 16)
            g.dma_start(out=dbb, in_=bbc[:]).then_inc(S.h["qg"], 16)
            g.dma_start(out=dya, in_=yac[:]).then_inc(S.h["qg"], 16)
            g.dma_start(out=dyf, in_=yfn[:]).then_inc(S.h["qg"], 16)
            g.wait_ge(S.h["qg"], qg_f + 96)
        es_b.close()
        es_mw.close()
        es.close()
        return nc

    es_b.close()
    es_mw.close()

    if False:
        pass

    es_c = ExitStack()

    def sbc(name, shape, dtype):
        return es_c.enter_context(nc.sbuf_tensor(name, shape, dtype))

    xnh = sbc("xnh", [P, 8 * TH], BF16)
    racc = sbc("racc", [P, 8 * TH], F32)
    cft = sbc("cft", [NE + 1, TH], BF16)
    lgb = sbc("lgb", [NE, 512], F32)
    spb = sbc("spb", [NE, 512], F32)
    acb = sbc("acb", [NE, 512], F32)
    sptok = sbc("sptok", [P, 4 * NE], F32)
    actok = sbc("actok", [P, 4 * NE], F32)
    cfe = sbc("cfe", [P, 4 * (NE + 1)], F32)
    m1b = sbc("m1b", [P, 3], F32)
    rwT = sbc("rwT", [P, 8 * NE], BF16)
    rb2x = sbc("rb2x", [NE + 1, D], BF16)
    w1e = sbc("w1e", [P, 2 * 8 * HF], BF16)
    w2e = sbc("w2e", [P, 2 * 4 * D], BF16)
    hbuf = sbc("hbuf", [P, 4 * 512], BF16)
    cfbb = sbc("cfbb", [P, 512], BF16)
    rtmp = sbc("rtmp", [P, 2 * 512], BF16)
    pobuf = sbc("pobuf", [P, 2 * 8 * 512], BF16)
    xrl = sbc("xrl", [P, 4 * D], F32)
    otok = sbc("otok", [P, 2 * D], F32)

    @block.gpsimd
    def _(g):
        g.wait_ge(S.h["cc"], T_BAR2)
        S.inc(g.dma_start(out=rwT[:].rearrange("p (k e) -> p k e", k=8),
                          in_=drw.rearrange("(k p) e -> p k e", p=P)), "qg", 16)
        S.inc(g.dma_start(out=rb2x[:], in_=db2), "qg", 16)
    QG_C0 = S.c["qg"]

    @block.sync
    def _(sy):
        sy.wait_ge(S.h["cc"], T_BAR2)
        h = get_h(sy, "sync")
        with sy.If_eq(h, 0):
            sy.dma_start(out=xnh[:].rearrange("p (d t) -> p d t", d=8),
                         in_=pxnt.rearrange("d p l -> p d l")[
                             :, :, 0:TH]).then_inc(S.h["qs"], 16)
        with sy.Else():
            sy.dma_start(out=xnh[:].rearrange("p (d t) -> p d t", d=8),
                         in_=pxnt.rearrange("d p l -> p d l")[
                             :, :, TH:L]).then_inc(S.h["qs"], 16)
        S.c["qs"] += 16
    T_XNH = S.c["qs"]

    es_c1 = ExitStack()
    ps_lg = es_c1.enter_context(nc.psum_tensor("ps_lg", [NE, 512], F32))
    ps_tp = es_c1.enter_context(nc.psum_tensor("ps_tp", [P, 64], F32))
    ps_cf = es_c1.enter_context(nc.psum_tensor("ps_cf", [NE + 1, 512], F32))

    t_rt_done = 0
    t_top2 = [0] * nTB
    for tb in range(nTB):
        @block.tensor
        def _(t, tb=tb):
            t.wait_ge(S.h["qs"], T_XNH)
            if tb == 0:
                t.wait_ge(S.h["qg"], QG_C0)
            else:
                t.wait_ge(S.h["sv"], t_top2[tb - 1])
            for kt in range(8):
                _li = t.matmul(ps_lg[:], rwT[:, kt * NE:(kt + 1) * NE],
                         xnh[:, kt * TH + tb * 512:kt * TH + (tb + 1) * 512],
                         start=(kt == 0), stop=(kt == 7))
            S.inc(_li, "sp")
        t_lgm = S.c["sp"]

        @block.vector
        def _(v, t_lgm=t_lgm):
            v.wait_ge(S.h["sp"], t_lgm)
            _li = v.tensor_scalar(out=lgb[:], in0=ps_lg[:], scalar1=ebv[:, 0:1],
                            scalar2=None, op0=ALU.add)
            S.inc(_li, "sv")
        t_lg = S.c["sv"]

        @block.scalar
        def _(s, t_lg=t_lg):
            s.wait_ge(S.h["sv"], t_lg)
            s.activation(out=spb[:], in_=lgb[:], func=AF.Abs)
            s.activation(out=spb[:], in_=spb[:], func=AF.Exp,
                         scale=cst[0:NE, C_MONE:C_MONE + 1])
            _li = s.activation(out=spb[:], in_=spb[:], func=AF.Ln,
                         bias=cst[0:NE, C_ONE:C_ONE + 1])
            S.inc(_li, "sa")
        t_sp1 = S.c["sa"]

        @block.vector
        def _(v, t_sp1=t_sp1):
            v.wait_ge(S.h["sa"], t_sp1)
            v.tensor_relu(acb[:], lgb[:])
            v.tensor_tensor(spb[:], spb[:], acb[:], ALU.add)
            _li = v.tensor_scalar_max(spb[:], spb[:], cst[0:NE, C_TINY:C_TINY + 1])
            S.inc(_li, "sv")
        t_sp2 = S.c["sv"]

        @block.scalar
        def _(s, t_sp2=t_sp2):
            s.wait_ge(S.h["sv"], t_sp2)
            s.activation(out=acb[:], in_=spb[:], func=AF.Ln)
            _li = s.activation(out=acb[:], in_=acb[:], func=AF.Exp,
                         scale=cst[0:NE, C_HALF:C_HALF + 1])
            S.inc(_li, "sa")
        t_act = S.c["sa"]

        @block.tensor
        def _(t, t_act=t_act, tb=tb):
            t.wait_ge(S.h["sa"], t_act)
            if tb > 0:
                t.wait_ge(S.h["sa"], S.c["sa"])
            for tt in range(4):
                t.transpose(ps_tp[:, tt * 8:(tt + 1) * 8],
                            spb[:, tt * P:(tt + 1) * P], idf[0:NE, 0:NE])
                _li = t.transpose(ps_tp[:, 32 + tt * 8:32 + (tt + 1) * 8],
                            acb[:, tt * P:(tt + 1) * P], idf[0:NE, 0:NE])
            S.inc(_li, "sp")
        t_tpm = S.c["sp"]

        @block.scalar
        def _(s, t_tpm=t_tpm):
            s.wait_ge(S.h["sp"], t_tpm)
            s.activation(out=sptok[:], in_=ps_tp[:, 0:32], func=AF.Copy)
            _li = s.activation(out=actok[:], in_=ps_tp[:, 32:64], func=AF.Copy)
            S.inc(_li, "sa")
        t_tpe = S.c["sa"]

        @block.vector
        def _(v, t_tpe=t_tpe):
            v.wait_ge(S.h["sa"], t_tpe)
            def rtw(inst):
                S.inc(inst, "sv")
                v.wait_ge(S.h["sv"], S.c["sv"])
            for tt in range(4):
                sp_t = sptok[:, tt * NE:(tt + 1) * NE]
                ac_t = actok[:, tt * NE:(tt + 1) * NE]
                cf_t = cfe[:, tt * (NE + 1):(tt + 1) * (NE + 1)]
                rtw(v.reduce_max(m1b[:, 0:1], sp_t, axis=mybir.AxisListType.X))
                rtw(v.tensor_scalar(out=cf_t[:, 0:NE], in0=sp_t,
                                    scalar1=m1b[:, 0:1], scalar2=None,
                                    op0=ALU.is_equal))
                rtw(v.tensor_scalar_mul(cf_t[:, 0:NE], cf_t[:, 0:NE], 1e30))
                rtw(v.tensor_tensor(cf_t[:, 0:NE], sp_t, cf_t[:, 0:NE],
                                    ALU.subtract))
                rtw(v.reduce_max(m1b[:, 1:2], cf_t[:, 0:NE],
                                 axis=mybir.AxisListType.X))
                rtw(v.tensor_scalar(out=cf_t[:, 0:NE], in0=sp_t,
                                    scalar1=m1b[:, 1:2], scalar2=None,
                                    op0=ALU.is_ge))
                v.tensor_tensor(cf_t[:, 0:NE], cf_t[:, 0:NE], ac_t, ALU.mult)
                _li = v.memset(cf_t[:, NE:NE + 1], 1.0)
            S.inc(_li, "sv")
        t_top2[tb] = S.c["sv"]

        @block.tensor
        def _(t, tb=tb):
            t.wait_ge(S.h["sv"], t_top2[tb])
            if tb > 0:
                t.wait_ge(S.h["sa"], S.c["sa"])
            for tt in range(4):
                _li = t.transpose(ps_cf[:, tt * P:(tt + 1) * P],
                            cfe[:, tt * (NE + 1):(tt + 1) * (NE + 1)], idf[:])
            S.inc(_li, "sp")
        t_cfm = S.c["sp"]

        @block.scalar
        def _(s, t_cfm=t_cfm, tb=tb):
            s.wait_ge(S.h["sp"], t_cfm)
            _li = s.activation(out=cft[:, tb * 512:(tb + 1) * 512], in_=ps_cf[:],
                         func=AF.Copy)
            S.inc(_li, "sa")
        t_rt_done = S.c["sa"]

    es_c1.close()

    es_c2 = ExitStack()
    ps_h = [es_c2.enter_context(nc.psum_tensor(f"ps_h{i}", [P, 512], F32))
            for i in range(2)]
    ps_rt = [es_c2.enter_context(nc.psum_tensor(f"ps_rt{i}", [P, 512], F32))
             for i in range(2)]
    ps_cb = es_c2.enter_context(nc.psum_tensor("ps_cb", [P, 512], F32))

    e_order = [8] + list(range(8))
    sp0e, sa0e, sv0e = S.c["sp"], S.c["sa"], S.c["sv"]
    SP_MM1 = {}
    SP_CFB = {}
    SP_MM2 = {}
    SA_M1E = {}
    SV_CFB = {}
    SV_RT = {}
    spv, sav, svv = sp0e, sa0e, sv0e
    for ei, e in enumerate(e_order):
        for tb in range(nTB):
            for mt1 in range(4):
                spv += 1; SP_MM1[(ei, tb, mt1)] = spv
                sav += 1; SA_M1E[(ei, tb, mt1)] = sav
            if e != 8:
                spv += 1; SP_CFB[(ei, tb)] = spv
                svv += 1; SV_CFB[(ei, tb)] = svv
            for mt in range(8):
                spv += 1; SP_MM2[(ei, tb, mt)] = spv
                svv += 1; SV_RT[(ei, tb, mt)] = svv
    SP_E_END, SA_E_END, SV_E_END = spv, sav, svv

    def prev_mm2_last(ei, tb):
        if tb > 0:
            return SP_MM2[(ei, tb - 1, 7)]
        if ei > 0:
            return SP_MM2[(ei - 1, nTB - 1, 7)]
        return None

    t_wld = [0] * 9

    @block.gpsimd
    def _(g):
        for ei, e in enumerate(e_order):
            ep = ei % 2
            if ei >= 2:
                g.wait_ge(S.h["sp"], SP_MM2[(ei - 2, nTB - 1, 7)])
            S.inc(g.dma_start(
                out=w1e[:, ep * 8 * HF:(ep + 1) * 8 * HF].rearrange(
                    "p (k c) -> p k c", k=8),
                in_=dw1[e].rearrange("(k p) c -> p k c", p=P)), "qg", 16)
            S.inc(g.dma_start(
                out=w2e[:, ep * 4 * D:(ep + 1) * 4 * D].rearrange(
                    "p (k c) -> p k c", k=4),
                in_=dw2[e].rearrange("(k p) c -> p k c", p=P)), "qg", 16)
            t_wld[ei] = S.c["qg"]

    @block.tensor
    def _(t):
        t.wait_ge(S.h["sa"], t_rt_done)
        for ei, e in enumerate(e_order):
            ep = ei % 2
            for tb in range(nTB):
                if tb == 0:
                    t.wait_ge(S.h["qg"], t_wld[ei])
                for mt1 in range(4):
                    g_prev = (ei * nTB + tb) * 4 + mt1 - 2
                    if g_prev >= 0:
                        pei, r = divmod(g_prev, nTB * 4)
                        ptb, pmt1 = divmod(r, 4)
                        t.wait_ge(S.h["sa"], SA_M1E[(pei, ptb, pmt1)])
                    for kt in range(8):
                        _li = t.matmul(
                            ps_h[mt1 % 2][:],
                            w1e[:, ep * 8 * HF + kt * HF + mt1 * P:
                                ep * 8 * HF + kt * HF + (mt1 + 1) * P],
                            xnh[:, kt * TH + tb * 512:kt * TH + (tb + 1) * 512],
                            start=(kt == 0), stop=(kt == 7))
                    S.inc(_li, "sp")
                if e != 8:
                    keys = sorted(SV_CFB)
                    my_i = keys.index((ei, tb))
                    if my_i > 0:
                        t.wait_ge(S.h["sv"], SV_CFB[keys[my_i - 1]])
                    _li = t.matmul(ps_cb[:], e9[:, e * P:(e + 1) * P],
                                   cft[:, tb * 512:(tb + 1) * 512],
                                   start=True, stop=True)
                    S.inc(_li, "sp")
                for mt in range(8):
                    if mt == 0:
                        if e != 8:
                            t.wait_ge(S.h["sv"], SV_CFB[(ei, tb)])
                        else:
                            t.wait_ge(S.h["sa"], SA_M1E[(ei, tb, 3)])
                    g2 = (ei * nTB + tb) * 8 + mt - 2
                    if g2 >= 0:
                        pei, r = divmod(g2, nTB * 8)
                        ptb, pmt = divmod(r, 8)
                        t.wait_ge(S.h["sv"], SV_RT[(pei, ptb, pmt)])
                    for kt in range(4):
                        _li = t.matmul(
                            ps_rt[mt % 2][:],
                            w2e[:, ep * 4 * D + kt * D + mt * P:
                                ep * 4 * D + kt * D + (mt + 1) * P],
                            hbuf[:, kt * 512:(kt + 1) * 512],
                            start=(kt == 0), stop=(kt == 3 and e != 8))
                    if e == 8:
                        _li = t.matmul(ps_rt[mt % 2][:],
                                       rb2x[:, mt * P:(mt + 1) * P],
                                       cft[:, tb * 512:(tb + 1) * 512],
                                       start=False, stop=True)
                    S.inc(_li, "sp")
    assert S.c["sp"] == SP_E_END, (S.c["sp"], SP_E_END)

    @block.scalar
    def _(s):
        for ei, e in enumerate(e_order):
            for tb in range(nTB):
                pm = prev_mm2_last(ei, tb)
                for mt1 in range(4):
                    s.wait_ge(S.h["sp"], SP_MM1[(ei, tb, mt1)])
                    if pm is not None and mt1 == 0:
                        s.wait_ge(S.h["sp"], pm)
                    _li = s.activation(out=hbuf[:, mt1 * 512:(mt1 + 1) * 512],
                                       in_=ps_h[mt1 % 2][:], func=AF.Silu,
                                       bias=b1v[:, e * 4 + mt1:e * 4 + mt1 + 1])
                    S.inc(_li, "sa")
    assert S.c["sa"] == SA_E_END, (S.c["sa"], SA_E_END)

    @block.vector
    def _(v):
        for ei, e in enumerate(e_order):
            for tb in range(nTB):
                if e != 8:
                    v.wait_ge(S.h["sp"], SP_CFB[(ei, tb)])
                    v.wait_ge(S.h["sa"], SA_M1E[(ei, tb, 3)])
                    v.tensor_copy(cfbb[:], ps_cb[:])
                    for mt1 in range(4):
                        _li = v.tensor_tensor(
                            hbuf[:, mt1 * 512:(mt1 + 1) * 512],
                            hbuf[:, mt1 * 512:(mt1 + 1) * 512],
                            cfbb[:], ALU.mult)
                    S.inc(_li, "sv")
                for mt in range(8):
                    v.wait_ge(S.h["sp"], SP_MM2[(ei, tb, mt)])
                    dst = racc[:, mt * TH + tb * 512:mt * TH + (tb + 1) * 512]
                    if e == 8:
                        _li = v.tensor_copy(dst, ps_rt[mt % 2][:])
                    else:
                        _li = v.tensor_tensor(dst, dst, ps_rt[mt % 2][:],
                                              ALU.add)
                    S.inc(_li, "sv")
    assert S.c["sv"] == SV_E_END, (S.c["sv"], SV_E_END)

    T_EXP_DVE = SV_E_END

    es_c2.close()

    es_c3 = ExitStack()
    ps_ot = [es_c3.enter_context(nc.psum_tensor(f"ps_ot{i}", [P, D], F32))
             for i in range(2)]

    t_po_qs = [0] * nTB
    t_po_qg = [0] * nTB
    t_xr_qs = [0] * nTB
    t_xr_qo = [0] * nTB
    t_accs = [0] * nTB
    t_fin, t_finev, t_oadd, t_odma = [], [], [], []

    def _po_dma(eng, qn, w, tb, h):
        for hh in range(2):
            ctx = eng.If_eq(h, 0) if hh == 0 else eng.Else()
            with ctx:
                eng.dma_start(
                    out=pobuf[:, w * 8 * 512:(w + 1) * 8 * 512].rearrange(
                        "p (d t) -> p d t", d=8),
                    in_=shpo.rearrange("w d p l -> p w d l")[
                        :, w, :, hh * TH + tb * 512:hh * TH + (tb + 1) * 512]
                ).then_inc(S.h[qn], 16)
        S.c[qn] += 16

    def _xr_dma(eng, qn, i, tb, h):
        for hh in range(2):
            ctx = eng.If_eq(h, 0) if hh == 0 else eng.Else()
            with ctx:
                eng.dma_start(out=xrl[:, i * D:(i + 1) * D],
                              in_=dxin[hh * TH + tb * 512 + i * 128:
                                       hh * TH + tb * 512 + (i + 1) * 128, :]
                              ).then_inc(S.h[qn], 16)
        S.c[qn] += 16

    def emit_loads(tb):
        if tb == 0:
            @block.sync
            def _(sy):
                h = get_h(sy, "sync")
                for w in range(2):
                    _po_dma(sy, "qs", w, 0, h)
                for i in range(4):
                    _xr_dma(sy, "qs", i, 0, h)
            t_po_qg[0] = S.c["qg"]
            t_xr_qo[0] = S.c["qo"]
        else:
            @block.sync
            def _(sy, tb=tb):
                sy.wait_ge(S.h["sv"], t_accs[tb - 1])
                h = get_h(sy, "sync")
                _po_dma(sy, "qs", 0, tb, h)
            t_po_qs[tb] = S.c["qs"]

            @block.sync
            def _(sy, tb=tb):
                sy.wait_ge(S.h["sg"], t_oadd[(tb - 1) * 4 + 3])
                h = get_h(sy, "sync")
                for i in range(2):
                    _xr_dma(sy, "qs", i, tb, h)

            @block.gpsimd
            def _(g, tb=tb):
                g.wait_ge(S.h["sv"], t_accs[tb - 1])
                h = get_h(g, "gpsimd")
                _po_dma(g, "qg", 1, tb, h)
            t_po_qg[tb] = S.c["qg"]

            @block.scalar
            def _(s, tb=tb):
                s.wait_ge(S.h["sg"], t_oadd[(tb - 1) * 4 + 3])
                h = get_h(s, "scalar")
                for i in range(2, 4):
                    _xr_dma(s, "qo", i, tb, h)
            t_xr_qo[tb] = S.c["qo"]
        if tb == 0:
            t_po_qs[0] = S.c["qs"]
        t_xr_qs[tb] = S.c["qs"]

    emit_loads(0)
    for tb in range(nTB):
        @block.vector
        def _(v, tb=tb):
            v.wait_ge(S.h["qs"], t_po_qs[tb])
            if tb >= 1:
                v.wait_ge(S.h["qg"], t_po_qg[tb])
            for mt in range(8):
                dst = racc[:, mt * TH + tb * 512:mt * TH + (tb + 1) * 512]
                for w in range(2):
                    _li = v.tensor_tensor(
                        dst, dst,
                        pobuf[:, (w * 8 + mt) * 512:(w * 8 + mt + 1) * 512],
                        ALU.add)
            S.inc(_li, "sv")
        t_accs[tb] = S.c["sv"]

        for tt in range(4):
            fi = len(t_fin)

            @block.tensor
            def _(t, tb=tb, tt=tt, fi=fi):
                t.wait_ge(S.h["sv"], t_accs[tb])
                if fi >= 2:
                    t.wait_ge(S.h["sa"], t_finev[fi - 2])
                for mt in range(8):
                    _li = t.transpose(
                        ps_ot[fi % 2][:, mt * P:(mt + 1) * P],
                        racc[:, mt * TH + tb * 512 + tt * P:
                             mt * TH + tb * 512 + (tt + 1) * P], idf[:])
                S.inc(_li, "sp")
            t_fin.append(S.c["sp"])

            @block.scalar
            def _(s, fi=fi):
                s.wait_ge(S.h["sp"], t_fin[fi])
                if fi >= 2:
                    s.wait_ge(S.h["sg"], t_oadd[fi - 2])
                    s.wait_ge(S.h["qo"], t_odma[fi - 2])
                _li = s.activation(out=otok[:, (fi % 2) * D:((fi % 2) + 1) * D],
                             in_=ps_ot[fi % 2][:], func=AF.Copy)
                S.inc(_li, "sa")
            t_finev.append(S.c["sa"])

            @block.gpsimd
            def _(g, tb=tb, tt=tt, fi=fi):
                g.wait_ge(S.h["sa"], t_finev[fi])
                g.wait_ge(S.h["qs"], t_xr_qs[tb])
                if tt >= 2:
                    g.wait_ge(S.h["qo"], t_xr_qo[tb])
                _li = g.tensor_tensor(otok[:, (fi % 2) * D:((fi % 2) + 1) * D],
                                otok[:, (fi % 2) * D:((fi % 2) + 1) * D],
                                xrl[:, tt * D:(tt + 1) * D], ALU.add)
                S.inc(_li, "sg")
            t_oadd.append(S.c["sg"])

            @block.scalar
            def _(s, tb=tb, tt=tt, fi=fi):
                s.wait_ge(S.h["sg"], t_oadd[fi])
                S.inc(s.dma_start(
                    out=dout[tb * 512 + tt * P:tb * 512 + (tt + 1) * P, :],
                    in_=otok[:, (fi % 2) * D:((fi % 2) + 1) * D]), "qo", 16)
            t_odma.append(S.c["qo"])
        if tb + 1 < nTB:
            emit_loads(tb + 1)

    QO_END = S.c["qo"]
    QS_END = S.c["qs"]
    QG_END = S.c["qg"]

    @block.gpsimd
    def _(g):
        g.wait_ge(S.h["qo"], QO_END)
        g.wait_ge(S.h["qs"], QS_END)
        g.wait_ge(S.h["qg"], QG_END)

    es_c3.close()
    es_c.close()
    es.close()
    return nc


_NC_CACHE = {}


def _get_nc(L):
    if L not in _NC_CACHE:
        _NC_CACHE[L] = build_program(L)
    return _NC_CACHE[L]


def make_in_maps(inputs, L):
    import ml_dtypes
    x = np.asarray(inputs["L0_attn_x"], np.float32)
    w_in = np.asarray(inputs["L0_attn_W_in"], np.float32)
    conv_w = np.asarray(inputs["L0_attn_conv_w"], np.float32)
    conv_b = np.asarray(inputs["L0_attn_conv_b"], np.float32)
    w_xp = np.asarray(inputs["L0_attn_W_xproj"], np.float32)
    w_dt = np.asarray(inputs["L0_attn_W_dt"], np.float32)
    b_dt = np.asarray(inputs["L0_attn_b_dt"], np.float32)
    a_log = np.asarray(inputs["L0_attn_A_log"], np.float32)
    d_p = np.asarray(inputs["L0_attn_Dp"], np.float32)
    w_out = np.asarray(inputs["L0_attn_W_out"], np.float32)
    r_w = np.asarray(inputs["L0_ffn_routing_W"], np.float32)
    e_b = np.asarray(inputs["L0_ffn_expert_bias"], np.float32)
    sW1 = np.asarray(inputs["L0_ffn_sW1"], np.float32)
    sb1 = np.asarray(inputs["L0_ffn_sb1"], np.float32)
    sW2 = np.asarray(inputs["L0_ffn_sW2"], np.float32)
    sb2 = np.asarray(inputs["L0_ffn_sb2"], np.float32)
    rW1 = np.asarray(inputs["L0_ffn_rW1"], np.float32)
    rb1 = np.asarray(inputs["L0_ffn_rb1"], np.float32)
    rW2 = np.asarray(inputs["L0_ffn_rW2"], np.float32)
    rb2 = np.asarray(inputs["L0_ffn_rb2"], np.float32)

    w1 = np.ascontiguousarray(np.concatenate([rW1, sW1[None]], 0))
    b1 = np.ascontiguousarray(np.concatenate([rb1, sb1[None]], 0))
    w2 = np.ascontiguousarray(np.concatenate([rW2, sW2[None]], 0))
    b2 = np.ascontiguousarray(np.concatenate([rb2, sb2[None]], 0))
    identf = np.eye(P, dtype=np.float32)
    identb = np.eye(P, dtype=np.float32).astype(ml_dtypes.bfloat16)
    e16m = np.zeros((NST, NST * P), np.float32)
    for n in range(NST):
        e16m[n, n * P:(n + 1) * P] = 1.0
    e16m = e16m.astype(ml_dtypes.bfloat16)
    e9m = np.zeros((NE + 1, (NE + 1) * P), np.float32)
    for n in range(NE + 1):
        e9m[n, n * P:(n + 1) * P] = 1.0
    e9m = e9m.astype(ml_dtypes.bfloat16)

    c = np.ascontiguousarray
    in_maps = []
    for core in range(8):
        b, h = core // 2, core % 2
        own = slice(h * HD, (h + 1) * HD)
        pair = slice((1 - h) * HD, (2 - h) * HD)
        in_maps.append({
            "x": c(x[b, :L]),
            "w_in": c(np.concatenate(
                [w_in[:, own], w_in[:, DI + h * HD:DI + (h + 1) * HD]], 1)),
            "conv_w": c(conv_w[own, 0, :]),
            "conv_b": c(conv_b[own].reshape(HD, 1)),
            "w_x2": c(np.concatenate(
                [w_xp[own, NST:2 * NST], w_xp[pair, NST:2 * NST]], 0)),
            "w_dt": c(np.concatenate([w_dt[own, own], w_dt[pair, own]], 0)),
            "b_dt": c(b_dt[own].reshape(HD, 1)),
            "a_log": c(a_log[own]),
            "d_p": c(d_p[own].reshape(HD, 1)),
            "w_out": c(w_out[own, :]),
            "r_w": c(r_w.T),
            "e_b": c(e_b.reshape(NE, 1)),
            "w1": w1, "b1": b1, "w2": w2, "b2": b2,
            "identf": identf, "identb": identb, "e16": e16m, "e9": e9m,
        })
    return in_maps


def kernel(**inputs):
    L = 4096
    in_maps = make_in_maps(inputs, L)
    nc = _get_nc(L)
    res = run_bass_kernel_spmd(nc, in_maps, core_ids=list(range(8)))
    out = np.empty((4, L, D), np.float32)
    for core in range(8):
        b, h = core // 2, core % 2
        out[b, h * (L // 2):(h + 1) * (L // 2), :] = res.results[core]["out"]
    return out


# version 12
# speedup vs baseline: 1.1927x; 1.0115x over previous
import numpy as np
from contextlib import ExitStack

import concourse.bass as bass
import concourse.mybir as mybir
from concourse.bass_utils import run_bass_kernel_spmd

F32 = mybir.dt.float32
BF16 = mybir.dt.bfloat16
AF = mybir.ActivationFunctionType
ALU = mybir.AluOpType

P = 128
D = 1024
DI = 2048
HD = 1024
NST = 16
NE = 8
HF = 512
TS = 512
PAIRS = [[0, 1], [2, 3], [4, 5], [6, 7]]


class Sems:
    def __init__(self, es, nc, names):
        self.h = {n: es.enter_context(nc.semaphore(n)) for n in names}
        self.c = {n: 0 for n in names}

    def inc(self, inst, name, k=1):
        inst.then_inc(self.h[name], k)
        self.c[name] += k
        return self.c[name]


def build_program(L, stop_after=None):
    nCH = L // 512
    nSU = L // TS
    TH = L // 2
    nTB = TH // 512

    nc = bass.Bass("TRN2", target_bir_lowering=False, num_devices=8,
                   detect_race_conditions=False)

    dxin = nc.dram_tensor("x", [L, D], F32, kind="ExternalInput").ap()
    dwin = nc.dram_tensor("w_in", [D, DI], F32, kind="ExternalInput").ap()
    dcw = nc.dram_tensor("conv_w", [HD, 4], F32, kind="ExternalInput").ap()
    dcb = nc.dram_tensor("conv_b", [HD, 1], F32, kind="ExternalInput").ap()
    dwx2 = nc.dram_tensor("w_x2", [DI, NST], F32, kind="ExternalInput").ap()
    dwdt = nc.dram_tensor("w_dt", [DI, HD], F32, kind="ExternalInput").ap()
    dbdt = nc.dram_tensor("b_dt", [HD, 1], F32, kind="ExternalInput").ap()
    dalog = nc.dram_tensor("a_log", [HD, NST], F32, kind="ExternalInput").ap()
    ddp = nc.dram_tensor("d_p", [HD, 1], F32, kind="ExternalInput").ap()
    dwout = nc.dram_tensor("w_out", [HD, D], F32, kind="ExternalInput").ap()
    drw = nc.dram_tensor("r_w", [D, NE], F32, kind="ExternalInput").ap()
    deb = nc.dram_tensor("e_b", [NE, 1], F32, kind="ExternalInput").ap()
    dw1 = nc.dram_tensor("w1", [NE + 1, D, HF], F32, kind="ExternalInput").ap()
    db1 = nc.dram_tensor("b1", [NE + 1, HF], F32, kind="ExternalInput").ap()
    dw2 = nc.dram_tensor("w2", [NE + 1, HF, D], F32, kind="ExternalInput").ap()
    db2 = nc.dram_tensor("b2", [NE + 1, D], F32, kind="ExternalInput").ap()
    didf = nc.dram_tensor("identf", [P, P], F32, kind="ExternalInput").ap()
    didb = nc.dram_tensor("identb", [P, P], BF16, kind="ExternalInput").ap()
    de16 = nc.dram_tensor("e16", [NST, NST * P], BF16, kind="ExternalInput").ap()
    de9 = nc.dram_tensor("e9", [NE + 1, (NE + 1) * P], BF16,
                         kind="ExternalInput").ap()
    dout = nc.dram_tensor("out", [TH, D], F32, kind="ExternalOutput").ap()

    shxc = nc.dram_tensor("sh_xc", [16, P, L], BF16, kind="Internal",
                          addr_space="Shared").ap()
    shpo = nc.dram_tensor("sh_po", [2, 8, P, L], BF16, kind="Internal",
                          addr_space="Shared").ap()
    pz = nc.dram_tensor("p_z", [8, P, L], BF16, kind="Internal").ap()
    pxnt = nc.dram_tensor("p_xnt", [8, P, L], BF16, kind="Internal").ap()
    bi1 = nc.dram_tensor("bar_i1", [1, 16], F32, kind="Internal").ap()
    bo1 = nc.dram_tensor("bar_o1", [1, 16], F32, kind="Internal").ap()
    bi2 = nc.dram_tensor("bar_i2", [1, 16], F32, kind="Internal").ap()
    bo2 = nc.dram_tensor("bar_o2", [1, 16], F32, kind="Internal").ap()

    es = ExitStack()

    def sb(name, shape, dtype):
        return es.enter_context(nc.sbuf_tensor(name, shape, dtype))

    S = Sems(es, nc, ["qs", "qg", "qo", "sa", "sv", "sp", "sg", "cc"])
    block = es.enter_context(nc.Block())
    hreg = {}

    def get_h(eng, ename):
        if ename not in hreg:
            hreg[ename] = eng.to_reg(eng.partition_id() % 2)
        return hreg[ename]

    wx2 = sb("wx2_s", [P, 16 * NST], BF16)
    cwv = sb("cwv", [P, 8 * 4], F32)
    cbv = sb("cbv", [P, 8], F32)
    bdtv = sb("bdtv", [P, 8], F32)
    dpv = sb("dpv", [P, 8], F32)
    alog = sb("alog", [P, 8 * NST], F32)
    aneg = sb("aneg", [P, 8 * NST], F32)
    idf = sb("idf", [P, P], F32)
    idb = sb("idb", [P, P], BF16)
    e16 = sb("e16s", [NST, NST * P], BF16)
    e9 = sb("e9s", [NE + 1, (NE + 1) * P], BF16)
    cst = sb("cst", [P, 8], F32)
    zb = sb("zb", [1, 16], F32)
    C_EPS, C_MH, C_ONE, C_MONE, C_HALF, C_TINY = range(6)
    carry = sb("carry", [P, NST * 8], F32)
    ebv = sb("ebv", [NE, 1], F32)
    b1v = sb("b1v", [P, (NE + 1) * 4], F32)

    def cc_(i):
        return cst[:, i:i + 1]

    es_mw = ExitStack()
    es_wi = ExitStack()
    wdt = es_mw.enter_context(nc.sbuf_tensor("wdt_s", [P, 16 * HD], BF16))
    wout = es_mw.enter_context(nc.sbuf_tensor("wout_s", [P, 8 * D], BF16))
    win = es_wi.enter_context(nc.sbuf_tensor("win_s", [P, 8 * DI], BF16))

    def win_t(kt, mt):
        return win[:, kt * DI + mt * P:kt * DI + (mt + 1) * P]

    def wdt_t(kt, mt):
        return wdt[:, kt * HD + mt * P:kt * HD + (mt + 1) * P]

    def wout_t(kt, mt):
        return wout[:, kt * D + mt * P:kt * D + (mt + 1) * P]

    @block.gpsimd
    def _(g):
        S.inc(g.dma_start(out=win[:].rearrange("p (k c) -> p k c", k=8),
                          in_=dwin.rearrange("(k p) c -> p k c", p=P)), "qg", 16)
        S.inc(g.dma_start(out=wdt[:].rearrange("p (k c) -> p k c", k=16),
                          in_=dwdt.rearrange("(k p) c -> p k c", p=P)), "qg", 16)
        S.inc(g.dma_start(out=wout[:].rearrange("p (k c) -> p k c", k=8),
                          in_=dwout.rearrange("(k p) c -> p k c", p=P)), "qg", 16)
        S.inc(g.dma_start(out=wx2[:].rearrange("p (k n) -> p k n", k=16),
                          in_=dwx2.rearrange("(k p) n -> p k n", p=P)), "qg", 16)
        S.inc(g.dma_start(out=idb[:], in_=didb), "qg", 16)
        S.inc(g.dma_start(out=e16[:], in_=de16), "qg", 16)
        S.inc(g.dma_start(out=e9[:], in_=de9), "qg", 16)
        g.memset(zb[:], 0.0)
        g.memset(cst[:, C_EPS:C_EPS + 1], 1e-5)
        g.memset(cst[:, C_MH:C_MH + 1], -0.5)
        g.memset(cst[:, C_ONE:C_ONE + 1], 1.0)
        g.memset(cst[:, C_MONE:C_MONE + 1], -1.0)
        g.memset(cst[:, C_HALF:C_HALF + 1], 0.5)
        S.inc(g.memset(cst[:, C_TINY:C_TINY + 1], 1e-30), "sg", 1)
        g.wait_ge(S.h["sg"], 1)
        S.inc(g.dma_start(out=bi1, in_=zb[0:1, :]), "qg", 16)
        S.inc(g.dma_start(out=bi2, in_=zb[0:1, :]), "qg", 16)
    QG_W = S.c["qg"]

    @block.sync
    def _(sy):
        S.inc(sy.dma_start(out=cwv[:].rearrange("p (d f) -> p d f", d=8),
                           in_=dcw.rearrange("(d p) f -> p d f", p=P)), "qs", 16)
        S.inc(sy.dma_start(out=cbv[:],
                           in_=dcb.rearrange("(d p) o -> p (d o)", p=P), allow_slow_non_contiguous=True), "qs", 16)
        S.inc(sy.dma_start(out=bdtv[:],
                           in_=dbdt.rearrange("(d p) o -> p (d o)", p=P), allow_slow_non_contiguous=True), "qs", 16)
        S.inc(sy.dma_start(out=dpv[:],
                           in_=ddp.rearrange("(d p) o -> p (d o)", p=P), allow_slow_non_contiguous=True), "qs", 16)
        S.inc(sy.dma_start(out=alog[:].rearrange("p (d n) -> p d n", d=8),
                           in_=dalog.rearrange("(d p) n -> p d n", p=P)), "qs", 16)
        S.inc(sy.dma_start(out=idf[:], in_=didf), "qs", 16)
        S.inc(sy.dma_start(out=ebv[:], in_=deb), "qs", 16)
        S.inc(sy.dma_start(out=b1v[:].rearrange("p (e m) -> p e m", e=NE + 1),
                           in_=db1.rearrange("e (m p) -> p e m", p=P), allow_slow_non_contiguous=True), "qs", 16)
    QS_SETUP = S.c["qs"]

    @block.scalar
    def _(s):
        s.wait_ge(S.h["qs"], QS_SETUP)
        s.wait_ge(S.h["sg"], 1)
        s.activation(out=cst[:, 6:7], in_=cst[:, C_ONE:C_ONE + 1], func=AF.Ln)
        s.activation(out=cst[:, 6:7], in_=cst[:, C_ONE:C_ONE + 1], func=AF.Silu)
        S.inc(s.activation(out=aneg[:], in_=alog[:], func=AF.Exp), "sa")
    t_aexp = S.c["sa"]

    @block.vector
    def _(v):
        v.wait_ge(S.h["sa"], t_aexp)
        S.inc(v.tensor_scalar_mul(aneg[:], aneg[:], -1.0), "sv")
    T_ANEG = S.c["sv"]

    if stop_after == "S":
        qs_f, qg_f = S.c["qs"], S.c["qg"]

        @block.gpsimd
        def _(g):
            g.wait_ge(S.h["qs"], qs_f)
            g.wait_ge(S.h["qg"], qg_f)
            g.wait_ge(S.h["sv"], T_ANEG)
        es_mw.close()
        es.close()
        return nc

    es_a = ExitStack()

    def sba(name, shape, dtype):
        return es_a.enter_context(nc.sbuf_tensor(name, shape, dtype))

    xa = sba("xa", [P, 2 * 4 * D], BF16)
    stats = sba("stats", [P, 48], F32)
    mvb = sba("mvb", [P, 8], F32)
    istd = sba("istd", [P, 2 * 4], F32)
    negm = sba("negm", [P, 2 * 4], F32)
    xn = sba("xn", [P, 2 * 4 * D], BF16)
    xnt = sba("xnt", [P, 2 * 8 * 512], BF16)
    xcp = sba("xcp", [P, 2 * 8 * 515], BF16)
    cvt = sba("cvt", [P, 8 * 512], BF16)
    cvtm = sba("cvtm", [P, 512], BF16)
    xco = sba("xco", [P, 2 * 8 * 512], BF16)
    zso = sba("zso", [P, 2 * 8 * 512], BF16)
    ps_tr = [es_a.enter_context(nc.psum_tensor(f"ps_tr{i}", [P, 512], BF16))
             for i in range(4)]
    ps_xz = [es_a.enter_context(nc.psum_tensor(f"ps_xz{i}", [P, 512], F32))
             for i in range(4)]

    t_ln = [0] * nCH
    t_tr = [0] * nCH
    t_mm = [0] * nCH
    t_cv = [0] * nCH
    t_sil = [0] * nCH
    t_silq = [0] * nCH
    t_trg, t_trgev, t_mmg = [], [], []
    ec_tok = {}


    def a_views(k):
        pb = k % 2
        return dict(
            pb=pb,
            xa_v=xa[:, pb * 4 * D:(pb + 1) * 4 * D],
            xn_v=xn[:, pb * 4 * D:(pb + 1) * 4 * D],
            xnt_v=xnt[:, pb * 8 * 512:(pb + 1) * 8 * 512],
            xcp_v=xcp[:, pb * 8 * 515:(pb + 1) * 8 * 515],
            xcp_p=xcp[:, (1 - pb) * 8 * 515:(2 - pb) * 8 * 515],
            cvt_v=cvt[:, 0:8 * 512],
            xco_v=xco[:, pb * 8 * 512:(pb + 1) * 8 * 512],
            zso_v=zso[:, pb * 8 * 512:(pb + 1) * 8 * 512],
            istd_v=istd[:, pb * 4:(pb + 1) * 4],
            negm_v=negm[:, pb * 4:(pb + 1) * 4])

    def emit_load(k):
        vw = a_views(k)

        @block.gpsimd
        def _(g, k=k, xa_v=vw["xa_v"]):
            if k >= 2:
                g.wait_ge(S.h["sv"], t_ln[k - 2])
            for i in range(4):
                t0 = k * 512 + i * 128
                S.inc(g.dma_start(out=xa_v[:, i * D:(i + 1) * D],
                                  in_=dxin[t0:t0 + 128, :]), "qg", 16)
        return S.c["qg"]

    def emit_stats(k, t_ldx):
        vw = a_views(k)

        @block.vector
        def _(v, t_ldx=t_ldx, xa_v=vw["xa_v"], pb=vw["pb"]):
            v.wait_ge(S.h["qg"], t_ldx)
            for i in range(4):
                v.bn_stats(stats[:, i * 12:i * 12 + 6],
                           xa_v[:, i * D:i * D + 512])
                _li = v.bn_stats(stats[:, i * 12 + 6:i * 12 + 12],
                                 xa_v[:, i * D + 512:(i + 1) * D])
            S.inc(_li, "sv")
            v.wait_ge(S.h["sv"], S.c["sv"])
            for i in range(4):
                _li = v.bn_aggr(mvb[:, 2 * i:2 * i + 2],
                                stats[:, i * 12:(i + 1) * 12])
            S.inc(_li, "sv")
            v.wait_ge(S.h["sv"], S.c["sv"])
            for i in range(4):
                v.tensor_copy(istd[:, pb * 4 + i:pb * 4 + i + 1],
                              mvb[:, 2 * i + 1:2 * i + 2])
                _li = v.tensor_copy(negm[:, pb * 4 + i:pb * 4 + i + 1],
                                    mvb[:, 2 * i:2 * i + 1])
            S.inc(_li, "sv")
        return S.c["sv"]

    def emit_istd(k, t_bn):
        vw = a_views(k)

        @block.scalar
        def _(s, t_bn=t_bn, istd_v=vw["istd_v"]):
            s.wait_ge(S.h["sv"], t_bn)
            for i in range(4):
                s.activation(out=istd_v[:, i:i + 1], in_=istd_v[:, i:i + 1],
                             func=AF.Ln, bias=cc_(C_EPS))
                _li = s.activation(out=istd_v[:, i:i + 1],
                                   in_=istd_v[:, i:i + 1],
                                   func=AF.Exp, scale=cc_(C_MH))
            S.inc(_li, "sa")
        return S.c["sa"]

    def emit_norm(k, t_istd):
        vw = a_views(k)

        @block.vector
        def _(v, k=k, t_istd=t_istd, xa_v=vw["xa_v"], xn_v=vw["xn_v"],
              istd_v=vw["istd_v"], negm_v=vw["negm_v"]):
            v.wait_ge(S.h["sa"], t_istd)
            if k >= 2:
                v.wait_ge(S.h["sp"], t_tr[k - 2])
            for i in range(4):
                _li = v.tensor_scalar(out=negm_v[:, i:i + 1],
                                      in0=negm_v[:, i:i + 1],
                                      scalar1=istd_v[:, i:i + 1],
                                      scalar2=-1.0,
                                      op0=ALU.mult, op1=ALU.mult)
            S.inc(_li, "sv")
            v.wait_ge(S.h["sv"], S.c["sv"])
            for i in range(4):
                _li = v.tensor_scalar(out=xn_v[:, i * D:(i + 1) * D],
                                in0=xa_v[:, i * D:(i + 1) * D],
                                scalar1=negm_v[:, i:i + 1],
                                scalar2=istd_v[:, i:i + 1],
                                op0=ALU.add, op1=ALU.mult)
            S.inc(_li, "sv")
        t_ln[k] = S.c["sv"]

    def emit_pe(k):
        vw = a_views(k)
        ec_base = S.c["sv"] + (3 if k < nCH - 1 else 0)
        sa_off = 1 if k < nCH - 1 else 0
        tok = dict(trev=[ec_base + 1 + g for g in range(4)],
                   xzev=[ec_base + 5 + g for g in range(4)],
                   cv=ec_base + 9,
                   zsil=[S.c["sa"] + sa_off + 1 + g for g in range(4)],
                   silcv=S.c["sa"] + sa_off + 5)
        ec_tok[k] = tok

        for g_ in range(4):
            @block.tensor
            def _(t, k=k, g_=g_, xn_v=vw["xn_v"]):
                if g_ == 0:
                    t.wait_ge(S.h["sv"], t_ln[k])
                    if k == 0:
                        t.wait_ge(S.h["qg"], QG_W)
                if g_ >= 2:
                    t.wait_ge(S.h["sv"], tok["trev"][g_ - 2])
                elif k >= 1:
                    t.wait_ge(S.h["sv"], ec_tok[k - 1]["trev"][g_ + 2])
                for j in range(2):
                    dt = g_ * 2 + j
                    pt = ps_tr[(g_ % 2) * 2 + j]
                    for i in range(4):
                        _li = t.transpose(pt[:, i * P:(i + 1) * P],
                                    xn_v[:, i * D + dt * P:i * D + (dt + 1) * P],
                                    idb[:])
                S.inc(_li, "sp")
            t_trg.append(S.c["sp"])
        t_tr[k] = S.c["sp"]

        for g_ in range(8):
            @block.tensor
            def _(t, k=k, g_=g_, xnt_v=vw["xnt_v"]):
                if g_ == 0:
                    t.wait_ge(S.h["sv"], tok["trev"][3])
                    if k == 0:
                        t.wait_ge(S.h["sv"], T_ANEG)
                    if k >= 2:
                        t.wait_ge(S.h["sa"], t_sil[k - 2])
                    if k >= 1:
                        t.wait_ge(S.h["sv"], ec_tok[k - 1]["cv"])
                if g_ in (2, 3, 4, 5):
                    t.wait_ge(S.h["sv"], tok["xzev"][g_ - 2])
                elif g_ in (6, 7):
                    t.wait_ge(S.h["sa"], tok["zsil"][g_ - 6])
                elif k >= 1:
                    t.wait_ge(S.h["sa"], ec_tok[k - 1]["zsil"][g_ + 2])
                for j in range(2):
                    mt = g_ * 2 + j
                    pj = ps_xz[(g_ % 2) * 2 + j]
                    for kt in range(8):
                        _li = t.matmul(pj[:], win_t(kt, mt),
                                 xnt_v[:, kt * 512:(kt + 1) * 512],
                                 start=(kt == 0), stop=(kt == 7))
                S.inc(_li, "sp")
            t_mmg.append(S.c["sp"])
        t_mm[k] = S.c["sp"]

    def emit_ec(k):
        vw = a_views(k)
        for g_ in range(4):
            @block.vector
            def _(v, k=k, g_=g_, xnt_v=vw["xnt_v"]):
                v.wait_ge(S.h["sp"], t_trg[k * 4 + g_])
                if k >= 2 and g_ == 0:
                    v.wait_ge(S.h["qo"], t_silq[k - 2])
                    v.wait_ge(S.h["sp"], t_mm[k - 2])
                for j in range(2):
                    dt = g_ * 2 + j
                    _li = v.tensor_copy(xnt_v[:, dt * 512:(dt + 1) * 512],
                                        ps_tr[(g_ % 2) * 2 + j][:])
                S.inc(_li, "sv")
            t_trgev.append(S.c["sv"])
        for g_ in range(4):
            @block.vector
            def _(v, k=k, g_=g_, xcp_v=vw["xcp_v"]):
                v.wait_ge(S.h["sp"], t_mmg[k * 8 + g_])
                for j in range(2):
                    mt = g_ * 2 + j
                    pj = ps_xz[(g_ % 2) * 2 + j]
                    _li = v.tensor_copy(
                        xcp_v[:, mt * 515 + 3:mt * 515 + 515], pj[:])
                S.inc(_li, "sv")

        @block.vector
        def _(v, k=k, xcp_v=vw["xcp_v"], xcp_p=vw["xcp_p"],
              cvt_v=vw["cvt_v"]):
            if k >= 1:
                v.wait_ge(S.h["sa"], t_sil[k - 1])
            for dt in range(8):
                if k == 0:
                    v.memset(xcp_v[:, dt * 515:dt * 515 + 3], 0.0)
                else:
                    v.tensor_copy(xcp_v[:, dt * 515:dt * 515 + 3],
                                  xcp_p[:, dt * 515 + 512:dt * 515 + 515])
                cv = cvt_v[:, dt * 512:(dt + 1) * 512]
                v.tensor_scalar_mul(cv, xcp_v[:, dt * 515:dt * 515 + 512],
                                    cwv[:, dt * 4:dt * 4 + 1])
                for j in range(1, 4):
                    v.tensor_scalar_mul(cvtm[:],
                                        xcp_v[:, dt * 515 + j:dt * 515 + j + 512],
                                        cwv[:, dt * 4 + j:dt * 4 + j + 1])
                    _li = v.tensor_tensor(cv, cv, cvtm[:], ALU.add)
            S.inc(_li, "sv")
        t_cv[k] = S.c["sv"]
        assert t_cv[k] == ec_tok[k]["cv"], (k, t_cv[k], ec_tok[k]["cv"])
        assert t_trgev[k * 4:k * 4 + 4] == ec_tok[k]["trev"], k

    def emit_sil(k):
        vw = a_views(k)

        @block.scalar
        def _(s, k=k, xnt_v=vw["xnt_v"]):
            s.wait_ge(S.h["sv"], t_trgev[k * 4 + 3])
            S.inc(s.dma_start(
                out=pxnt.rearrange("d p l -> p d l")[:, :, k * 512:(k + 1) * 512],
                in_=xnt_v[:].rearrange("p (d t) -> p d t", d=8)), "qo", 16)

        for g_ in range(4, 8):
            @block.scalar
            def _(s, k=k, g_=g_, zso_v=vw["zso_v"]):
                s.wait_ge(S.h["sp"], t_mmg[k * 8 + g_])
                for j in range(2):
                    z = g_ * 2 + j - 8
                    pj = ps_xz[(g_ % 2) * 2 + j]
                    _li = s.activation(out=zso_v[:, z * 512:(z + 1) * 512],
                                 in_=pj[:], func=AF.Silu)
                S.inc(_li, "sa")
        assert S.c["sa"] == ec_tok[k]["zsil"][3], k

        @block.scalar
        def _(s, k=k, zso_v=vw["zso_v"]):
            s.wait_ge(S.h["sa"], S.c["sa"])
            S.inc(s.dma_start(
                out=pz.rearrange("d p l -> p d l")[:, :, k * 512:(k + 1) * 512],
                in_=zso_v[:].rearrange("p (d t) -> p d t", d=8)), "qo", 16)

        @block.scalar
        def _(s, k=k, cvt_v=vw["cvt_v"], xco_v=vw["xco_v"]):
            s.wait_ge(S.h["sv"], t_cv[k])
            if k >= 2:
                s.wait_ge(S.h["qo"], t_silq[k - 2])
            for dt in range(8):
                _li = s.activation(out=xco_v[:, dt * 512:(dt + 1) * 512],
                                   in_=cvt_v[:, dt * 512:(dt + 1) * 512],
                                   func=AF.Silu, bias=cbv[:, dt:dt + 1])
            S.inc(_li, "sa")
            s.wait_ge(S.h["sa"], S.c["sa"])
            h = get_h(s, "scalar")
            src = xco_v[:].rearrange("p (d t) -> p d t", d=8)
            with s.If_eq(h, 0):
                s.dma_start(out=shxc.rearrange("d p l -> p d l")[
                    :, 0:8, k * 512:(k + 1) * 512],
                    in_=src).then_inc(S.h["qo"], 16)
            with s.Else():
                s.dma_start(out=shxc.rearrange("d p l -> p d l")[
                    :, 8:16, k * 512:(k + 1) * 512],
                    in_=src).then_inc(S.h["qo"], 16)
            S.c["qo"] += 16
        t_sil[k] = S.c["sa"]
        t_silq[k] = S.c["qo"]
        assert t_sil[k] == ec_tok[k]["silcv"], k

    for k in range(nCH):
        t_ldx = emit_load(k)
        t_bn = emit_stats(k, t_ldx)
        if k >= 1:
            emit_ec(k - 1)
        t_istd = emit_istd(k, t_bn)
        if k >= 1:
            emit_sil(k - 1)
        emit_norm(k, t_istd)
        emit_pe(k)
    emit_ec(nCH - 1)
    emit_sil(nCH - 1)

    QO_A = S.c["qo"]
    QS_A = S.c["qs"]

    @block.gpsimd
    def _(g):
        g.wait_ge(S.h["qo"], QO_A)
        g.wait_ge(S.h["qs"], QS_A)
        g.wait_ge(S.h["sa"], t_sil[nCH - 1])
        g.wait_ge(S.h["qg"], QG_W)
        S.inc(g.collective_compute(
            "AllReduce", ALU.add, replica_groups=PAIRS,
            ins=[bi1], outs=[bo1]), "cc", 1)
    T_BAR1 = S.c["cc"]

    if stop_after == "A":
        dxc = nc.dram_tensor("dbg_xc", [16, P, L], F32, kind="ExternalOutput").ap()
        dzz = nc.dram_tensor("dbg_z", [8, P, L], F32, kind="ExternalOutput").ap()
        dxn = nc.dram_tensor("dbg_xnt", [8, P, L], F32, kind="ExternalOutput").ap()
        dst_ = nc.dram_tensor("dbg_stats", [P, 32], F32, kind="ExternalOutput").ap()
        dxa = nc.dram_tensor("dbg_xa", [P, 4 * D], F32, kind="ExternalOutput").ap()
        qo_f, qs_f, qg_f = S.c["qo"], S.c["qs"], S.c["qg"]

        @block.gpsimd
        def _(g):
            g.wait_ge(S.h["qo"], qo_f)
            g.wait_ge(S.h["qs"], qs_f)
            g.wait_ge(S.h["qg"], qg_f)
            g.wait_ge(S.h["cc"], T_BAR1)
            g.dma_start(out=dxc, in_=shxc).then_inc(S.h["qg"], 16)
            g.dma_start(out=dzz, in_=pz).then_inc(S.h["qg"], 16)
            g.dma_start(out=dxn, in_=pxnt).then_inc(S.h["qg"], 16)
            g.dma_start(out=dst_[:, 0:8], in_=istd[:]).then_inc(S.h["qg"], 16)
            g.dma_start(out=dst_[:, 8:16], in_=negm[:]).then_inc(S.h["qg"], 16)
            g.dma_start(out=dxa, in_=xa[:, 0:4 * D]).then_inc(S.h["qg"], 16)
            g.wait_ge(S.h["qg"], qg_f + 96)
        es_a.close()
        es_wi.close()
        es_mw.close()
        es.close()
        return nc

    es_a.close()
    es_wi.close()

    es_b = ExitStack()

    def sbb(name, shape, dtype):
        return es_b.enter_context(nc.sbuf_tensor(name, shape, dtype))

    xcs2 = sbb("xcs2", [P, 2 * 16 * TS], BF16)
    zss2 = sbb("zss2", [P, 8 * TS], BF16)
    dlt = sbb("dlt", [P, 8 * TS], BF16)
    dxb = sbb("dxb", [P, 8 * TS], BF16)
    dab = sbb("dab", [P, 2 * 8 * TS], BF16)
    ubf = sbb("ubf", [P, TS], BF16)
    sbf = sbb("sbf", [P, TS], BF16)
    pbf2 = sbb("pbf2", [P, TS], BF16)
    yac = sbb("yac", [P, 8 * TS], BF16)
    yfn2 = sbb("yfn2", [P, 2 * 8 * TS], BF16)
    bsb = sbb("bsb", [NST, TS], BF16)
    bbc = sbb("bbc", [P, NST * TS], BF16)
    pre = sbb("pre", [P, 2 * TS], F32)
    spa = sbb("spa", [P, TS], F32)
    spr = sbb("spr", [P, 2 * TS], BF16)
    post = sbb("post", [P, 8 * TS], BF16)
    ps_dt = [es_b.enter_context(nc.psum_tensor(f"ps_dt{i}", [P, TS], F32))
             for i in range(2)]
    ps_bt = es_b.enter_context(nc.psum_tensor("ps_bt", [NST, TS], F32))
    ps_bb = es_b.enter_context(nc.psum_tensor("ps_bb", [P, TS], F32))
    ps_wo = [es_b.enter_context(nc.psum_tensor(f"ps_wo{i}", [P, TS], F32))
             for i in range(2)]

    t_ldb = [0] * nSU
    t_ldz = [0] * nSU
    t_powq = [0] * nSU
    prev = None

    prev2 = None
    for s_ in range(nSU):
        t0 = s_ * TS
        par = s_ % 2
        xcs = xcs2[:, par * 16 * TS:(par + 1) * 16 * TS]
        zss = zss2[:, 0:8 * TS]
        yfn = yfn2[:, par * 8 * TS:(par + 1) * 8 * TS]
        yfn_p = yfn2[:, (1 - par) * 8 * TS:(2 - par) * 8 * TS]
        sp0, sa0, sv0 = S.c["sp"], S.c["sa"], S.c["sv"]

        has_wo = s_ >= 1
        SP_WX2 = sp0 + 1
        SP_WDT = [SP_WX2 + 1 + m for m in range(8)]
        SP_BB = [SP_WX2 + 9 + n for n in range(16)]
        SP_WO = [SP_WX2 + 25 + m for m in range(8)] if has_wo else []
        SA_BSB = sa0 + 1
        SA_SP = [SA_BSB + 1 + m for m in range(8)]
        SA_DA = [SA_BSB + 9 + n for n in range(16)]
        SA_WOEV = [SA_BSB + 25 + m for m in range(8)] if has_wo else []
        seq = []
        for m in range(8):
            if m >= 2:
                seq.append(("d", m - 2))
            seq.append(("p", m))
        seq += [("d", 6), ("d", 7)]
        SV_PRE, SV_DLT = {}, {}
        for i, (kind, m) in enumerate(seq):
            (SV_PRE if kind == "p" else SV_DLT)[m] = sv0 + i + 1
        SV_DX = sv0 + 17
        SV_BBEV = [SV_DX + 1 + n for n in range(16)]
        SV_SCAN = [SV_DX + 17 + n for n in range(16)]
        SV_YFN = SV_DX + 33

        @block.sync
        def _(sy, s_=s_, t0=t0, xcs=xcs, zss=zss):
            if s_ == 0:
                sy.wait_ge(S.h["qo"], QO_A)
            if s_ >= 2:
                sy.wait_ge(S.h["sv"], prev2["SV_YFN"])
                sy.wait_ge(S.h["sp"], prev2["SP_WDT7"])
            h = get_h(sy, "sync")
            xv = shxc.rearrange("d p l -> p d l")
            with sy.If_eq(h, 0):
                sy.dma_start(out=xcs[:, 0:8 * TS].rearrange(
                    "p (d t) -> p d t", d=8),
                    in_=xv[:, 0:8, t0:t0 + TS]).then_inc(S.h["qs"], 16)
            with sy.Else():
                sy.dma_start(out=xcs[:, 0:8 * TS].rearrange(
                    "p (d t) -> p d t", d=8),
                    in_=xv[:, 8:16, t0:t0 + TS]).then_inc(S.h["qs"], 16)
            S.c["qs"] += 16
            if s_ == 0:
                sy.wait_ge(S.h["cc"], T_BAR1)
            with sy.If_eq(h, 0):
                sy.dma_start(out=xcs[:, 8 * TS:16 * TS].rearrange(
                    "p (d t) -> p d t", d=8),
                    in_=xv[:, 8:16, t0:t0 + TS]).then_inc(S.h["qs"], 16)
            with sy.Else():
                sy.dma_start(out=xcs[:, 8 * TS:16 * TS].rearrange(
                    "p (d t) -> p d t", d=8),
                    in_=xv[:, 0:8, t0:t0 + TS]).then_inc(S.h["qs"], 16)
            S.c["qs"] += 16
        t_ldb[s_] = S.c["qs"]

        @block.sync
        def _(sy, s_=s_, t0=t0, zss=zss, prev=prev):
            if s_ >= 1:
                sy.wait_ge(S.h["sv"], prev["SV_YFN"])
            S.inc(sy.dma_start(out=zss[:].rearrange("p (d t) -> p d t", d=8),
                               in_=pz.rearrange("d p l -> p d l")[
                                   :, :, t0:t0 + TS]), "qs", 16)
        t_ldz[s_] = S.c["qs"]

        @block.tensor
        def _(t, s_=s_, SP_WDT=SP_WDT, SV_PRE=SV_PRE, SV_BBEV=SV_BBEV,
              SV_YFN=SV_YFN, SA_BSB=SA_BSB, SA_WOEV=SA_WOEV, SA_DA=SA_DA,
              prev=prev, xcs=xcs, yfn=yfn, yfn_p=yfn_p):
            t.wait_ge(S.h["qs"], t_ldb[s_])
            if s_ == 0:
                t.wait_ge(S.h["cc"], T_BAR1)
                t.wait_ge(S.h["qg"], QG_W)
            else:
                t.wait_ge(S.h["sa"], prev["SA_BSB"])
            for kt in range(16):
                _li = t.matmul(ps_bt[:], wx2[:, kt * NST:(kt + 1) * NST],
                               xcs[:, kt * TS:(kt + 1) * TS],
                               start=(kt == 0), stop=(kt == 15))
            S.inc(_li, "sp")
            for mt in range(8):
                if mt >= 2:
                    t.wait_ge(S.h["sv"], SV_PRE[mt - 2])
                if mt == 0 and s_ >= 1:
                    t.wait_ge(S.h["sa"], prev["SA_DA15"])
                for kt in range(16):
                    _li = t.matmul(ps_dt[mt % 2][:], wdt_t(kt, mt),
                                   xcs[:, kt * TS:(kt + 1) * TS],
                                   start=(kt == 0), stop=(kt == 15))
                S.inc(_li, "sp")
            for n in range(NST):
                if n == 0:
                    t.wait_ge(S.h["sa"], SA_BSB)
                else:
                    t.wait_ge(S.h["sv"], SV_BBEV[n - 1])
                _li = t.matmul(ps_bb[:], e16[:, n * P:(n + 1) * P], bsb[:],
                               start=True, stop=True)
                S.inc(_li, "sp")
            if s_ >= 1:
                for mt in range(8):
                    if mt == 0:
                        t.wait_ge(S.h["sv"], prev["SV_YFN"])
                        if s_ >= 2:
                            t.wait_ge(S.h["sa"], prev["SA_WOEV7"])
                    if mt >= 2:
                        t.wait_ge(S.h["sa"], SA_WOEV[mt - 2])
                    for kt in range(8):
                        _li = t.matmul(ps_wo[mt % 2][:], wout_t(kt, mt),
                                       yfn_p[:, kt * TS:(kt + 1) * TS],
                                       start=(kt == 0), stop=(kt == 7))
                    S.inc(_li, "sp")

        @block.scalar
        def _(s, s_=s_, t0=t0, SP_WX2=SP_WX2, SP_WDT=SP_WDT, SP_BB=SP_BB,
              SP_WO=SP_WO, SV_PRE=SV_PRE, SV_DLT=SV_DLT, SV_DX=SV_DX,
              SV_SCAN=SV_SCAN, SA_WOEV=SA_WOEV, prev=prev):
            s.wait_ge(S.h["sp"], SP_WX2)
            if s_ >= 1:
                s.wait_ge(S.h["sp"], prev["SP_BB15"])
            _li = s.activation(out=bsb[:], in_=ps_bt[:], func=AF.Copy)
            S.inc(_li, "sa")
            for mt in range(8):
                s.wait_ge(S.h["sv"], SV_PRE[mt])
                if mt >= 2:
                    s.wait_ge(S.h["sv"], SV_DLT[mt - 2])
                pv = pre[:, (mt % 2) * TS:(mt % 2 + 1) * TS]
                s.activation(out=spa[:], in_=pv, func=AF.Abs)
                s.activation(out=spa[:], in_=spa[:], func=AF.Exp,
                             scale=cc_(C_MONE))
                _li = s.activation(out=spr[:, (mt % 2) * TS:(mt % 2 + 1) * TS],
                                   in_=spa[:], func=AF.Ln, bias=cc_(C_ONE))
                S.inc(_li, "sa")
            for n in range(NST):
                if n == 0:
                    s.wait_ge(S.h["sv"], SV_DX)
                elif n >= 2:
                    s.wait_ge(S.h["sv"], SV_SCAN[n - 2])
                np_ = n % 2
                for dt in range(8):
                    _li = s.activation(
                        out=dab[:, (np_ * 8 + dt) * TS:(np_ * 8 + dt + 1) * TS],
                        in_=dlt[:, dt * TS:(dt + 1) * TS], func=AF.Exp,
                        scale=aneg[:, dt * NST + n:dt * NST + n + 1])
                S.inc(_li, "sa")
            if s_ >= 1:
                for mt in range(8):
                    s.wait_ge(S.h["sp"], SP_WO[mt])
                    if s_ >= 2 and mt == 0:
                        s.wait_ge(S.h["qo"], t_powq[s_ - 1])
                    _li = s.activation(out=post[:, mt * TS:(mt + 1) * TS],
                                       in_=ps_wo[mt % 2][:], func=AF.Copy)
                    S.inc(_li, "sa")
                s.wait_ge(S.h["sa"], S.c["sa"])
                h = get_h(s, "scalar")
                srcp = post[:].rearrange("p (d t) -> p d t", d=8)
                tp0 = (s_ - 1) * TS
                with s.If_eq(h, 0):
                    s.dma_start(out=shpo.rearrange("w d p l -> p w d l")[
                        :, 0, :, tp0:tp0 + TS], in_=srcp).then_inc(S.h["qo"], 16)
                with s.Else():
                    s.dma_start(out=shpo.rearrange("w d p l -> p w d l")[
                        :, 1, :, tp0:tp0 + TS], in_=srcp).then_inc(S.h["qo"], 16)
                S.c["qo"] += 16
        t_powq[s_] = S.c["qo"]

        @block.vector
        def _(v, s_=s_, seq=seq, SP_WDT=SP_WDT, SP_BB=SP_BB, SA_SP=SA_SP,
              SA_DA=SA_DA, prev=prev, xcs=xcs, zss=zss, yfn=yfn):
            for kind, m in seq:
                if kind == "p":
                    v.wait_ge(S.h["sp"], SP_WDT[m])
                    if m >= 2:
                        v.wait_ge(S.h["sa"], SA_SP[m - 2])
                    _li = v.tensor_scalar(
                        out=pre[:, (m % 2) * TS:(m % 2 + 1) * TS],
                        in0=ps_dt[m % 2][:], scalar1=bdtv[:, m:m + 1],
                        scalar2=None, op0=ALU.add)
                    S.inc(_li, "sv")
                else:
                    v.wait_ge(S.h["sa"], SA_SP[m])
                    pv = pre[:, (m % 2) * TS:(m % 2 + 1) * TS]
                    dv = dlt[:, m * TS:(m + 1) * TS]
                    v.tensor_relu(dv, pv)
                    _li = v.tensor_tensor(
                        dv, dv, spr[:, (m % 2) * TS:(m % 2 + 1) * TS], ALU.add)
                    S.inc(_li, "sv")
            for dt in range(8):
                _li = v.tensor_tensor(dxb[:, dt * TS:(dt + 1) * TS],
                                      dlt[:, dt * TS:(dt + 1) * TS],
                                      xcs[:, dt * TS:(dt + 1) * TS], ALU.mult)
            S.inc(_li, "sv")
            for n in range(NST):
                v.wait_ge(S.h["sp"], SP_BB[n])
                _li = v.tensor_copy(bbc[:, n * TS:(n + 1) * TS], ps_bb[:])
                S.inc(_li, "sv")
            for n in range(NST):
                v.wait_ge(S.h["sa"], SA_DA[n])
                np_ = n % 2
                bb_v = bbc[:, n * TS:(n + 1) * TS]
                for dt in range(8):
                    da_v = dab[:, (np_ * 8 + dt) * TS:(np_ * 8 + dt + 1) * TS]
                    y_v = yac[:, dt * TS:(dt + 1) * TS]
                    v.tensor_tensor(ubf[:], dxb[:, dt * TS:(dt + 1) * TS],
                                    bb_v, ALU.mult)
                    ini = 0.0 if s_ == 0 else carry[:, n * 8 + dt:n * 8 + dt + 1]
                    v.tensor_tensor_scan(sbf[:], da_v, ubf[:], ini,
                                         ALU.mult, ALU.add)
                    if n == 0:
                        v.tensor_tensor(y_v, sbf[:], bb_v, ALU.mult)
                    else:
                        v.tensor_tensor(pbf2[:, 0:TS], sbf[:], bb_v, ALU.mult)
                        v.tensor_tensor(y_v, y_v, pbf2[:, 0:TS], ALU.add)
                    _li = v.tensor_copy(carry[:, n * 8 + dt:n * 8 + dt + 1],
                                        sbf[:, TS - 1:TS])
                S.inc(_li, "sv")
            if s_ >= 2:
                v.wait_ge(S.h["sp"], prev["SP_WO7"])
            v.wait_ge(S.h["qs"], t_ldz[s_])
            for dt in range(8):
                v.scalar_tensor_tensor(
                    out=yfn[:, dt * TS:(dt + 1) * TS],
                    in0=xcs[:, dt * TS:(dt + 1) * TS],
                    scalar=dpv[:, dt:dt + 1],
                    in1=yac[:, dt * TS:(dt + 1) * TS],
                    op0=ALU.mult, op1=ALU.add)
                _li = v.tensor_tensor(yfn[:, dt * TS:(dt + 1) * TS],
                                      yfn[:, dt * TS:(dt + 1) * TS],
                                      zss[:, dt * TS:(dt + 1) * TS], ALU.mult)
            S.inc(_li, "sv")

        assert S.c["sp"] == (SP_WO[7] if has_wo else SP_BB[15]), S.c["sp"]
        assert S.c["sa"] == (SA_WOEV[7] if has_wo else SA_DA[15]), S.c["sa"]
        assert S.c["sv"] == SV_YFN, (S.c["sv"], SV_YFN)
        prev2 = prev
        prev = {"SV_YFN": SV_YFN, "SP_WDT7": SP_WDT[7], "SA_BSB": SA_BSB,
                "SA_DA15": SA_DA[15],
                "SA_WOEV7": SA_WOEV[7] if has_wo else
                (prev["SA_WOEV7"] if prev else 0),
                "SP_BB15": SP_BB[15],
                "SP_WO7": SP_WO[7] if has_wo else
                (prev["SP_WO7"] if prev else 0)}


    par_f = (nSU - 1) % 2
    yfn_f = yfn2[:, par_f * 8 * TS:(par_f + 1) * 8 * TS]
    SP_WOF = [S.c["sp"] + 1 + m for m in range(8)]
    SA_WOFE = [S.c["sa"] + 1 + m for m in range(8)]

    @block.tensor
    def _(t):
        for mt in range(8):
            if mt == 0:
                t.wait_ge(S.h["sv"], prev["SV_YFN"])
                t.wait_ge(S.h["sa"], prev["SA_WOEV7"])
            if mt >= 2:
                t.wait_ge(S.h["sa"], SA_WOFE[mt - 2])
            for kt in range(8):
                _li = t.matmul(ps_wo[mt % 2][:], wout_t(kt, mt),
                               yfn_f[:, kt * TS:(kt + 1) * TS],
                               start=(kt == 0), stop=(kt == 7))
            S.inc(_li, "sp")

    @block.scalar
    def _(s):
        for mt in range(8):
            s.wait_ge(S.h["sp"], SP_WOF[mt])
            if mt == 0:
                s.wait_ge(S.h["qo"], t_powq[nSU - 1])
            _li = s.activation(out=post[:, mt * TS:(mt + 1) * TS],
                               in_=ps_wo[mt % 2][:], func=AF.Copy)
            S.inc(_li, "sa")
        s.wait_ge(S.h["sa"], S.c["sa"])
        h = get_h(s, "scalar")
        srcp = post[:].rearrange("p (d t) -> p d t", d=8)
        tf0 = (nSU - 1) * TS
        with s.If_eq(h, 0):
            s.dma_start(out=shpo.rearrange("w d p l -> p w d l")[
                :, 0, :, tf0:tf0 + TS], in_=srcp).then_inc(S.h["qo"], 16)
        with s.Else():
            s.dma_start(out=shpo.rearrange("w d p l -> p w d l")[
                :, 1, :, tf0:tf0 + TS], in_=srcp).then_inc(S.h["qo"], 16)
        S.c["qo"] += 16
    assert S.c["sp"] == SP_WOF[7]
    assert S.c["sa"] == SA_WOFE[7]

    QO_B = S.c["qo"]

    @block.gpsimd
    def _(g):
        g.wait_ge(S.h["qo"], QO_B)
        S.inc(g.collective_compute(
            "AllReduce", ALU.add, replica_groups=PAIRS,
            ins=[bi2], outs=[bo2]), "cc", 1)
    T_BAR2 = S.c["cc"]

    if stop_after == "B":
        dpo = nc.dram_tensor("dbg_po", [2, 8, P, L], F32,
                             kind="ExternalOutput").ap()
        ddl = nc.dram_tensor("dbg_dlt", [P, 8 * TS], F32,
                             kind="ExternalOutput").ap()
        dbs = nc.dram_tensor("dbg_bsb", [NST, TS], F32,
                             kind="ExternalOutput").ap()
        dbb = nc.dram_tensor("dbg_bbc", [P, NST * TS], F32,
                             kind="ExternalOutput").ap()
        dya = nc.dram_tensor("dbg_yac", [P, 8 * TS], F32,
                             kind="ExternalOutput").ap()
        dyf = nc.dram_tensor("dbg_yfn", [P, 8 * TS], F32,
                             kind="ExternalOutput").ap()
        qo_f, qs_f, qg_f = S.c["qo"], S.c["qs"], S.c["qg"]

        @block.gpsimd
        def _(g):
            g.wait_ge(S.h["qo"], qo_f)
            g.wait_ge(S.h["qs"], qs_f)
            g.wait_ge(S.h["qg"], qg_f)
            g.wait_ge(S.h["cc"], T_BAR2)
            g.dma_start(out=dpo, in_=shpo).then_inc(S.h["qg"], 16)
            g.dma_start(out=ddl, in_=dlt[:]).then_inc(S.h["qg"], 16)
            g.dma_start(out=dbs, in_=bsb[:]).then_inc(S.h["qg"], 16)
            g.dma_start(out=dbb, in_=bbc[:]).then_inc(S.h["qg"], 16)
            g.dma_start(out=dya, in_=yac[:]).then_inc(S.h["qg"], 16)
            g.dma_start(out=dyf, in_=yfn[:]).then_inc(S.h["qg"], 16)
            g.wait_ge(S.h["qg"], qg_f + 96)
        es_b.close()
        es_mw.close()
        es.close()
        return nc

    es_b.close()
    es_mw.close()

    if False:
        pass

    es_c = ExitStack()

    def sbc(name, shape, dtype):
        return es_c.enter_context(nc.sbuf_tensor(name, shape, dtype))

    xnh = sbc("xnh", [P, 8 * TH], BF16)
    racc = sbc("racc", [P, 8 * TH], F32)
    cft = sbc("cft", [NE + 1, TH], BF16)
    lgb = sbc("lgb", [NE, 512], F32)
    spb = sbc("spb", [NE, 512], F32)
    acb = sbc("acb", [NE, 512], F32)
    sptok = sbc("sptok", [P, 4 * NE], F32)
    actok = sbc("actok", [P, 4 * NE], F32)
    cfe = sbc("cfe", [P, 4 * (NE + 1)], F32)
    m1b = sbc("m1b", [P, 3], F32)
    rwT = sbc("rwT", [P, 8 * NE], BF16)
    rb2x = sbc("rb2x", [NE + 1, D], BF16)
    w1e = sbc("w1e", [P, 2 * 8 * HF], BF16)
    w2e = sbc("w2e", [P, 2 * 4 * D], BF16)
    hbuf = sbc("hbuf", [P, 4 * 512], BF16)
    cfbb = sbc("cfbb", [P, 512], BF16)
    rtmp = sbc("rtmp", [P, 2 * 512], BF16)
    pobuf = sbc("pobuf", [P, 2 * 8 * 512], BF16)
    xrl = sbc("xrl", [P, 4 * D], F32)
    otok = sbc("otok", [P, 2 * D], F32)

    @block.gpsimd
    def _(g):
        g.wait_ge(S.h["cc"], T_BAR2)
        S.inc(g.dma_start(out=rwT[:].rearrange("p (k e) -> p k e", k=8),
                          in_=drw.rearrange("(k p) e -> p k e", p=P)), "qg", 16)
        S.inc(g.dma_start(out=rb2x[:], in_=db2), "qg", 16)
    QG_C0 = S.c["qg"]

    @block.sync
    def _(sy):
        sy.wait_ge(S.h["cc"], T_BAR2)
        h = get_h(sy, "sync")
        with sy.If_eq(h, 0):
            sy.dma_start(out=xnh[:].rearrange("p (d t) -> p d t", d=8),
                         in_=pxnt.rearrange("d p l -> p d l")[
                             :, :, 0:TH]).then_inc(S.h["qs"], 16)
        with sy.Else():
            sy.dma_start(out=xnh[:].rearrange("p (d t) -> p d t", d=8),
                         in_=pxnt.rearrange("d p l -> p d l")[
                             :, :, TH:L]).then_inc(S.h["qs"], 16)
        S.c["qs"] += 16
    T_XNH = S.c["qs"]

    es_c1 = ExitStack()
    ps_lg = es_c1.enter_context(nc.psum_tensor("ps_lg", [NE, 512], F32))
    ps_tp = es_c1.enter_context(nc.psum_tensor("ps_tp", [P, 64], F32))
    ps_cf = es_c1.enter_context(nc.psum_tensor("ps_cf", [NE + 1, 512], F32))

    t_rt_done = 0
    t_top2 = [0] * nTB
    for tb in range(nTB):
        @block.tensor
        def _(t, tb=tb):
            t.wait_ge(S.h["qs"], T_XNH)
            if tb == 0:
                t.wait_ge(S.h["qg"], QG_C0)
            else:
                t.wait_ge(S.h["sv"], t_top2[tb - 1])
            for kt in range(8):
                _li = t.matmul(ps_lg[:], rwT[:, kt * NE:(kt + 1) * NE],
                         xnh[:, kt * TH + tb * 512:kt * TH + (tb + 1) * 512],
                         start=(kt == 0), stop=(kt == 7))
            S.inc(_li, "sp")
        t_lgm = S.c["sp"]

        @block.vector
        def _(v, t_lgm=t_lgm):
            v.wait_ge(S.h["sp"], t_lgm)
            _li = v.tensor_scalar(out=lgb[:], in0=ps_lg[:], scalar1=ebv[:, 0:1],
                            scalar2=None, op0=ALU.add)
            S.inc(_li, "sv")
        t_lg = S.c["sv"]

        @block.scalar
        def _(s, t_lg=t_lg):
            s.wait_ge(S.h["sv"], t_lg)
            s.activation(out=spb[:], in_=lgb[:], func=AF.Abs)
            s.activation(out=spb[:], in_=spb[:], func=AF.Exp,
                         scale=cst[0:NE, C_MONE:C_MONE + 1])
            _li = s.activation(out=spb[:], in_=spb[:], func=AF.Ln,
                         bias=cst[0:NE, C_ONE:C_ONE + 1])
            S.inc(_li, "sa")
        t_sp1 = S.c["sa"]

        @block.vector
        def _(v, t_sp1=t_sp1):
            v.wait_ge(S.h["sa"], t_sp1)
            v.tensor_relu(acb[:], lgb[:])
            v.tensor_tensor(spb[:], spb[:], acb[:], ALU.add)
            _li = v.tensor_scalar_max(spb[:], spb[:], cst[0:NE, C_TINY:C_TINY + 1])
            S.inc(_li, "sv")
        t_sp2 = S.c["sv"]

        @block.scalar
        def _(s, t_sp2=t_sp2):
            s.wait_ge(S.h["sv"], t_sp2)
            s.activation(out=acb[:], in_=spb[:], func=AF.Ln)
            _li = s.activation(out=acb[:], in_=acb[:], func=AF.Exp,
                         scale=cst[0:NE, C_HALF:C_HALF + 1])
            S.inc(_li, "sa")
        t_act = S.c["sa"]

        @block.tensor
        def _(t, t_act=t_act, tb=tb):
            t.wait_ge(S.h["sa"], t_act)
            if tb > 0:
                t.wait_ge(S.h["sa"], S.c["sa"])
            for tt in range(4):
                t.transpose(ps_tp[:, tt * 8:(tt + 1) * 8],
                            spb[:, tt * P:(tt + 1) * P], idf[0:NE, 0:NE])
                _li = t.transpose(ps_tp[:, 32 + tt * 8:32 + (tt + 1) * 8],
                            acb[:, tt * P:(tt + 1) * P], idf[0:NE, 0:NE])
            S.inc(_li, "sp")
        t_tpm = S.c["sp"]

        @block.scalar
        def _(s, t_tpm=t_tpm):
            s.wait_ge(S.h["sp"], t_tpm)
            s.activation(out=sptok[:], in_=ps_tp[:, 0:32], func=AF.Copy)
            _li = s.activation(out=actok[:], in_=ps_tp[:, 32:64], func=AF.Copy)
            S.inc(_li, "sa")
        t_tpe = S.c["sa"]

        @block.vector
        def _(v, t_tpe=t_tpe):
            v.wait_ge(S.h["sa"], t_tpe)
            def rtw(inst):
                S.inc(inst, "sv")
                v.wait_ge(S.h["sv"], S.c["sv"])
            for tt in range(4):
                sp_t = sptok[:, tt * NE:(tt + 1) * NE]
                ac_t = actok[:, tt * NE:(tt + 1) * NE]
                cf_t = cfe[:, tt * (NE + 1):(tt + 1) * (NE + 1)]
                rtw(v.reduce_max(m1b[:, 0:1], sp_t, axis=mybir.AxisListType.X))
                rtw(v.tensor_scalar(out=cf_t[:, 0:NE], in0=sp_t,
                                    scalar1=m1b[:, 0:1], scalar2=None,
                                    op0=ALU.is_equal))
                rtw(v.tensor_scalar_mul(cf_t[:, 0:NE], cf_t[:, 0:NE], 1e30))
                rtw(v.tensor_tensor(cf_t[:, 0:NE], sp_t, cf_t[:, 0:NE],
                                    ALU.subtract))
                rtw(v.reduce_max(m1b[:, 1:2], cf_t[:, 0:NE],
                                 axis=mybir.AxisListType.X))
                rtw(v.tensor_scalar(out=cf_t[:, 0:NE], in0=sp_t,
                                    scalar1=m1b[:, 1:2], scalar2=None,
                                    op0=ALU.is_ge))
                v.tensor_tensor(cf_t[:, 0:NE], cf_t[:, 0:NE], ac_t, ALU.mult)
                _li = v.memset(cf_t[:, NE:NE + 1], 1.0)
            S.inc(_li, "sv")
        t_top2[tb] = S.c["sv"]

        @block.tensor
        def _(t, tb=tb):
            t.wait_ge(S.h["sv"], t_top2[tb])
            if tb > 0:
                t.wait_ge(S.h["sa"], S.c["sa"])
            for tt in range(4):
                _li = t.transpose(ps_cf[:, tt * P:(tt + 1) * P],
                            cfe[:, tt * (NE + 1):(tt + 1) * (NE + 1)], idf[:])
            S.inc(_li, "sp")
        t_cfm = S.c["sp"]

        @block.scalar
        def _(s, t_cfm=t_cfm, tb=tb):
            s.wait_ge(S.h["sp"], t_cfm)
            _li = s.activation(out=cft[:, tb * 512:(tb + 1) * 512], in_=ps_cf[:],
                         func=AF.Copy)
            S.inc(_li, "sa")
        t_rt_done = S.c["sa"]

    es_c1.close()

    es_c2 = ExitStack()
    ps_h = [es_c2.enter_context(nc.psum_tensor(f"ps_h{i}", [P, 512], F32))
            for i in range(2)]
    ps_rt = [es_c2.enter_context(nc.psum_tensor(f"ps_rt{i}", [P, 512], F32))
             for i in range(2)]
    ps_cb = es_c2.enter_context(nc.psum_tensor("ps_cb", [P, 512], F32))

    e_order = [8] + list(range(8))
    sp0e, sa0e, sv0e = S.c["sp"], S.c["sa"], S.c["sv"]
    SP_MM1 = {}
    SP_CFB = {}
    SP_MM2 = {}
    SA_M1E = {}
    SV_CFB = {}
    SV_RT = {}
    spv, sav, svv = sp0e, sa0e, sv0e
    for ei, e in enumerate(e_order):
        for tb in range(nTB):
            for mt1 in range(4):
                spv += 1; SP_MM1[(ei, tb, mt1)] = spv
                sav += 1; SA_M1E[(ei, tb, mt1)] = sav
            if e != 8:
                spv += 1; SP_CFB[(ei, tb)] = spv
                svv += 1; SV_CFB[(ei, tb)] = svv
            for mt in range(8):
                spv += 1; SP_MM2[(ei, tb, mt)] = spv
                svv += 1; SV_RT[(ei, tb, mt)] = svv
    SP_E_END, SA_E_END, SV_E_END = spv, sav, svv

    def prev_mm2_last(ei, tb):
        if tb > 0:
            return SP_MM2[(ei, tb - 1, 7)]
        if ei > 0:
            return SP_MM2[(ei - 1, nTB - 1, 7)]
        return None

    t_wld = [0] * 9

    @block.gpsimd
    def _(g):
        for ei, e in enumerate(e_order):
            ep = ei % 2
            if ei >= 2:
                g.wait_ge(S.h["sp"], SP_MM2[(ei - 2, nTB - 1, 7)])
            S.inc(g.dma_start(
                out=w1e[:, ep * 8 * HF:(ep + 1) * 8 * HF].rearrange(
                    "p (k c) -> p k c", k=8),
                in_=dw1[e].rearrange("(k p) c -> p k c", p=P)), "qg", 16)
            S.inc(g.dma_start(
                out=w2e[:, ep * 4 * D:(ep + 1) * 4 * D].rearrange(
                    "p (k c) -> p k c", k=4),
                in_=dw2[e].rearrange("(k p) c -> p k c", p=P)), "qg", 16)
            t_wld[ei] = S.c["qg"]

    @block.tensor
    def _(t):
        t.wait_ge(S.h["sa"], t_rt_done)
        for ei, e in enumerate(e_order):
            ep = ei % 2
            for tb in range(nTB):
                if tb == 0:
                    t.wait_ge(S.h["qg"], t_wld[ei])
                for mt1 in range(4):
                    g_prev = (ei * nTB + tb) * 4 + mt1 - 2
                    if g_prev >= 0:
                        pei, r = divmod(g_prev, nTB * 4)
                        ptb, pmt1 = divmod(r, 4)
                        t.wait_ge(S.h["sa"], SA_M1E[(pei, ptb, pmt1)])
                    for kt in range(8):
                        _li = t.matmul(
                            ps_h[mt1 % 2][:],
                            w1e[:, ep * 8 * HF + kt * HF + mt1 * P:
                                ep * 8 * HF + kt * HF + (mt1 + 1) * P],
                            xnh[:, kt * TH + tb * 512:kt * TH + (tb + 1) * 512],
                            start=(kt == 0), stop=(kt == 7))
                    S.inc(_li, "sp")
                if e != 8:
                    keys = sorted(SV_CFB)
                    my_i = keys.index((ei, tb))
                    if my_i > 0:
                        t.wait_ge(S.h["sv"], SV_CFB[keys[my_i - 1]])
                    _li = t.matmul(ps_cb[:], e9[:, e * P:(e + 1) * P],
                                   cft[:, tb * 512:(tb + 1) * 512],
                                   start=True, stop=True)
                    S.inc(_li, "sp")
                for mt in range(8):
                    if mt == 0:
                        if e != 8:
                            t.wait_ge(S.h["sv"], SV_CFB[(ei, tb)])
                        else:
                            t.wait_ge(S.h["sa"], SA_M1E[(ei, tb, 3)])
                    g2 = (ei * nTB + tb) * 8 + mt - 2
                    if g2 >= 0:
                        pei, r = divmod(g2, nTB * 8)
                        ptb, pmt = divmod(r, 8)
                        t.wait_ge(S.h["sv"], SV_RT[(pei, ptb, pmt)])
                    for kt in range(4):
                        _li = t.matmul(
                            ps_rt[mt % 2][:],
                            w2e[:, ep * 4 * D + kt * D + mt * P:
                                ep * 4 * D + kt * D + (mt + 1) * P],
                            hbuf[:, kt * 512:(kt + 1) * 512],
                            start=(kt == 0), stop=(kt == 3 and e != 8))
                    if e == 8:
                        _li = t.matmul(ps_rt[mt % 2][:],
                                       rb2x[:, mt * P:(mt + 1) * P],
                                       cft[:, tb * 512:(tb + 1) * 512],
                                       start=False, stop=True)
                    S.inc(_li, "sp")
    assert S.c["sp"] == SP_E_END, (S.c["sp"], SP_E_END)

    @block.scalar
    def _(s):
        for ei, e in enumerate(e_order):
            for tb in range(nTB):
                pm = prev_mm2_last(ei, tb)
                for mt1 in range(4):
                    s.wait_ge(S.h["sp"], SP_MM1[(ei, tb, mt1)])
                    if pm is not None and mt1 == 0:
                        s.wait_ge(S.h["sp"], pm)
                    _li = s.activation(out=hbuf[:, mt1 * 512:(mt1 + 1) * 512],
                                       in_=ps_h[mt1 % 2][:], func=AF.Silu,
                                       bias=b1v[:, e * 4 + mt1:e * 4 + mt1 + 1])
                    S.inc(_li, "sa")
    assert S.c["sa"] == SA_E_END, (S.c["sa"], SA_E_END)

    @block.vector
    def _(v):
        for ei, e in enumerate(e_order):
            for tb in range(nTB):
                if e != 8:
                    v.wait_ge(S.h["sp"], SP_CFB[(ei, tb)])
                    v.wait_ge(S.h["sa"], SA_M1E[(ei, tb, 3)])
                    v.tensor_copy(cfbb[:], ps_cb[:])
                    for mt1 in range(4):
                        _li = v.tensor_tensor(
                            hbuf[:, mt1 * 512:(mt1 + 1) * 512],
                            hbuf[:, mt1 * 512:(mt1 + 1) * 512],
                            cfbb[:], ALU.mult)
                    S.inc(_li, "sv")
                for mt in range(8):
                    v.wait_ge(S.h["sp"], SP_MM2[(ei, tb, mt)])
                    dst = racc[:, mt * TH + tb * 512:mt * TH + (tb + 1) * 512]
                    if e == 8:
                        _li = v.tensor_copy(dst, ps_rt[mt % 2][:])
                    else:
                        _li = v.tensor_tensor(dst, dst, ps_rt[mt % 2][:],
                                              ALU.add)
                    S.inc(_li, "sv")
    assert S.c["sv"] == SV_E_END, (S.c["sv"], SV_E_END)

    T_EXP_DVE = SV_E_END

    es_c2.close()

    es_c3 = ExitStack()
    ps_ot = [es_c3.enter_context(nc.psum_tensor(f"ps_ot{i}", [P, D], F32))
             for i in range(2)]

    t_po_qs = [0] * nTB
    t_po_qo = [0] * nTB
    t_xr_qs = [0] * nTB
    t_xr_qo = [0] * nTB
    t_accs = [0] * nTB
    t_fin, t_finev, t_oadd, t_odma = [], [], [], []

    def _po_dma(eng, qn, w, tb, h):
        for hh in range(2):
            ctx = eng.If_eq(h, 0) if hh == 0 else eng.Else()
            with ctx:
                eng.dma_start(
                    out=pobuf[:, w * 8 * 512:(w + 1) * 8 * 512].rearrange(
                        "p (d t) -> p d t", d=8),
                    in_=shpo.rearrange("w d p l -> p w d l")[
                        :, w, :, hh * TH + tb * 512:hh * TH + (tb + 1) * 512]
                ).then_inc(S.h[qn], 16)
        S.c[qn] += 16

    def _xr_dma(eng, qn, i, tb, h):
        for hh in range(2):
            ctx = eng.If_eq(h, 0) if hh == 0 else eng.Else()
            with ctx:
                eng.dma_start(out=xrl[:, i * D:(i + 1) * D],
                              in_=dxin[hh * TH + tb * 512 + i * 128:
                                       hh * TH + tb * 512 + (i + 1) * 128, :]
                              ).then_inc(S.h[qn], 16)
        S.c[qn] += 16

    def emit_loads0():
        @block.sync
        def _(sy):
            h = get_h(sy, "sync")
            for w in range(2):
                _po_dma(sy, "qs", w, 0, h)
            for i in range(4):
                _xr_dma(sy, "qs", i, 0, h)
        t_po_qs[0] = S.c["qs"]
        t_xr_qs[0] = S.c["qs"]
        t_xr_qo[0] = S.c["qo"]

    def emit_po(tb):
        @block.sync
        def _(sy, tb=tb):
            sy.wait_ge(S.h["sv"], t_accs[tb - 1])
            h = get_h(sy, "sync")
            _po_dma(sy, "qs", 0, tb, h)
        t_po_qs[tb] = S.c["qs"]

        @block.scalar
        def _(s, tb=tb):
            s.wait_ge(S.h["sv"], t_accs[tb - 1])
            h = get_h(s, "scalar")
            _po_dma(s, "qo", 1, tb, h)
        t_po_qo[tb] = S.c["qo"]

    def emit_xr(tb):
        @block.sync
        def _(sy, tb=tb):
            sy.wait_ge(S.h["sg"], t_oadd[(tb - 1) * 4 + 3])
            h = get_h(sy, "sync")
            for i in range(2):
                _xr_dma(sy, "qs", i, tb, h)
        t_xr_qs[tb] = S.c["qs"]

        @block.scalar
        def _(s, tb=tb):
            s.wait_ge(S.h["sg"], t_oadd[(tb - 1) * 4 + 3])
            h = get_h(s, "scalar")
            for i in range(2, 4):
                _xr_dma(s, "qo", i, tb, h)
        t_xr_qo[tb] = S.c["qo"]

    emit_loads0()
    for tb in range(nTB):
        @block.vector
        def _(v, tb=tb):
            v.wait_ge(S.h["qs"], t_po_qs[tb])
            if tb >= 1:
                v.wait_ge(S.h["qo"], t_po_qo[tb])
            for mt in range(8):
                dst = racc[:, mt * TH + tb * 512:mt * TH + (tb + 1) * 512]
                for w in range(2):
                    _li = v.tensor_tensor(
                        dst, dst,
                        pobuf[:, (w * 8 + mt) * 512:(w * 8 + mt + 1) * 512],
                        ALU.add)
            S.inc(_li, "sv")
        t_accs[tb] = S.c["sv"]
        if tb + 1 < nTB:
            emit_po(tb + 1)

        for tt in range(4):
            fi = len(t_fin)

            @block.tensor
            def _(t, tb=tb, tt=tt, fi=fi):
                t.wait_ge(S.h["sv"], t_accs[tb])
                if fi >= 2:
                    t.wait_ge(S.h["sa"], t_finev[fi - 2])
                for mt in range(8):
                    _li = t.transpose(
                        ps_ot[fi % 2][:, mt * P:(mt + 1) * P],
                        racc[:, mt * TH + tb * 512 + tt * P:
                             mt * TH + tb * 512 + (tt + 1) * P], idf[:])
                S.inc(_li, "sp")
            t_fin.append(S.c["sp"])

            @block.scalar
            def _(s, fi=fi):
                s.wait_ge(S.h["sp"], t_fin[fi])
                if fi >= 2:
                    s.wait_ge(S.h["sg"], t_oadd[fi - 2])
                    s.wait_ge(S.h["qo"], t_odma[fi - 2])
                _li = s.activation(out=otok[:, (fi % 2) * D:((fi % 2) + 1) * D],
                             in_=ps_ot[fi % 2][:], func=AF.Copy)
                S.inc(_li, "sa")
            t_finev.append(S.c["sa"])

            @block.gpsimd
            def _(g, tb=tb, tt=tt, fi=fi):
                g.wait_ge(S.h["sa"], t_finev[fi])
                g.wait_ge(S.h["qs"], t_xr_qs[tb])
                if tt >= 2:
                    g.wait_ge(S.h["qo"], t_xr_qo[tb])
                _li = g.tensor_tensor(otok[:, (fi % 2) * D:((fi % 2) + 1) * D],
                                otok[:, (fi % 2) * D:((fi % 2) + 1) * D],
                                xrl[:, tt * D:(tt + 1) * D], ALU.add)
                S.inc(_li, "sg")
            t_oadd.append(S.c["sg"])

            @block.scalar
            def _(s, tb=tb, tt=tt, fi=fi):
                s.wait_ge(S.h["sg"], t_oadd[fi])
                S.inc(s.dma_start(
                    out=dout[tb * 512 + tt * P:tb * 512 + (tt + 1) * P, :],
                    in_=otok[:, (fi % 2) * D:((fi % 2) + 1) * D]), "qo", 16)
            t_odma.append(S.c["qo"])
        if tb + 1 < nTB:
            emit_xr(tb + 1)

    QO_END = S.c["qo"]
    QS_END = S.c["qs"]
    QG_END = S.c["qg"]

    @block.gpsimd
    def _(g):
        g.wait_ge(S.h["qo"], QO_END)
        g.wait_ge(S.h["qs"], QS_END)
        g.wait_ge(S.h["qg"], QG_END)

    es_c3.close()
    es_c.close()
    es.close()
    return nc


_NC_CACHE = {}


def _get_nc(L):
    if L not in _NC_CACHE:
        _NC_CACHE[L] = build_program(L)
    return _NC_CACHE[L]


def make_in_maps(inputs, L):
    import ml_dtypes
    x = np.asarray(inputs["L0_attn_x"], np.float32)
    w_in = np.asarray(inputs["L0_attn_W_in"], np.float32)
    conv_w = np.asarray(inputs["L0_attn_conv_w"], np.float32)
    conv_b = np.asarray(inputs["L0_attn_conv_b"], np.float32)
    w_xp = np.asarray(inputs["L0_attn_W_xproj"], np.float32)
    w_dt = np.asarray(inputs["L0_attn_W_dt"], np.float32)
    b_dt = np.asarray(inputs["L0_attn_b_dt"], np.float32)
    a_log = np.asarray(inputs["L0_attn_A_log"], np.float32)
    d_p = np.asarray(inputs["L0_attn_Dp"], np.float32)
    w_out = np.asarray(inputs["L0_attn_W_out"], np.float32)
    r_w = np.asarray(inputs["L0_ffn_routing_W"], np.float32)
    e_b = np.asarray(inputs["L0_ffn_expert_bias"], np.float32)
    sW1 = np.asarray(inputs["L0_ffn_sW1"], np.float32)
    sb1 = np.asarray(inputs["L0_ffn_sb1"], np.float32)
    sW2 = np.asarray(inputs["L0_ffn_sW2"], np.float32)
    sb2 = np.asarray(inputs["L0_ffn_sb2"], np.float32)
    rW1 = np.asarray(inputs["L0_ffn_rW1"], np.float32)
    rb1 = np.asarray(inputs["L0_ffn_rb1"], np.float32)
    rW2 = np.asarray(inputs["L0_ffn_rW2"], np.float32)
    rb2 = np.asarray(inputs["L0_ffn_rb2"], np.float32)

    w1 = np.ascontiguousarray(np.concatenate([rW1, sW1[None]], 0))
    b1 = np.ascontiguousarray(np.concatenate([rb1, sb1[None]], 0))
    w2 = np.ascontiguousarray(np.concatenate([rW2, sW2[None]], 0))
    b2 = np.ascontiguousarray(np.concatenate([rb2, sb2[None]], 0))
    identf = np.eye(P, dtype=np.float32)
    identb = np.eye(P, dtype=np.float32).astype(ml_dtypes.bfloat16)
    e16m = np.zeros((NST, NST * P), np.float32)
    for n in range(NST):
        e16m[n, n * P:(n + 1) * P] = 1.0
    e16m = e16m.astype(ml_dtypes.bfloat16)
    e9m = np.zeros((NE + 1, (NE + 1) * P), np.float32)
    for n in range(NE + 1):
        e9m[n, n * P:(n + 1) * P] = 1.0
    e9m = e9m.astype(ml_dtypes.bfloat16)

    c = np.ascontiguousarray
    in_maps = []
    for core in range(8):
        b, h = core // 2, core % 2
        own = slice(h * HD, (h + 1) * HD)
        pair = slice((1 - h) * HD, (2 - h) * HD)
        in_maps.append({
            "x": c(x[b, :L]),
            "w_in": c(np.concatenate(
                [w_in[:, own], w_in[:, DI + h * HD:DI + (h + 1) * HD]], 1)),
            "conv_w": c(conv_w[own, 0, :]),
            "conv_b": c(conv_b[own].reshape(HD, 1)),
            "w_x2": c(np.concatenate(
                [w_xp[own, NST:2 * NST], w_xp[pair, NST:2 * NST]], 0)),
            "w_dt": c(np.concatenate([w_dt[own, own], w_dt[pair, own]], 0)),
            "b_dt": c(b_dt[own].reshape(HD, 1)),
            "a_log": c(a_log[own]),
            "d_p": c(d_p[own].reshape(HD, 1)),
            "w_out": c(w_out[own, :]),
            "r_w": c(r_w.T),
            "e_b": c(e_b.reshape(NE, 1)),
            "w1": w1, "b1": b1, "w2": w2, "b2": b2,
            "identf": identf, "identb": identb, "e16": e16m, "e9": e9m,
        })
    return in_maps


def kernel(**inputs):
    L = 4096
    in_maps = make_in_maps(inputs, L)
    nc = _get_nc(L)
    res = run_bass_kernel_spmd(nc, in_maps, core_ids=list(range(8)))
    out = np.empty((4, L, D), np.float32)
    for core in range(8):
        b, h = core // 2, core % 2
        out[b, h * (L // 2):(h + 1) * (L // 2), :] = res.results[core]["out"]
    return out


# version 13 (final)
# speedup vs baseline: 1.1983x; 1.0047x over previous
import numpy as np
from contextlib import ExitStack

import concourse.bass as bass
import concourse.mybir as mybir
from concourse.bass_utils import run_bass_kernel_spmd

F32 = mybir.dt.float32
BF16 = mybir.dt.bfloat16
AF = mybir.ActivationFunctionType
ALU = mybir.AluOpType

P = 128
D = 1024
DI = 2048
HD = 1024
NST = 16
NE = 8
HF = 512
TS = 512
PAIRS = [[0, 1], [2, 3], [4, 5], [6, 7]]


class Sems:
    def __init__(self, es, nc, names):
        self.h = {n: es.enter_context(nc.semaphore(n)) for n in names}
        self.c = {n: 0 for n in names}

    def inc(self, inst, name, k=1):
        inst.then_inc(self.h[name], k)
        self.c[name] += k
        return self.c[name]


def build_program(L, stop_after=None):
    nCH = L // 512
    nSU = L // TS
    TH = L // 2
    nTB = TH // 512

    nc = bass.Bass("TRN2", target_bir_lowering=False, num_devices=8,
                   detect_race_conditions=False)

    dxin = nc.dram_tensor("x", [L, D], F32, kind="ExternalInput").ap()
    dwin = nc.dram_tensor("w_in", [D, DI], F32, kind="ExternalInput").ap()
    dcw = nc.dram_tensor("conv_w", [HD, 4], F32, kind="ExternalInput").ap()
    dcb = nc.dram_tensor("conv_b", [HD, 1], F32, kind="ExternalInput").ap()
    dwx2 = nc.dram_tensor("w_x2", [DI, NST], F32, kind="ExternalInput").ap()
    dwdt = nc.dram_tensor("w_dt", [DI, HD], F32, kind="ExternalInput").ap()
    dbdt = nc.dram_tensor("b_dt", [HD, 1], F32, kind="ExternalInput").ap()
    dalog = nc.dram_tensor("a_log", [HD, NST], F32, kind="ExternalInput").ap()
    ddp = nc.dram_tensor("d_p", [HD, 1], F32, kind="ExternalInput").ap()
    dwout = nc.dram_tensor("w_out", [HD, D], F32, kind="ExternalInput").ap()
    drw = nc.dram_tensor("r_w", [D, NE], F32, kind="ExternalInput").ap()
    deb = nc.dram_tensor("e_b", [NE, 1], F32, kind="ExternalInput").ap()
    dw1 = nc.dram_tensor("w1", [NE + 1, D, HF], F32, kind="ExternalInput").ap()
    db1 = nc.dram_tensor("b1", [NE + 1, HF], F32, kind="ExternalInput").ap()
    dw2 = nc.dram_tensor("w2", [NE + 1, HF, D], F32, kind="ExternalInput").ap()
    db2 = nc.dram_tensor("b2", [NE + 1, D], F32, kind="ExternalInput").ap()
    didf = nc.dram_tensor("identf", [P, P], F32, kind="ExternalInput").ap()
    didb = nc.dram_tensor("identb", [P, P], BF16, kind="ExternalInput").ap()
    de16 = nc.dram_tensor("e16", [NST, NST * P], BF16, kind="ExternalInput").ap()
    de9 = nc.dram_tensor("e9", [NE + 1, (NE + 1) * P], BF16,
                         kind="ExternalInput").ap()
    dout = nc.dram_tensor("out", [TH, D], F32, kind="ExternalOutput").ap()

    shxc = nc.dram_tensor("sh_xc", [16, P, L], BF16, kind="Internal",
                          addr_space="Shared").ap()
    shpo = nc.dram_tensor("sh_po", [2, 8, P, L], BF16, kind="Internal",
                          addr_space="Shared").ap()
    pz = nc.dram_tensor("p_z", [8, P, L], BF16, kind="Internal").ap()
    pxnt = nc.dram_tensor("p_xnt", [8, P, L], BF16, kind="Internal").ap()
    bi1 = nc.dram_tensor("bar_i1", [1, 16], F32, kind="Internal").ap()
    bo1 = nc.dram_tensor("bar_o1", [1, 16], F32, kind="Internal").ap()
    bi2 = nc.dram_tensor("bar_i2", [1, 16], F32, kind="Internal").ap()
    bo2 = nc.dram_tensor("bar_o2", [1, 16], F32, kind="Internal").ap()

    es = ExitStack()

    def sb(name, shape, dtype):
        return es.enter_context(nc.sbuf_tensor(name, shape, dtype))

    S = Sems(es, nc, ["qs", "qg", "qo", "sa", "sv", "sp", "sg", "cc"])
    block = es.enter_context(nc.Block())
    hreg = {}

    def get_h(eng, ename):
        if ename not in hreg:
            hreg[ename] = eng.to_reg(eng.partition_id() % 2)
        return hreg[ename]

    wx2 = sb("wx2_s", [P, 16 * NST], BF16)
    cwv = sb("cwv", [P, 8 * 4], F32)
    cbv = sb("cbv", [P, 8], F32)
    bdtv = sb("bdtv", [P, 8], F32)
    dpv = sb("dpv", [P, 8], F32)
    alog = sb("alog", [P, 8 * NST], F32)
    aneg = sb("aneg", [P, 8 * NST], F32)
    idf = sb("idf", [P, P], F32)
    idb = sb("idb", [P, P], BF16)
    e16 = sb("e16s", [NST, NST * P], BF16)
    e9 = sb("e9s", [NE + 1, (NE + 1) * P], BF16)
    cst = sb("cst", [P, 8], F32)
    zb = sb("zb", [1, 16], F32)
    C_EPS, C_MH, C_ONE, C_MONE, C_HALF, C_TINY = range(6)
    carry = sb("carry", [P, NST * 8], F32)
    ebv = sb("ebv", [NE, 1], F32)
    b1v = sb("b1v", [P, (NE + 1) * 4], F32)

    def cc_(i):
        return cst[:, i:i + 1]

    es_mw = ExitStack()
    es_wi = ExitStack()
    wdt = es_mw.enter_context(nc.sbuf_tensor("wdt_s", [P, 16 * HD], BF16))
    wout = es_mw.enter_context(nc.sbuf_tensor("wout_s", [P, 8 * D], BF16))
    win = es_wi.enter_context(nc.sbuf_tensor("win_s", [P, 8 * DI], BF16))

    def win_t(kt, mt):
        return win[:, kt * DI + mt * P:kt * DI + (mt + 1) * P]

    def wdt_t(kt, mt):
        return wdt[:, kt * HD + mt * P:kt * HD + (mt + 1) * P]

    def wout_t(kt, mt):
        return wout[:, kt * D + mt * P:kt * D + (mt + 1) * P]

    @block.gpsimd
    def _(g):
        S.inc(g.dma_start(out=win[:].rearrange("p (k c) -> p k c", k=8),
                          in_=dwin.rearrange("(k p) c -> p k c", p=P)), "qg", 16)
        S.inc(g.dma_start(out=wdt[:].rearrange("p (k c) -> p k c", k=16),
                          in_=dwdt.rearrange("(k p) c -> p k c", p=P)), "qg", 16)
        S.inc(g.dma_start(out=wout[:].rearrange("p (k c) -> p k c", k=8),
                          in_=dwout.rearrange("(k p) c -> p k c", p=P)), "qg", 16)
        S.inc(g.dma_start(out=wx2[:].rearrange("p (k n) -> p k n", k=16),
                          in_=dwx2.rearrange("(k p) n -> p k n", p=P)), "qg", 16)
        S.inc(g.dma_start(out=idb[:], in_=didb), "qg", 16)
        S.inc(g.dma_start(out=e16[:], in_=de16), "qg", 16)
        S.inc(g.dma_start(out=e9[:], in_=de9), "qg", 16)
        g.memset(zb[:], 0.0)
        g.memset(cst[:, C_EPS:C_EPS + 1], 1e-5)
        g.memset(cst[:, C_MH:C_MH + 1], -0.5)
        g.memset(cst[:, C_ONE:C_ONE + 1], 1.0)
        g.memset(cst[:, C_MONE:C_MONE + 1], -1.0)
        g.memset(cst[:, C_HALF:C_HALF + 1], 0.5)
        S.inc(g.memset(cst[:, C_TINY:C_TINY + 1], 1e-30), "sg", 1)
        g.wait_ge(S.h["sg"], 1)
        S.inc(g.dma_start(out=bi1, in_=zb[0:1, :]), "qg", 16)
        S.inc(g.dma_start(out=bi2, in_=zb[0:1, :]), "qg", 16)
    QG_W = S.c["qg"]

    @block.sync
    def _(sy):
        S.inc(sy.dma_start(out=cwv[:].rearrange("p (d f) -> p d f", d=8),
                           in_=dcw.rearrange("(d p) f -> p d f", p=P)), "qs", 16)
        S.inc(sy.dma_start(out=cbv[:],
                           in_=dcb.rearrange("(d p) o -> p (d o)", p=P), allow_slow_non_contiguous=True), "qs", 16)
        S.inc(sy.dma_start(out=bdtv[:],
                           in_=dbdt.rearrange("(d p) o -> p (d o)", p=P), allow_slow_non_contiguous=True), "qs", 16)
        S.inc(sy.dma_start(out=dpv[:],
                           in_=ddp.rearrange("(d p) o -> p (d o)", p=P), allow_slow_non_contiguous=True), "qs", 16)
        S.inc(sy.dma_start(out=alog[:].rearrange("p (d n) -> p d n", d=8),
                           in_=dalog.rearrange("(d p) n -> p d n", p=P)), "qs", 16)
        S.inc(sy.dma_start(out=idf[:], in_=didf), "qs", 16)
        S.inc(sy.dma_start(out=ebv[:], in_=deb), "qs", 16)
        S.inc(sy.dma_start(out=b1v[:].rearrange("p (e m) -> p e m", e=NE + 1),
                           in_=db1.rearrange("e (m p) -> p e m", p=P), allow_slow_non_contiguous=True), "qs", 16)
    QS_SETUP = S.c["qs"]

    @block.scalar
    def _(s):
        s.wait_ge(S.h["qs"], QS_SETUP)
        s.wait_ge(S.h["sg"], 1)
        s.activation(out=cst[:, 6:7], in_=cst[:, C_ONE:C_ONE + 1], func=AF.Ln)
        s.activation(out=cst[:, 6:7], in_=cst[:, C_ONE:C_ONE + 1], func=AF.Silu)
        S.inc(s.activation(out=aneg[:], in_=alog[:], func=AF.Exp), "sa")
    t_aexp = S.c["sa"]

    @block.vector
    def _(v):
        v.wait_ge(S.h["sa"], t_aexp)
        S.inc(v.tensor_scalar_mul(aneg[:], aneg[:], -1.0), "sv")
    T_ANEG = S.c["sv"]

    if stop_after == "S":
        qs_f, qg_f = S.c["qs"], S.c["qg"]

        @block.gpsimd
        def _(g):
            g.wait_ge(S.h["qs"], qs_f)
            g.wait_ge(S.h["qg"], qg_f)
            g.wait_ge(S.h["sv"], T_ANEG)
        es_mw.close()
        es.close()
        return nc

    es_a = ExitStack()

    def sba(name, shape, dtype):
        return es_a.enter_context(nc.sbuf_tensor(name, shape, dtype))

    xa = sba("xa", [P, 2 * 4 * D], BF16)
    stats = sba("stats", [P, 48], F32)
    mvb = sba("mvb", [P, 8], F32)
    istd = sba("istd", [P, 2 * 4], F32)
    negm = sba("negm", [P, 2 * 4], F32)
    xn = sba("xn", [P, 2 * 4 * D], BF16)
    xnt = sba("xnt", [P, 2 * 8 * 512], BF16)
    xcp = sba("xcp", [P, 2 * 8 * 515], BF16)
    cvt = sba("cvt", [P, 8 * 512], BF16)
    cvtm = sba("cvtm", [P, 512], BF16)
    xco = sba("xco", [P, 2 * 8 * 512], BF16)
    zso = sba("zso", [P, 2 * 8 * 512], BF16)
    ps_tr = [es_a.enter_context(nc.psum_tensor(f"ps_tr{i}", [P, 512], BF16))
             for i in range(4)]
    ps_xz = [es_a.enter_context(nc.psum_tensor(f"ps_xz{i}", [P, 512], F32))
             for i in range(4)]

    t_ln = [0] * nCH
    t_tr = [0] * nCH
    t_mm = [0] * nCH
    t_cv = [0] * nCH
    t_sil = [0] * nCH
    t_silq = [0] * nCH
    t_trg, t_trgev, t_mmg = [], [], []
    ec_tok = {}


    def a_views(k):
        pb = k % 2
        return dict(
            pb=pb,
            xa_v=xa[:, pb * 4 * D:(pb + 1) * 4 * D],
            xn_v=xn[:, pb * 4 * D:(pb + 1) * 4 * D],
            xnt_v=xnt[:, pb * 8 * 512:(pb + 1) * 8 * 512],
            xcp_v=xcp[:, pb * 8 * 515:(pb + 1) * 8 * 515],
            xcp_p=xcp[:, (1 - pb) * 8 * 515:(2 - pb) * 8 * 515],
            cvt_v=cvt[:, 0:8 * 512],
            xco_v=xco[:, pb * 8 * 512:(pb + 1) * 8 * 512],
            zso_v=zso[:, pb * 8 * 512:(pb + 1) * 8 * 512],
            istd_v=istd[:, pb * 4:(pb + 1) * 4],
            negm_v=negm[:, pb * 4:(pb + 1) * 4])

    def emit_load(k):
        vw = a_views(k)

        @block.gpsimd
        def _(g, k=k, xa_v=vw["xa_v"]):
            if k >= 2:
                g.wait_ge(S.h["sv"], t_ln[k - 2])
            for i in range(4):
                t0 = k * 512 + i * 128
                S.inc(g.dma_start(out=xa_v[:, i * D:(i + 1) * D],
                                  in_=dxin[t0:t0 + 128, :]), "qg", 16)
        return S.c["qg"]

    def emit_stats(k, t_ldx):
        vw = a_views(k)

        @block.vector
        def _(v, t_ldx=t_ldx, xa_v=vw["xa_v"], pb=vw["pb"]):
            v.wait_ge(S.h["qg"], t_ldx)
            for i in range(4):
                v.bn_stats(stats[:, i * 12:i * 12 + 6],
                           xa_v[:, i * D:i * D + 512])
                _li = v.bn_stats(stats[:, i * 12 + 6:i * 12 + 12],
                                 xa_v[:, i * D + 512:(i + 1) * D])
            S.inc(_li, "sv")
            v.wait_ge(S.h["sv"], S.c["sv"])
            for i in range(4):
                _li = v.bn_aggr(mvb[:, 2 * i:2 * i + 2],
                                stats[:, i * 12:(i + 1) * 12])
            S.inc(_li, "sv")
            v.wait_ge(S.h["sv"], S.c["sv"])
            for i in range(4):
                v.tensor_copy(istd[:, pb * 4 + i:pb * 4 + i + 1],
                              mvb[:, 2 * i + 1:2 * i + 2])
                _li = v.tensor_copy(negm[:, pb * 4 + i:pb * 4 + i + 1],
                                    mvb[:, 2 * i:2 * i + 1])
            S.inc(_li, "sv")
        return S.c["sv"]

    def emit_istd(k, t_bn):
        vw = a_views(k)

        @block.scalar
        def _(s, t_bn=t_bn, istd_v=vw["istd_v"]):
            s.wait_ge(S.h["sv"], t_bn)
            for i in range(4):
                s.activation(out=istd_v[:, i:i + 1], in_=istd_v[:, i:i + 1],
                             func=AF.Ln, bias=cc_(C_EPS))
                _li = s.activation(out=istd_v[:, i:i + 1],
                                   in_=istd_v[:, i:i + 1],
                                   func=AF.Exp, scale=cc_(C_MH))
            S.inc(_li, "sa")
        return S.c["sa"]

    def emit_norm(k, t_istd):
        vw = a_views(k)

        @block.vector
        def _(v, k=k, t_istd=t_istd, xa_v=vw["xa_v"], xn_v=vw["xn_v"],
              istd_v=vw["istd_v"], negm_v=vw["negm_v"]):
            v.wait_ge(S.h["sa"], t_istd)
            if k >= 2:
                v.wait_ge(S.h["sp"], t_tr[k - 2])
            for i in range(4):
                _li = v.tensor_scalar(out=negm_v[:, i:i + 1],
                                      in0=negm_v[:, i:i + 1],
                                      scalar1=istd_v[:, i:i + 1],
                                      scalar2=-1.0,
                                      op0=ALU.mult, op1=ALU.mult)
            S.inc(_li, "sv")
            v.wait_ge(S.h["sv"], S.c["sv"])
            for i in range(4):
                _li = v.tensor_scalar(out=xn_v[:, i * D:(i + 1) * D],
                                in0=xa_v[:, i * D:(i + 1) * D],
                                scalar1=negm_v[:, i:i + 1],
                                scalar2=istd_v[:, i:i + 1],
                                op0=ALU.add, op1=ALU.mult)
            S.inc(_li, "sv")
        t_ln[k] = S.c["sv"]

    def emit_pe(k):
        vw = a_views(k)
        ec_base = S.c["sv"] + (3 if k < nCH - 1 else 0)
        sa_off = 1 if k < nCH - 1 else 0
        tok = dict(trev=[ec_base + 1 + g for g in range(4)],
                   xzev=[ec_base + 5 + g for g in range(4)],
                   cv=ec_base + 9,
                   zsil=[S.c["sa"] + sa_off + 1 + g for g in range(4)],
                   silcv=S.c["sa"] + sa_off + 5)
        ec_tok[k] = tok

        for g_ in range(4):
            @block.tensor
            def _(t, k=k, g_=g_, xn_v=vw["xn_v"]):
                if g_ == 0:
                    t.wait_ge(S.h["sv"], t_ln[k])
                    if k == 0:
                        t.wait_ge(S.h["qg"], QG_W)
                if g_ >= 2:
                    t.wait_ge(S.h["sv"], tok["trev"][g_ - 2])
                elif k >= 1:
                    t.wait_ge(S.h["sv"], ec_tok[k - 1]["trev"][g_ + 2])
                for j in range(2):
                    dt = g_ * 2 + j
                    pt = ps_tr[(g_ % 2) * 2 + j]
                    for i in range(4):
                        _li = t.transpose(pt[:, i * P:(i + 1) * P],
                                    xn_v[:, i * D + dt * P:i * D + (dt + 1) * P],
                                    idb[:])
                S.inc(_li, "sp")
            t_trg.append(S.c["sp"])
        t_tr[k] = S.c["sp"]

        for g_ in range(8):
            @block.tensor
            def _(t, k=k, g_=g_, xnt_v=vw["xnt_v"]):
                if g_ == 0:
                    t.wait_ge(S.h["sv"], tok["trev"][3])
                    if k == 0:
                        t.wait_ge(S.h["sv"], T_ANEG)
                    if k >= 2:
                        t.wait_ge(S.h["sa"], t_sil[k - 2])
                    if k >= 1:
                        t.wait_ge(S.h["sv"], ec_tok[k - 1]["cv"])
                if g_ in (2, 3, 4, 5):
                    t.wait_ge(S.h["sv"], tok["xzev"][g_ - 2])
                elif g_ in (6, 7):
                    t.wait_ge(S.h["sa"], tok["zsil"][g_ - 6])
                elif k >= 1:
                    t.wait_ge(S.h["sa"], ec_tok[k - 1]["zsil"][g_ + 2])
                for j in range(2):
                    mt = g_ * 2 + j
                    pj = ps_xz[(g_ % 2) * 2 + j]
                    for kt in range(8):
                        _li = t.matmul(pj[:], win_t(kt, mt),
                                 xnt_v[:, kt * 512:(kt + 1) * 512],
                                 start=(kt == 0), stop=(kt == 7))
                S.inc(_li, "sp")
            t_mmg.append(S.c["sp"])
        t_mm[k] = S.c["sp"]

    def emit_ec(k):
        vw = a_views(k)
        for g_ in range(4):
            @block.vector
            def _(v, k=k, g_=g_, xnt_v=vw["xnt_v"]):
                v.wait_ge(S.h["sp"], t_trg[k * 4 + g_])
                if k >= 2 and g_ == 0:
                    v.wait_ge(S.h["qo"], t_silq[k - 2])
                    v.wait_ge(S.h["sp"], t_mm[k - 2])
                for j in range(2):
                    dt = g_ * 2 + j
                    _li = v.tensor_copy(xnt_v[:, dt * 512:(dt + 1) * 512],
                                        ps_tr[(g_ % 2) * 2 + j][:])
                S.inc(_li, "sv")
            t_trgev.append(S.c["sv"])
        for g_ in range(4):
            @block.vector
            def _(v, k=k, g_=g_, xcp_v=vw["xcp_v"]):
                v.wait_ge(S.h["sp"], t_mmg[k * 8 + g_])
                for j in range(2):
                    mt = g_ * 2 + j
                    pj = ps_xz[(g_ % 2) * 2 + j]
                    _li = v.tensor_copy(
                        xcp_v[:, mt * 515 + 3:mt * 515 + 515], pj[:])
                S.inc(_li, "sv")

        @block.vector
        def _(v, k=k, xcp_v=vw["xcp_v"], xcp_p=vw["xcp_p"],
              cvt_v=vw["cvt_v"]):
            if k >= 1:
                v.wait_ge(S.h["sa"], t_sil[k - 1])
            for dt in range(8):
                if k == 0:
                    v.memset(xcp_v[:, dt * 515:dt * 515 + 3], 0.0)
                else:
                    v.tensor_copy(xcp_v[:, dt * 515:dt * 515 + 3],
                                  xcp_p[:, dt * 515 + 512:dt * 515 + 515])
                cv = cvt_v[:, dt * 512:(dt + 1) * 512]
                v.tensor_scalar_mul(cv, xcp_v[:, dt * 515:dt * 515 + 512],
                                    cwv[:, dt * 4:dt * 4 + 1])
                for j in range(1, 4):
                    v.tensor_scalar_mul(cvtm[:],
                                        xcp_v[:, dt * 515 + j:dt * 515 + j + 512],
                                        cwv[:, dt * 4 + j:dt * 4 + j + 1])
                    _li = v.tensor_tensor(cv, cv, cvtm[:], ALU.add)
            S.inc(_li, "sv")
        t_cv[k] = S.c["sv"]
        assert t_cv[k] == ec_tok[k]["cv"], (k, t_cv[k], ec_tok[k]["cv"])
        assert t_trgev[k * 4:k * 4 + 4] == ec_tok[k]["trev"], k

    def emit_sil(k):
        vw = a_views(k)

        @block.scalar
        def _(s, k=k, xnt_v=vw["xnt_v"]):
            s.wait_ge(S.h["sv"], t_trgev[k * 4 + 3])
            S.inc(s.dma_start(
                out=pxnt.rearrange("d p l -> p d l")[:, :, k * 512:(k + 1) * 512],
                in_=xnt_v[:].rearrange("p (d t) -> p d t", d=8)), "qo", 16)

        for g_ in range(4, 8):
            @block.scalar
            def _(s, k=k, g_=g_, zso_v=vw["zso_v"]):
                s.wait_ge(S.h["sp"], t_mmg[k * 8 + g_])
                for j in range(2):
                    z = g_ * 2 + j - 8
                    pj = ps_xz[(g_ % 2) * 2 + j]
                    _li = s.activation(out=zso_v[:, z * 512:(z + 1) * 512],
                                 in_=pj[:], func=AF.Silu)
                S.inc(_li, "sa")
        assert S.c["sa"] == ec_tok[k]["zsil"][3], k

        @block.scalar
        def _(s, k=k, zso_v=vw["zso_v"]):
            s.wait_ge(S.h["sa"], S.c["sa"])
            S.inc(s.dma_start(
                out=pz.rearrange("d p l -> p d l")[:, :, k * 512:(k + 1) * 512],
                in_=zso_v[:].rearrange("p (d t) -> p d t", d=8)), "qo", 16)

        @block.scalar
        def _(s, k=k, cvt_v=vw["cvt_v"], xco_v=vw["xco_v"]):
            s.wait_ge(S.h["sv"], t_cv[k])
            if k >= 2:
                s.wait_ge(S.h["qo"], t_silq[k - 2])
            for dt in range(8):
                _li = s.activation(out=xco_v[:, dt * 512:(dt + 1) * 512],
                                   in_=cvt_v[:, dt * 512:(dt + 1) * 512],
                                   func=AF.Silu, bias=cbv[:, dt:dt + 1])
            S.inc(_li, "sa")
            s.wait_ge(S.h["sa"], S.c["sa"])
            h = get_h(s, "scalar")
            src = xco_v[:].rearrange("p (d t) -> p d t", d=8)
            with s.If_eq(h, 0):
                s.dma_start(out=shxc.rearrange("d p l -> p d l")[
                    :, 0:8, k * 512:(k + 1) * 512],
                    in_=src).then_inc(S.h["qo"], 16)
            with s.Else():
                s.dma_start(out=shxc.rearrange("d p l -> p d l")[
                    :, 8:16, k * 512:(k + 1) * 512],
                    in_=src).then_inc(S.h["qo"], 16)
            S.c["qo"] += 16
        t_sil[k] = S.c["sa"]
        t_silq[k] = S.c["qo"]
        assert t_sil[k] == ec_tok[k]["silcv"], k

    for k in range(nCH):
        t_ldx = emit_load(k)
        t_bn = emit_stats(k, t_ldx)
        if k >= 1:
            emit_ec(k - 1)
        t_istd = emit_istd(k, t_bn)
        if k >= 1:
            emit_sil(k - 1)
        emit_norm(k, t_istd)
        emit_pe(k)
    emit_ec(nCH - 1)
    emit_sil(nCH - 1)

    QO_A = S.c["qo"]
    QS_A = S.c["qs"]

    @block.gpsimd
    def _(g):
        g.wait_ge(S.h["qo"], QO_A)
        g.wait_ge(S.h["qs"], QS_A)
        g.wait_ge(S.h["sa"], t_sil[nCH - 1])
        g.wait_ge(S.h["qg"], QG_W)
        S.inc(g.collective_compute(
            "AllReduce", ALU.add, replica_groups=PAIRS,
            ins=[bi1], outs=[bo1]), "cc", 1)
    T_BAR1 = S.c["cc"]

    if stop_after == "A":
        dxc = nc.dram_tensor("dbg_xc", [16, P, L], F32, kind="ExternalOutput").ap()
        dzz = nc.dram_tensor("dbg_z", [8, P, L], F32, kind="ExternalOutput").ap()
        dxn = nc.dram_tensor("dbg_xnt", [8, P, L], F32, kind="ExternalOutput").ap()
        dst_ = nc.dram_tensor("dbg_stats", [P, 32], F32, kind="ExternalOutput").ap()
        dxa = nc.dram_tensor("dbg_xa", [P, 4 * D], F32, kind="ExternalOutput").ap()
        qo_f, qs_f, qg_f = S.c["qo"], S.c["qs"], S.c["qg"]

        @block.gpsimd
        def _(g):
            g.wait_ge(S.h["qo"], qo_f)
            g.wait_ge(S.h["qs"], qs_f)
            g.wait_ge(S.h["qg"], qg_f)
            g.wait_ge(S.h["cc"], T_BAR1)
            g.dma_start(out=dxc, in_=shxc).then_inc(S.h["qg"], 16)
            g.dma_start(out=dzz, in_=pz).then_inc(S.h["qg"], 16)
            g.dma_start(out=dxn, in_=pxnt).then_inc(S.h["qg"], 16)
            g.dma_start(out=dst_[:, 0:8], in_=istd[:]).then_inc(S.h["qg"], 16)
            g.dma_start(out=dst_[:, 8:16], in_=negm[:]).then_inc(S.h["qg"], 16)
            g.dma_start(out=dxa, in_=xa[:, 0:4 * D]).then_inc(S.h["qg"], 16)
            g.wait_ge(S.h["qg"], qg_f + 96)
        es_a.close()
        es_wi.close()
        es_mw.close()
        es.close()
        return nc

    es_a.close()
    es_wi.close()

    es_b = ExitStack()

    def sbb(name, shape, dtype):
        return es_b.enter_context(nc.sbuf_tensor(name, shape, dtype))

    xcs2 = sbb("xcs2", [P, 2 * 16 * TS], BF16)
    zss2 = sbb("zss2", [P, 8 * TS], BF16)
    dlt = sbb("dlt", [P, 8 * TS], BF16)
    dxb = sbb("dxb", [P, 8 * TS], BF16)
    dab = sbb("dab", [P, 2 * 8 * TS], BF16)
    ubf = sbb("ubf", [P, TS], BF16)
    sbf = sbb("sbf", [P, TS], BF16)
    pbf2 = sbb("pbf2", [P, TS], BF16)
    yac = sbb("yac", [P, 8 * TS], BF16)
    yfn2 = sbb("yfn2", [P, 2 * 8 * TS], BF16)
    bsb = sbb("bsb", [NST, TS], BF16)
    bbc = sbb("bbc", [P, NST * TS], BF16)
    pre = sbb("pre", [P, 2 * TS], F32)
    spa = sbb("spa", [P, TS], F32)
    spr = sbb("spr", [P, 2 * TS], BF16)
    post = sbb("post", [P, 8 * TS], BF16)
    ps_dt = [es_b.enter_context(nc.psum_tensor(f"ps_dt{i}", [P, TS], F32))
             for i in range(2)]
    ps_bt = es_b.enter_context(nc.psum_tensor("ps_bt", [NST, TS], F32))
    ps_bb = es_b.enter_context(nc.psum_tensor("ps_bb", [P, TS], F32))
    ps_wo = [es_b.enter_context(nc.psum_tensor(f"ps_wo{i}", [P, TS], F32))
             for i in range(2)]

    t_ldb = [0] * nSU
    t_ldz = [0] * nSU
    t_powq = [0] * nSU
    prev = None

    prev2 = None
    for s_ in range(nSU):
        t0 = s_ * TS
        par = s_ % 2
        xcs = xcs2[:, par * 16 * TS:(par + 1) * 16 * TS]
        zss = zss2[:, 0:8 * TS]
        yfn = yfn2[:, par * 8 * TS:(par + 1) * 8 * TS]
        yfn_p = yfn2[:, (1 - par) * 8 * TS:(2 - par) * 8 * TS]
        sp0, sa0, sv0 = S.c["sp"], S.c["sa"], S.c["sv"]

        has_wo = s_ >= 1
        SP_WX2 = sp0 + 1
        SP_WDT = [SP_WX2 + 1 + m for m in range(8)]
        SP_BB = [SP_WX2 + 9 + n for n in range(16)]
        SP_WO = [SP_WX2 + 25 + m for m in range(8)] if has_wo else []
        SA_BSB = sa0 + 1
        SA_SP = [SA_BSB + 1 + m for m in range(8)]
        SA_DA = [SA_BSB + 9 + n for n in range(16)]
        SA_WOEV = [SA_BSB + 25 + m for m in range(8)] if has_wo else []
        seq = []
        for m in range(8):
            if m >= 2:
                seq.append(("d", m - 2))
            seq.append(("p", m))
        seq += [("d", 6), ("d", 7)]
        SV_PRE, SV_DLT = {}, {}
        for i, (kind, m) in enumerate(seq):
            (SV_PRE if kind == "p" else SV_DLT)[m] = sv0 + i + 1
        SV_DX = sv0 + 17
        SV_BBEV = [SV_DX + 1 + n for n in range(16)]
        SV_SCAN = [SV_DX + 17 + n for n in range(16)]
        SV_YFN = SV_DX + 33

        @block.sync
        def _(sy, s_=s_, t0=t0, xcs=xcs, zss=zss):
            if s_ == 0:
                sy.wait_ge(S.h["qo"], QO_A)
            if s_ >= 2:
                sy.wait_ge(S.h["sv"], prev2["SV_YFN"])
                sy.wait_ge(S.h["sp"], prev2["SP_WDT7"])
            h = get_h(sy, "sync")
            xv = shxc.rearrange("d p l -> p d l")
            with sy.If_eq(h, 0):
                sy.dma_start(out=xcs[:, 0:8 * TS].rearrange(
                    "p (d t) -> p d t", d=8),
                    in_=xv[:, 0:8, t0:t0 + TS]).then_inc(S.h["qs"], 16)
            with sy.Else():
                sy.dma_start(out=xcs[:, 0:8 * TS].rearrange(
                    "p (d t) -> p d t", d=8),
                    in_=xv[:, 8:16, t0:t0 + TS]).then_inc(S.h["qs"], 16)
            S.c["qs"] += 16
            if s_ == 0:
                sy.wait_ge(S.h["cc"], T_BAR1)
            with sy.If_eq(h, 0):
                sy.dma_start(out=xcs[:, 8 * TS:16 * TS].rearrange(
                    "p (d t) -> p d t", d=8),
                    in_=xv[:, 8:16, t0:t0 + TS]).then_inc(S.h["qs"], 16)
            with sy.Else():
                sy.dma_start(out=xcs[:, 8 * TS:16 * TS].rearrange(
                    "p (d t) -> p d t", d=8),
                    in_=xv[:, 0:8, t0:t0 + TS]).then_inc(S.h["qs"], 16)
            S.c["qs"] += 16
        t_ldb[s_] = S.c["qs"]

        @block.sync
        def _(sy, s_=s_, t0=t0, zss=zss, prev=prev):
            if s_ >= 1:
                sy.wait_ge(S.h["sv"], prev["SV_YFN"])
            S.inc(sy.dma_start(out=zss[:].rearrange("p (d t) -> p d t", d=8),
                               in_=pz.rearrange("d p l -> p d l")[
                                   :, :, t0:t0 + TS]), "qs", 16)
        t_ldz[s_] = S.c["qs"]

        @block.tensor
        def _(t, s_=s_, SP_WDT=SP_WDT, SV_PRE=SV_PRE, SV_BBEV=SV_BBEV,
              SV_YFN=SV_YFN, SA_BSB=SA_BSB, SA_WOEV=SA_WOEV, SA_DA=SA_DA,
              prev=prev, xcs=xcs, yfn=yfn, yfn_p=yfn_p):
            t.wait_ge(S.h["qs"], t_ldb[s_])
            if s_ == 0:
                t.wait_ge(S.h["cc"], T_BAR1)
                t.wait_ge(S.h["qg"], QG_W)
            else:
                t.wait_ge(S.h["sa"], prev["SA_BSB"])
            for kt in range(16):
                _li = t.matmul(ps_bt[:], wx2[:, kt * NST:(kt + 1) * NST],
                               xcs[:, kt * TS:(kt + 1) * TS],
                               start=(kt == 0), stop=(kt == 15))
            S.inc(_li, "sp")
            for mt in range(8):
                if mt >= 2:
                    t.wait_ge(S.h["sv"], SV_PRE[mt - 2])
                if mt == 0 and s_ >= 1:
                    t.wait_ge(S.h["sa"], prev["SA_DA15"])
                for kt in range(16):
                    _li = t.matmul(ps_dt[mt % 2][:], wdt_t(kt, mt),
                                   xcs[:, kt * TS:(kt + 1) * TS],
                                   start=(kt == 0), stop=(kt == 15))
                S.inc(_li, "sp")
            for n in range(NST):
                if n == 0:
                    t.wait_ge(S.h["sa"], SA_BSB)
                else:
                    t.wait_ge(S.h["sv"], SV_BBEV[n - 1])
                _li = t.matmul(ps_bb[:], e16[:, n * P:(n + 1) * P], bsb[:],
                               start=True, stop=True)
                S.inc(_li, "sp")
            if s_ >= 1:
                for mt in range(8):
                    if mt == 0:
                        t.wait_ge(S.h["sv"], prev["SV_YFN"])
                        if s_ >= 2:
                            t.wait_ge(S.h["sa"], prev["SA_WOEV7"])
                    if mt >= 2:
                        t.wait_ge(S.h["sa"], SA_WOEV[mt - 2])
                    for kt in range(8):
                        _li = t.matmul(ps_wo[mt % 2][:], wout_t(kt, mt),
                                       yfn_p[:, kt * TS:(kt + 1) * TS],
                                       start=(kt == 0), stop=(kt == 7))
                    S.inc(_li, "sp")

        @block.scalar
        def _(s, s_=s_, t0=t0, SP_WX2=SP_WX2, SP_WDT=SP_WDT, SP_BB=SP_BB,
              SP_WO=SP_WO, SV_PRE=SV_PRE, SV_DLT=SV_DLT, SV_DX=SV_DX,
              SV_SCAN=SV_SCAN, SA_WOEV=SA_WOEV, prev=prev):
            s.wait_ge(S.h["sp"], SP_WX2)
            if s_ >= 1:
                s.wait_ge(S.h["sp"], prev["SP_BB15"])
            _li = s.activation(out=bsb[:], in_=ps_bt[:], func=AF.Copy)
            S.inc(_li, "sa")
            for mt in range(8):
                s.wait_ge(S.h["sv"], SV_PRE[mt])
                if mt >= 2:
                    s.wait_ge(S.h["sv"], SV_DLT[mt - 2])
                pv = pre[:, (mt % 2) * TS:(mt % 2 + 1) * TS]
                s.activation(out=spa[:], in_=pv, func=AF.Abs)
                s.activation(out=spa[:], in_=spa[:], func=AF.Exp,
                             scale=cc_(C_MONE))
                _li = s.activation(out=spr[:, (mt % 2) * TS:(mt % 2 + 1) * TS],
                                   in_=spa[:], func=AF.Ln, bias=cc_(C_ONE))
                S.inc(_li, "sa")
            for n in range(NST):
                if n == 0:
                    s.wait_ge(S.h["sv"], SV_DX)
                elif n >= 2:
                    s.wait_ge(S.h["sv"], SV_SCAN[n - 2])
                np_ = n % 2
                for dt in range(8):
                    _li = s.activation(
                        out=dab[:, (np_ * 8 + dt) * TS:(np_ * 8 + dt + 1) * TS],
                        in_=dlt[:, dt * TS:(dt + 1) * TS], func=AF.Exp,
                        scale=aneg[:, dt * NST + n:dt * NST + n + 1])
                S.inc(_li, "sa")
            if s_ >= 1:
                for mt in range(8):
                    s.wait_ge(S.h["sp"], SP_WO[mt])
                    if s_ >= 2 and mt == 0:
                        s.wait_ge(S.h["qo"], t_powq[s_ - 1])
                    _li = s.activation(out=post[:, mt * TS:(mt + 1) * TS],
                                       in_=ps_wo[mt % 2][:], func=AF.Copy)
                    S.inc(_li, "sa")
                s.wait_ge(S.h["sa"], S.c["sa"])
                h = get_h(s, "scalar")
                srcp = post[:].rearrange("p (d t) -> p d t", d=8)
                tp0 = (s_ - 1) * TS
                with s.If_eq(h, 0):
                    s.dma_start(out=shpo.rearrange("w d p l -> p w d l")[
                        :, 0, :, tp0:tp0 + TS], in_=srcp).then_inc(S.h["qo"], 16)
                with s.Else():
                    s.dma_start(out=shpo.rearrange("w d p l -> p w d l")[
                        :, 1, :, tp0:tp0 + TS], in_=srcp).then_inc(S.h["qo"], 16)
                S.c["qo"] += 16
        t_powq[s_] = S.c["qo"]

        @block.vector
        def _(v, s_=s_, seq=seq, SP_WDT=SP_WDT, SP_BB=SP_BB, SA_SP=SA_SP,
              SA_DA=SA_DA, prev=prev, xcs=xcs, zss=zss, yfn=yfn):
            for kind, m in seq:
                if kind == "p":
                    v.wait_ge(S.h["sp"], SP_WDT[m])
                    if m >= 2:
                        v.wait_ge(S.h["sa"], SA_SP[m - 2])
                    _li = v.tensor_scalar(
                        out=pre[:, (m % 2) * TS:(m % 2 + 1) * TS],
                        in0=ps_dt[m % 2][:], scalar1=bdtv[:, m:m + 1],
                        scalar2=None, op0=ALU.add)
                    S.inc(_li, "sv")
                else:
                    v.wait_ge(S.h["sa"], SA_SP[m])
                    pv = pre[:, (m % 2) * TS:(m % 2 + 1) * TS]
                    dv = dlt[:, m * TS:(m + 1) * TS]
                    v.tensor_relu(dv, pv)
                    _li = v.tensor_tensor(
                        dv, dv, spr[:, (m % 2) * TS:(m % 2 + 1) * TS], ALU.add)
                    S.inc(_li, "sv")
            for dt in range(8):
                _li = v.tensor_tensor(dxb[:, dt * TS:(dt + 1) * TS],
                                      dlt[:, dt * TS:(dt + 1) * TS],
                                      xcs[:, dt * TS:(dt + 1) * TS], ALU.mult)
            S.inc(_li, "sv")
            for n in range(NST):
                v.wait_ge(S.h["sp"], SP_BB[n])
                _li = v.tensor_copy(bbc[:, n * TS:(n + 1) * TS], ps_bb[:])
                S.inc(_li, "sv")
            for n in range(NST):
                v.wait_ge(S.h["sa"], SA_DA[n])
                np_ = n % 2
                bb_v = bbc[:, n * TS:(n + 1) * TS]
                for dt in range(8):
                    da_v = dab[:, (np_ * 8 + dt) * TS:(np_ * 8 + dt + 1) * TS]
                    y_v = yac[:, dt * TS:(dt + 1) * TS]
                    v.tensor_tensor(ubf[:], dxb[:, dt * TS:(dt + 1) * TS],
                                    bb_v, ALU.mult)
                    ini = 0.0 if s_ == 0 else carry[:, n * 8 + dt:n * 8 + dt + 1]
                    v.tensor_tensor_scan(sbf[:], da_v, ubf[:], ini,
                                         ALU.mult, ALU.add)
                    if n == 0:
                        v.tensor_tensor(y_v, sbf[:], bb_v, ALU.mult)
                    else:
                        v.tensor_tensor(pbf2[:, 0:TS], sbf[:], bb_v, ALU.mult)
                        v.tensor_tensor(y_v, y_v, pbf2[:, 0:TS], ALU.add)
                    _li = v.tensor_copy(carry[:, n * 8 + dt:n * 8 + dt + 1],
                                        sbf[:, TS - 1:TS])
                S.inc(_li, "sv")
            if s_ >= 2:
                v.wait_ge(S.h["sp"], prev["SP_WO7"])
            v.wait_ge(S.h["qs"], t_ldz[s_])
            for dt in range(8):
                v.scalar_tensor_tensor(
                    out=yfn[:, dt * TS:(dt + 1) * TS],
                    in0=xcs[:, dt * TS:(dt + 1) * TS],
                    scalar=dpv[:, dt:dt + 1],
                    in1=yac[:, dt * TS:(dt + 1) * TS],
                    op0=ALU.mult, op1=ALU.add)
                _li = v.tensor_tensor(yfn[:, dt * TS:(dt + 1) * TS],
                                      yfn[:, dt * TS:(dt + 1) * TS],
                                      zss[:, dt * TS:(dt + 1) * TS], ALU.mult)
            S.inc(_li, "sv")

        assert S.c["sp"] == (SP_WO[7] if has_wo else SP_BB[15]), S.c["sp"]
        assert S.c["sa"] == (SA_WOEV[7] if has_wo else SA_DA[15]), S.c["sa"]
        assert S.c["sv"] == SV_YFN, (S.c["sv"], SV_YFN)
        prev2 = prev
        prev = {"SV_YFN": SV_YFN, "SP_WDT7": SP_WDT[7], "SA_BSB": SA_BSB,
                "SA_DA15": SA_DA[15],
                "SA_WOEV7": SA_WOEV[7] if has_wo else
                (prev["SA_WOEV7"] if prev else 0),
                "SP_BB15": SP_BB[15],
                "SP_WO7": SP_WO[7] if has_wo else
                (prev["SP_WO7"] if prev else 0)}


    par_f = (nSU - 1) % 2
    yfn_f = yfn2[:, par_f * 8 * TS:(par_f + 1) * 8 * TS]
    SP_WOF = [S.c["sp"] + 1 + m for m in range(8)]
    SA_WOFE = [S.c["sa"] + 1 + m for m in range(8)]

    @block.tensor
    def _(t):
        for mt in range(8):
            if mt == 0:
                t.wait_ge(S.h["sv"], prev["SV_YFN"])
                t.wait_ge(S.h["sa"], prev["SA_WOEV7"])
            if mt >= 2:
                t.wait_ge(S.h["sa"], SA_WOFE[mt - 2])
            for kt in range(8):
                _li = t.matmul(ps_wo[mt % 2][:], wout_t(kt, mt),
                               yfn_f[:, kt * TS:(kt + 1) * TS],
                               start=(kt == 0), stop=(kt == 7))
            S.inc(_li, "sp")

    @block.scalar
    def _(s):
        for mt in range(8):
            s.wait_ge(S.h["sp"], SP_WOF[mt])
            if mt == 0:
                s.wait_ge(S.h["qo"], t_powq[nSU - 1])
            _li = s.activation(out=post[:, mt * TS:(mt + 1) * TS],
                               in_=ps_wo[mt % 2][:], func=AF.Copy)
            S.inc(_li, "sa")
        s.wait_ge(S.h["sa"], S.c["sa"])
        h = get_h(s, "scalar")
        srcp = post[:].rearrange("p (d t) -> p d t", d=8)
        tf0 = (nSU - 1) * TS
        with s.If_eq(h, 0):
            s.dma_start(out=shpo.rearrange("w d p l -> p w d l")[
                :, 0, :, tf0:tf0 + TS], in_=srcp).then_inc(S.h["qo"], 16)
        with s.Else():
            s.dma_start(out=shpo.rearrange("w d p l -> p w d l")[
                :, 1, :, tf0:tf0 + TS], in_=srcp).then_inc(S.h["qo"], 16)
        S.c["qo"] += 16
    assert S.c["sp"] == SP_WOF[7]
    assert S.c["sa"] == SA_WOFE[7]

    QO_B = S.c["qo"]

    @block.gpsimd
    def _(g):
        g.wait_ge(S.h["qo"], QO_B)
        S.inc(g.collective_compute(
            "AllReduce", ALU.add, replica_groups=PAIRS,
            ins=[bi2], outs=[bo2]), "cc", 1)
    T_BAR2 = S.c["cc"]

    if stop_after == "B":
        dpo = nc.dram_tensor("dbg_po", [2, 8, P, L], F32,
                             kind="ExternalOutput").ap()
        ddl = nc.dram_tensor("dbg_dlt", [P, 8 * TS], F32,
                             kind="ExternalOutput").ap()
        dbs = nc.dram_tensor("dbg_bsb", [NST, TS], F32,
                             kind="ExternalOutput").ap()
        dbb = nc.dram_tensor("dbg_bbc", [P, NST * TS], F32,
                             kind="ExternalOutput").ap()
        dya = nc.dram_tensor("dbg_yac", [P, 8 * TS], F32,
                             kind="ExternalOutput").ap()
        dyf = nc.dram_tensor("dbg_yfn", [P, 8 * TS], F32,
                             kind="ExternalOutput").ap()
        qo_f, qs_f, qg_f = S.c["qo"], S.c["qs"], S.c["qg"]

        @block.gpsimd
        def _(g):
            g.wait_ge(S.h["qo"], qo_f)
            g.wait_ge(S.h["qs"], qs_f)
            g.wait_ge(S.h["qg"], qg_f)
            g.wait_ge(S.h["cc"], T_BAR2)
            g.dma_start(out=dpo, in_=shpo).then_inc(S.h["qg"], 16)
            g.dma_start(out=ddl, in_=dlt[:]).then_inc(S.h["qg"], 16)
            g.dma_start(out=dbs, in_=bsb[:]).then_inc(S.h["qg"], 16)
            g.dma_start(out=dbb, in_=bbc[:]).then_inc(S.h["qg"], 16)
            g.dma_start(out=dya, in_=yac[:]).then_inc(S.h["qg"], 16)
            g.dma_start(out=dyf, in_=yfn[:]).then_inc(S.h["qg"], 16)
            g.wait_ge(S.h["qg"], qg_f + 96)
        es_b.close()
        es_mw.close()
        es.close()
        return nc

    es_b.close()
    es_mw.close()

    if False:
        pass

    es_c = ExitStack()

    def sbc(name, shape, dtype):
        return es_c.enter_context(nc.sbuf_tensor(name, shape, dtype))

    xnh = sbc("xnh", [P, 8 * TH], BF16)
    racc = sbc("racc", [P, 8 * TH], F32)
    cft = sbc("cft", [NE + 1, TH], BF16)
    lgb = sbc("lgb", [NE, 512], F32)
    spb = sbc("spb", [NE, 512], F32)
    acb = sbc("acb", [NE, 512], F32)
    sptok = sbc("sptok", [P, 4 * NE], F32)
    actok = sbc("actok", [P, 4 * NE], F32)
    cfe = sbc("cfe", [P, 4 * (NE + 1)], F32)
    m1b = sbc("m1b", [P, 3], F32)
    rwT = sbc("rwT", [P, 8 * NE], BF16)
    rb2x = sbc("rb2x", [NE + 1, D], BF16)
    w1e = sbc("w1e", [P, 2 * 8 * HF], BF16)
    w2e = sbc("w2e", [P, 2 * 4 * D], BF16)
    hbuf = sbc("hbuf", [P, 4 * 512], BF16)
    cfbb = sbc("cfbb", [P, 512], BF16)
    rtmp = sbc("rtmp", [P, 2 * 512], BF16)
    pobuf = sbc("pobuf", [P, 2 * 8 * 512], BF16)
    xrl = sbc("xrl", [P, 4 * D], F32)
    otok = sbc("otok", [P, 2 * D], F32)

    @block.gpsimd
    def _(g):
        g.wait_ge(S.h["qo"], QO_B)
        S.inc(g.dma_start(out=rwT[:].rearrange("p (k e) -> p k e", k=8),
                          in_=drw.rearrange("(k p) e -> p k e", p=P)), "qg", 16)
        S.inc(g.dma_start(out=rb2x[:], in_=db2), "qg", 16)
    QG_C0 = S.c["qg"]

    @block.sync
    def _(sy):
        sy.wait_ge(S.h["qo"], QO_B)
        h = get_h(sy, "sync")
        with sy.If_eq(h, 0):
            sy.dma_start(out=xnh[:].rearrange("p (d t) -> p d t", d=8),
                         in_=pxnt.rearrange("d p l -> p d l")[
                             :, :, 0:TH]).then_inc(S.h["qs"], 16)
        with sy.Else():
            sy.dma_start(out=xnh[:].rearrange("p (d t) -> p d t", d=8),
                         in_=pxnt.rearrange("d p l -> p d l")[
                             :, :, TH:L]).then_inc(S.h["qs"], 16)
        S.c["qs"] += 16
    T_XNH = S.c["qs"]

    es_c1 = ExitStack()
    ps_lg = es_c1.enter_context(nc.psum_tensor("ps_lg", [NE, 512], F32))
    ps_tp = es_c1.enter_context(nc.psum_tensor("ps_tp", [P, 64], F32))
    ps_cf = es_c1.enter_context(nc.psum_tensor("ps_cf", [NE + 1, 512], F32))

    t_rt_done = 0
    t_top2 = [0] * nTB
    for tb in range(nTB):
        @block.tensor
        def _(t, tb=tb):
            t.wait_ge(S.h["qs"], T_XNH)
            if tb == 0:
                t.wait_ge(S.h["qg"], QG_C0)
            else:
                t.wait_ge(S.h["sv"], t_top2[tb - 1])
            for kt in range(8):
                _li = t.matmul(ps_lg[:], rwT[:, kt * NE:(kt + 1) * NE],
                         xnh[:, kt * TH + tb * 512:kt * TH + (tb + 1) * 512],
                         start=(kt == 0), stop=(kt == 7))
            S.inc(_li, "sp")
        t_lgm = S.c["sp"]

        @block.vector
        def _(v, t_lgm=t_lgm):
            v.wait_ge(S.h["sp"], t_lgm)
            _li = v.tensor_scalar(out=lgb[:], in0=ps_lg[:], scalar1=ebv[:, 0:1],
                            scalar2=None, op0=ALU.add)
            S.inc(_li, "sv")
        t_lg = S.c["sv"]

        @block.scalar
        def _(s, t_lg=t_lg):
            s.wait_ge(S.h["sv"], t_lg)
            s.activation(out=spb[:], in_=lgb[:], func=AF.Abs)
            s.activation(out=spb[:], in_=spb[:], func=AF.Exp,
                         scale=cst[0:NE, C_MONE:C_MONE + 1])
            _li = s.activation(out=spb[:], in_=spb[:], func=AF.Ln,
                         bias=cst[0:NE, C_ONE:C_ONE + 1])
            S.inc(_li, "sa")
        t_sp1 = S.c["sa"]

        @block.vector
        def _(v, t_sp1=t_sp1):
            v.wait_ge(S.h["sa"], t_sp1)
            v.tensor_relu(acb[:], lgb[:])
            v.tensor_tensor(spb[:], spb[:], acb[:], ALU.add)
            _li = v.tensor_scalar_max(spb[:], spb[:], cst[0:NE, C_TINY:C_TINY + 1])
            S.inc(_li, "sv")
        t_sp2 = S.c["sv"]

        @block.scalar
        def _(s, t_sp2=t_sp2):
            s.wait_ge(S.h["sv"], t_sp2)
            s.activation(out=acb[:], in_=spb[:], func=AF.Ln)
            _li = s.activation(out=acb[:], in_=acb[:], func=AF.Exp,
                         scale=cst[0:NE, C_HALF:C_HALF + 1])
            S.inc(_li, "sa")
        t_act = S.c["sa"]

        @block.tensor
        def _(t, t_act=t_act, tb=tb):
            t.wait_ge(S.h["sa"], t_act)
            if tb > 0:
                t.wait_ge(S.h["sa"], S.c["sa"])
            for tt in range(4):
                t.transpose(ps_tp[:, tt * 8:(tt + 1) * 8],
                            spb[:, tt * P:(tt + 1) * P], idf[0:NE, 0:NE])
                _li = t.transpose(ps_tp[:, 32 + tt * 8:32 + (tt + 1) * 8],
                            acb[:, tt * P:(tt + 1) * P], idf[0:NE, 0:NE])
            S.inc(_li, "sp")
        t_tpm = S.c["sp"]

        @block.scalar
        def _(s, t_tpm=t_tpm):
            s.wait_ge(S.h["sp"], t_tpm)
            s.activation(out=sptok[:], in_=ps_tp[:, 0:32], func=AF.Copy)
            _li = s.activation(out=actok[:], in_=ps_tp[:, 32:64], func=AF.Copy)
            S.inc(_li, "sa")
        t_tpe = S.c["sa"]

        @block.vector
        def _(v, t_tpe=t_tpe):
            v.wait_ge(S.h["sa"], t_tpe)
            def rtw(inst):
                S.inc(inst, "sv")
                v.wait_ge(S.h["sv"], S.c["sv"])
            for tt in range(4):
                sp_t = sptok[:, tt * NE:(tt + 1) * NE]
                ac_t = actok[:, tt * NE:(tt + 1) * NE]
                cf_t = cfe[:, tt * (NE + 1):(tt + 1) * (NE + 1)]
                rtw(v.reduce_max(m1b[:, 0:1], sp_t, axis=mybir.AxisListType.X))
                rtw(v.tensor_scalar(out=cf_t[:, 0:NE], in0=sp_t,
                                    scalar1=m1b[:, 0:1], scalar2=None,
                                    op0=ALU.is_equal))
                rtw(v.tensor_scalar_mul(cf_t[:, 0:NE], cf_t[:, 0:NE], 1e30))
                rtw(v.tensor_tensor(cf_t[:, 0:NE], sp_t, cf_t[:, 0:NE],
                                    ALU.subtract))
                rtw(v.reduce_max(m1b[:, 1:2], cf_t[:, 0:NE],
                                 axis=mybir.AxisListType.X))
                rtw(v.tensor_scalar(out=cf_t[:, 0:NE], in0=sp_t,
                                    scalar1=m1b[:, 1:2], scalar2=None,
                                    op0=ALU.is_ge))
                v.tensor_tensor(cf_t[:, 0:NE], cf_t[:, 0:NE], ac_t, ALU.mult)
                _li = v.memset(cf_t[:, NE:NE + 1], 1.0)
            S.inc(_li, "sv")
        t_top2[tb] = S.c["sv"]

        @block.tensor
        def _(t, tb=tb):
            t.wait_ge(S.h["sv"], t_top2[tb])
            if tb > 0:
                t.wait_ge(S.h["sa"], S.c["sa"])
            for tt in range(4):
                _li = t.transpose(ps_cf[:, tt * P:(tt + 1) * P],
                            cfe[:, tt * (NE + 1):(tt + 1) * (NE + 1)], idf[:])
            S.inc(_li, "sp")
        t_cfm = S.c["sp"]

        @block.scalar
        def _(s, t_cfm=t_cfm, tb=tb):
            s.wait_ge(S.h["sp"], t_cfm)
            _li = s.activation(out=cft[:, tb * 512:(tb + 1) * 512], in_=ps_cf[:],
                         func=AF.Copy)
            S.inc(_li, "sa")
        t_rt_done = S.c["sa"]

    es_c1.close()

    es_c2 = ExitStack()
    ps_h = [es_c2.enter_context(nc.psum_tensor(f"ps_h{i}", [P, 512], F32))
            for i in range(2)]
    ps_rt = [es_c2.enter_context(nc.psum_tensor(f"ps_rt{i}", [P, 512], F32))
             for i in range(2)]
    ps_cb = es_c2.enter_context(nc.psum_tensor("ps_cb", [P, 512], F32))

    e_order = [8] + list(range(8))
    sp0e, sa0e, sv0e = S.c["sp"], S.c["sa"], S.c["sv"]
    SP_MM1 = {}
    SP_CFB = {}
    SP_MM2 = {}
    SA_M1E = {}
    SV_CFB = {}
    SV_RT = {}
    spv, sav, svv = sp0e, sa0e, sv0e
    for ei, e in enumerate(e_order):
        for tb in range(nTB):
            for mt1 in range(4):
                spv += 1; SP_MM1[(ei, tb, mt1)] = spv
                sav += 1; SA_M1E[(ei, tb, mt1)] = sav
            if e != 8:
                spv += 1; SP_CFB[(ei, tb)] = spv
                svv += 1; SV_CFB[(ei, tb)] = svv
            for mt in range(8):
                spv += 1; SP_MM2[(ei, tb, mt)] = spv
                svv += 1; SV_RT[(ei, tb, mt)] = svv
    SP_E_END, SA_E_END, SV_E_END = spv, sav, svv

    def prev_mm2_last(ei, tb):
        if tb > 0:
            return SP_MM2[(ei, tb - 1, 7)]
        if ei > 0:
            return SP_MM2[(ei - 1, nTB - 1, 7)]
        return None

    t_wld = [0] * 9

    @block.gpsimd
    def _(g):
        for ei, e in enumerate(e_order):
            ep = ei % 2
            if ei >= 2:
                g.wait_ge(S.h["sp"], SP_MM2[(ei - 2, nTB - 1, 7)])
            S.inc(g.dma_start(
                out=w1e[:, ep * 8 * HF:(ep + 1) * 8 * HF].rearrange(
                    "p (k c) -> p k c", k=8),
                in_=dw1[e].rearrange("(k p) c -> p k c", p=P)), "qg", 16)
            S.inc(g.dma_start(
                out=w2e[:, ep * 4 * D:(ep + 1) * 4 * D].rearrange(
                    "p (k c) -> p k c", k=4),
                in_=dw2[e].rearrange("(k p) c -> p k c", p=P)), "qg", 16)
            t_wld[ei] = S.c["qg"]

    @block.tensor
    def _(t):
        t.wait_ge(S.h["sa"], t_rt_done)
        for ei, e in enumerate(e_order):
            ep = ei % 2
            for tb in range(nTB):
                if tb == 0:
                    t.wait_ge(S.h["qg"], t_wld[ei])
                for mt1 in range(4):
                    g_prev = (ei * nTB + tb) * 4 + mt1 - 2
                    if g_prev >= 0:
                        pei, r = divmod(g_prev, nTB * 4)
                        ptb, pmt1 = divmod(r, 4)
                        t.wait_ge(S.h["sa"], SA_M1E[(pei, ptb, pmt1)])
                    for kt in range(8):
                        _li = t.matmul(
                            ps_h[mt1 % 2][:],
                            w1e[:, ep * 8 * HF + kt * HF + mt1 * P:
                                ep * 8 * HF + kt * HF + (mt1 + 1) * P],
                            xnh[:, kt * TH + tb * 512:kt * TH + (tb + 1) * 512],
                            start=(kt == 0), stop=(kt == 7))
                    S.inc(_li, "sp")
                if e != 8:
                    keys = sorted(SV_CFB)
                    my_i = keys.index((ei, tb))
                    if my_i > 0:
                        t.wait_ge(S.h["sv"], SV_CFB[keys[my_i - 1]])
                    _li = t.matmul(ps_cb[:], e9[:, e * P:(e + 1) * P],
                                   cft[:, tb * 512:(tb + 1) * 512],
                                   start=True, stop=True)
                    S.inc(_li, "sp")
                for mt in range(8):
                    if mt == 0:
                        if e != 8:
                            t.wait_ge(S.h["sv"], SV_CFB[(ei, tb)])
                        else:
                            t.wait_ge(S.h["sa"], SA_M1E[(ei, tb, 3)])
                    g2 = (ei * nTB + tb) * 8 + mt - 2
                    if g2 >= 0:
                        pei, r = divmod(g2, nTB * 8)
                        ptb, pmt = divmod(r, 8)
                        t.wait_ge(S.h["sv"], SV_RT[(pei, ptb, pmt)])
                    for kt in range(4):
                        _li = t.matmul(
                            ps_rt[mt % 2][:],
                            w2e[:, ep * 4 * D + kt * D + mt * P:
                                ep * 4 * D + kt * D + (mt + 1) * P],
                            hbuf[:, kt * 512:(kt + 1) * 512],
                            start=(kt == 0), stop=(kt == 3 and e != 8))
                    if e == 8:
                        _li = t.matmul(ps_rt[mt % 2][:],
                                       rb2x[:, mt * P:(mt + 1) * P],
                                       cft[:, tb * 512:(tb + 1) * 512],
                                       start=False, stop=True)
                    S.inc(_li, "sp")
    assert S.c["sp"] == SP_E_END, (S.c["sp"], SP_E_END)

    @block.scalar
    def _(s):
        for ei, e in enumerate(e_order):
            for tb in range(nTB):
                pm = prev_mm2_last(ei, tb)
                for mt1 in range(4):
                    s.wait_ge(S.h["sp"], SP_MM1[(ei, tb, mt1)])
                    if pm is not None and mt1 == 0:
                        s.wait_ge(S.h["sp"], pm)
                    _li = s.activation(out=hbuf[:, mt1 * 512:(mt1 + 1) * 512],
                                       in_=ps_h[mt1 % 2][:], func=AF.Silu,
                                       bias=b1v[:, e * 4 + mt1:e * 4 + mt1 + 1])
                    S.inc(_li, "sa")
    assert S.c["sa"] == SA_E_END, (S.c["sa"], SA_E_END)

    @block.vector
    def _(v):
        for ei, e in enumerate(e_order):
            for tb in range(nTB):
                if e != 8:
                    v.wait_ge(S.h["sp"], SP_CFB[(ei, tb)])
                    v.wait_ge(S.h["sa"], SA_M1E[(ei, tb, 3)])
                    v.tensor_copy(cfbb[:], ps_cb[:])
                    for mt1 in range(4):
                        _li = v.tensor_tensor(
                            hbuf[:, mt1 * 512:(mt1 + 1) * 512],
                            hbuf[:, mt1 * 512:(mt1 + 1) * 512],
                            cfbb[:], ALU.mult)
                    S.inc(_li, "sv")
                for mt in range(8):
                    v.wait_ge(S.h["sp"], SP_MM2[(ei, tb, mt)])
                    dst = racc[:, mt * TH + tb * 512:mt * TH + (tb + 1) * 512]
                    if e == 8:
                        _li = v.tensor_copy(dst, ps_rt[mt % 2][:])
                    else:
                        _li = v.tensor_tensor(dst, dst, ps_rt[mt % 2][:],
                                              ALU.add)
                    S.inc(_li, "sv")
    assert S.c["sv"] == SV_E_END, (S.c["sv"], SV_E_END)

    T_EXP_DVE = SV_E_END

    es_c2.close()

    es_c3 = ExitStack()
    ps_ot = [es_c3.enter_context(nc.psum_tensor(f"ps_ot{i}", [P, D], F32))
             for i in range(2)]

    t_po_qs = [0] * nTB
    t_po_qo = [0] * nTB
    t_xr_qs = [0] * nTB
    t_xr_qo = [0] * nTB
    t_accs = [0] * nTB
    t_fin, t_finev, t_oadd, t_odma = [], [], [], []

    def _po_dma(eng, qn, w, tb, h):
        for hh in range(2):
            ctx = eng.If_eq(h, 0) if hh == 0 else eng.Else()
            with ctx:
                eng.dma_start(
                    out=pobuf[:, w * 8 * 512:(w + 1) * 8 * 512].rearrange(
                        "p (d t) -> p d t", d=8),
                    in_=shpo.rearrange("w d p l -> p w d l")[
                        :, w, :, hh * TH + tb * 512:hh * TH + (tb + 1) * 512]
                ).then_inc(S.h[qn], 16)
        S.c[qn] += 16

    def _xr_dma(eng, qn, i, tb, h):
        for hh in range(2):
            ctx = eng.If_eq(h, 0) if hh == 0 else eng.Else()
            with ctx:
                eng.dma_start(out=xrl[:, i * D:(i + 1) * D],
                              in_=dxin[hh * TH + tb * 512 + i * 128:
                                       hh * TH + tb * 512 + (i + 1) * 128, :]
                              ).then_inc(S.h[qn], 16)
        S.c[qn] += 16

    def emit_loads0():
        @block.sync
        def _(sy):
            sy.wait_ge(S.h["cc"], T_BAR2)
            h = get_h(sy, "sync")
            for w in range(2):
                _po_dma(sy, "qs", w, 0, h)
            for i in range(4):
                _xr_dma(sy, "qs", i, 0, h)
        t_po_qs[0] = S.c["qs"]
        t_xr_qs[0] = S.c["qs"]
        t_xr_qo[0] = S.c["qo"]

    def emit_po(tb):
        @block.sync
        def _(sy, tb=tb):
            sy.wait_ge(S.h["sv"], t_accs[tb - 1])
            h = get_h(sy, "sync")
            _po_dma(sy, "qs", 0, tb, h)
        t_po_qs[tb] = S.c["qs"]

        @block.scalar
        def _(s, tb=tb):
            s.wait_ge(S.h["sv"], t_accs[tb - 1])
            h = get_h(s, "scalar")
            _po_dma(s, "qo", 1, tb, h)
        t_po_qo[tb] = S.c["qo"]

    def emit_xr(tb):
        @block.sync
        def _(sy, tb=tb):
            sy.wait_ge(S.h["sg"], t_oadd[(tb - 1) * 4 + 3])
            h = get_h(sy, "sync")
            for i in range(2):
                _xr_dma(sy, "qs", i, tb, h)
        t_xr_qs[tb] = S.c["qs"]

        @block.scalar
        def _(s, tb=tb):
            s.wait_ge(S.h["sg"], t_oadd[(tb - 1) * 4 + 3])
            h = get_h(s, "scalar")
            for i in range(2, 4):
                _xr_dma(s, "qo", i, tb, h)
        t_xr_qo[tb] = S.c["qo"]

    emit_loads0()
    for tb in range(nTB):
        @block.vector
        def _(v, tb=tb):
            v.wait_ge(S.h["qs"], t_po_qs[tb])
            if tb >= 1:
                v.wait_ge(S.h["qo"], t_po_qo[tb])
            for mt in range(8):
                dst = racc[:, mt * TH + tb * 512:mt * TH + (tb + 1) * 512]
                for w in range(2):
                    _li = v.tensor_tensor(
                        dst, dst,
                        pobuf[:, (w * 8 + mt) * 512:(w * 8 + mt + 1) * 512],
                        ALU.add)
            S.inc(_li, "sv")
        t_accs[tb] = S.c["sv"]
        if tb + 1 < nTB:
            emit_po(tb + 1)

        for tt in range(4):
            fi = len(t_fin)

            @block.tensor
            def _(t, tb=tb, tt=tt, fi=fi):
                t.wait_ge(S.h["sv"], t_accs[tb])
                if fi >= 2:
                    t.wait_ge(S.h["sa"], t_finev[fi - 2])
                for mt in range(8):
                    _li = t.transpose(
                        ps_ot[fi % 2][:, mt * P:(mt + 1) * P],
                        racc[:, mt * TH + tb * 512 + tt * P:
                             mt * TH + tb * 512 + (tt + 1) * P], idf[:])
                S.inc(_li, "sp")
            t_fin.append(S.c["sp"])

            @block.scalar
            def _(s, fi=fi):
                s.wait_ge(S.h["sp"], t_fin[fi])
                if fi >= 2:
                    s.wait_ge(S.h["sg"], t_oadd[fi - 2])
                    s.wait_ge(S.h["qo"], t_odma[fi - 2])
                _li = s.activation(out=otok[:, (fi % 2) * D:((fi % 2) + 1) * D],
                             in_=ps_ot[fi % 2][:], func=AF.Copy)
                S.inc(_li, "sa")
            t_finev.append(S.c["sa"])

            @block.gpsimd
            def _(g, tb=tb, tt=tt, fi=fi):
                g.wait_ge(S.h["sa"], t_finev[fi])
                g.wait_ge(S.h["qs"], t_xr_qs[tb])
                if tt >= 2:
                    g.wait_ge(S.h["qo"], t_xr_qo[tb])
                _li = g.tensor_tensor(otok[:, (fi % 2) * D:((fi % 2) + 1) * D],
                                otok[:, (fi % 2) * D:((fi % 2) + 1) * D],
                                xrl[:, tt * D:(tt + 1) * D], ALU.add)
                S.inc(_li, "sg")
            t_oadd.append(S.c["sg"])

            @block.scalar
            def _(s, tb=tb, tt=tt, fi=fi):
                s.wait_ge(S.h["sg"], t_oadd[fi])
                S.inc(s.dma_start(
                    out=dout[tb * 512 + tt * P:tb * 512 + (tt + 1) * P, :],
                    in_=otok[:, (fi % 2) * D:((fi % 2) + 1) * D]), "qo", 16)
            t_odma.append(S.c["qo"])
        if tb + 1 < nTB:
            emit_xr(tb + 1)

    QO_END = S.c["qo"]
    QS_END = S.c["qs"]
    QG_END = S.c["qg"]

    @block.gpsimd
    def _(g):
        g.wait_ge(S.h["qo"], QO_END)
        g.wait_ge(S.h["qs"], QS_END)
        g.wait_ge(S.h["qg"], QG_END)

    es_c3.close()
    es_c.close()
    es.close()
    return nc


_NC_CACHE = {}


def _get_nc(L):
    if L not in _NC_CACHE:
        _NC_CACHE[L] = build_program(L)
    return _NC_CACHE[L]


def make_in_maps(inputs, L):
    import ml_dtypes
    x = np.asarray(inputs["L0_attn_x"], np.float32)
    w_in = np.asarray(inputs["L0_attn_W_in"], np.float32)
    conv_w = np.asarray(inputs["L0_attn_conv_w"], np.float32)
    conv_b = np.asarray(inputs["L0_attn_conv_b"], np.float32)
    w_xp = np.asarray(inputs["L0_attn_W_xproj"], np.float32)
    w_dt = np.asarray(inputs["L0_attn_W_dt"], np.float32)
    b_dt = np.asarray(inputs["L0_attn_b_dt"], np.float32)
    a_log = np.asarray(inputs["L0_attn_A_log"], np.float32)
    d_p = np.asarray(inputs["L0_attn_Dp"], np.float32)
    w_out = np.asarray(inputs["L0_attn_W_out"], np.float32)
    r_w = np.asarray(inputs["L0_ffn_routing_W"], np.float32)
    e_b = np.asarray(inputs["L0_ffn_expert_bias"], np.float32)
    sW1 = np.asarray(inputs["L0_ffn_sW1"], np.float32)
    sb1 = np.asarray(inputs["L0_ffn_sb1"], np.float32)
    sW2 = np.asarray(inputs["L0_ffn_sW2"], np.float32)
    sb2 = np.asarray(inputs["L0_ffn_sb2"], np.float32)
    rW1 = np.asarray(inputs["L0_ffn_rW1"], np.float32)
    rb1 = np.asarray(inputs["L0_ffn_rb1"], np.float32)
    rW2 = np.asarray(inputs["L0_ffn_rW2"], np.float32)
    rb2 = np.asarray(inputs["L0_ffn_rb2"], np.float32)

    w1 = np.ascontiguousarray(np.concatenate([rW1, sW1[None]], 0))
    b1 = np.ascontiguousarray(np.concatenate([rb1, sb1[None]], 0))
    w2 = np.ascontiguousarray(np.concatenate([rW2, sW2[None]], 0))
    b2 = np.ascontiguousarray(np.concatenate([rb2, sb2[None]], 0))
    identf = np.eye(P, dtype=np.float32)
    identb = np.eye(P, dtype=np.float32).astype(ml_dtypes.bfloat16)
    e16m = np.zeros((NST, NST * P), np.float32)
    for n in range(NST):
        e16m[n, n * P:(n + 1) * P] = 1.0
    e16m = e16m.astype(ml_dtypes.bfloat16)
    e9m = np.zeros((NE + 1, (NE + 1) * P), np.float32)
    for n in range(NE + 1):
        e9m[n, n * P:(n + 1) * P] = 1.0
    e9m = e9m.astype(ml_dtypes.bfloat16)

    c = np.ascontiguousarray
    in_maps = []
    for core in range(8):
        b, h = core // 2, core % 2
        own = slice(h * HD, (h + 1) * HD)
        pair = slice((1 - h) * HD, (2 - h) * HD)
        in_maps.append({
            "x": c(x[b, :L]),
            "w_in": c(np.concatenate(
                [w_in[:, own], w_in[:, DI + h * HD:DI + (h + 1) * HD]], 1)),
            "conv_w": c(conv_w[own, 0, :]),
            "conv_b": c(conv_b[own].reshape(HD, 1)),
            "w_x2": c(np.concatenate(
                [w_xp[own, NST:2 * NST], w_xp[pair, NST:2 * NST]], 0)),
            "w_dt": c(np.concatenate([w_dt[own, own], w_dt[pair, own]], 0)),
            "b_dt": c(b_dt[own].reshape(HD, 1)),
            "a_log": c(a_log[own]),
            "d_p": c(d_p[own].reshape(HD, 1)),
            "w_out": c(w_out[own, :]),
            "r_w": c(r_w.T),
            "e_b": c(e_b.reshape(NE, 1)),
            "w1": w1, "b1": b1, "w2": w2, "b2": b2,
            "identf": identf, "identb": identb, "e16": e16m, "e9": e9m,
        })
    return in_maps


def kernel(**inputs):
    L = 4096
    in_maps = make_in_maps(inputs, L)
    nc = _get_nc(L)
    res = run_bass_kernel_spmd(nc, in_maps, core_ids=list(range(8)))
    out = np.empty((4, L, D), np.float32)
    for core in range(8):
        b, h = core // 2, core % 2
        out[b, h * (L // 2):(h + 1) * (L // 2), :] = res.results[core]["out"]
    return out
